# version 04
# baseline (speedup 1.0000x reference)
import numpy as np
import ml_dtypes
from contextlib import ExitStack

import concourse.bass as bass
import concourse.mybir as mybir
from concourse.bass_utils import run_bass_kernel_spmd

F32 = mybir.dt.float32
BF16 = mybir.dt.bfloat16
AF = mybir.ActivationFunctionType
ALU = mybir.AluOpType
AX = mybir.AxisListType

B, L, D = 2, 8192, 1024
CH, PT = 128, 512
NCH = PT // CH
TOK = 2048
GH = 512


class Sched:
    KEYS = ("T", "V", "S", "G", "Y")

    def __init__(self):
        self.items = []
        self.counts = {}
        self.h = {}

    def inc(self, key, sem, fn, n=1):
        self.counts[sem] = self.counts.get(sem, 0) + n

        def g(eng, fn=fn, sem=sem, n=n):
            fn(eng).then_inc(self.h[sem], n)
        self.items.append((key, g))
        return self.counts[sem]

    def run(self, key, fn):
        self.items.append((key, fn))

    def wait(self, key, sem, val=None):
        if val is None:
            val = self.counts.get(sem, 0)
        assert 0 <= val <= self.counts.get(sem, 0), (sem, val, self.counts.get(sem, 0))

        def g(eng, sem=sem, val=val):
            eng.wait_ge(self.h[sem], val)
        self.items.append((key, g))

    def n(self, sem):
        return self.counts.get(sem, 0)


def _lhsT(w):
    K, M = w.shape
    return np.ascontiguousarray(w.reshape(K // 128, 128, M).transpose(1, 0, 2).reshape(128, -1))


def prepare_core_inputs(A, core):
    b, h = core // 4, core % 4
    win = core % 4
    hs = slice(h * 256, (h + 1) * 256)
    d = {}
    d["x_attn"] = A["L0_attn_x"][b]
    d["x_res"] = A["L0_attn_x"][b][win * TOK:(win + 1) * TOK]
    d["wq"] = _lhsT(A["L0_attn_Wq"][:, hs])
    d["wk"] = _lhsT(A["L0_attn_Wk"][:, hs])
    d["wv"] = _lhsT(A["L0_attn_Wv"][:, hs])
    d["wb"] = _lhsT(A["L0_attn_Wb"][:, h:h + 1])
    cw = [A["L0_attn_" + nm][hs].reshape(2, 128, 4) for nm in ("qc_w", "kc_w", "vc_w")]
    d["convw"] = np.stack(cw, 0).reshape(6, 128, 4).transpose(1, 0, 2).reshape(128, 24)
    fs = A["L0_attn_fir_s"][h].reshape(2, 128, 3).transpose(1, 0, 2).reshape(128, 6)
    fl = A["L0_attn_fir_l"][h].reshape(2, 128, 31).transpose(1, 0, 2).reshape(128, 62)
    d["firw"] = np.concatenate([fs, fl], 1)
    g = A["L0_attn_gW1"]
    d["gw1h"] = _lhsT(g[0:1024])
    d["gw1b"] = g[4096:5120].reshape(8, 128, GH).transpose(1, 0, 2).reshape(128, 8 * GH)
    d["gw1s"] = g[1024:4096].reshape(24, 128, GH).transpose(1, 0, 2).reshape(128, 24 * GH)
    d["gb1"] = A["L0_attn_gb1"].reshape(4, 128).T
    d["gw2"] = A["L0_attn_gW2"].reshape(4, 128, 4).transpose(1, 0, 2).reshape(128, 16)
    d["gb2"] = A["L0_attn_gb2"].reshape(1, 4)
    d["temp"] = A["L0_attn_temp"][h].reshape(1, 1)
    d["floor"] = A["L0_attn_eps_floor"][h].reshape(1, 4)
    d["onorm"] = A["L0_attn_onorm_w"].reshape(1, 256)
    d["wo"] = A["L0_attn_Wo"][hs].reshape(2, 128, 1024).transpose(1, 0, 2).reshape(128, 2048)
    w1t, w2t = A["L0_ffn_w1_w"].T, A["L0_ffn_w2_w"].T
    h1t, h2t, w3t = A["L0_ffn_h1_w"].T, A["L0_ffn_h2_w"].T, A["L0_ffn_w3_w"].T
    d["w1t"] = w1t.reshape(8, 128, 8, 512).transpose(2, 1, 0, 3).reshape(8, 128, 4096)
    d["w2t"] = w2t.reshape(8, 128, 8, 512).transpose(2, 1, 0, 3).reshape(8, 128, 4096)
    d["h1t"] = (h1t.reshape(32, 128, 16, 256).transpose(2, 1, 0, 3).reshape(16, 128, 2, 4096)
                .transpose(0, 2, 1, 3).reshape(32, 128, 4096))
    d["h2t"] = (h2t.reshape(32, 128, 16, 256).transpose(2, 1, 0, 3).reshape(16, 128, 2, 4096)
                .transpose(0, 2, 1, 3).reshape(32, 128, 4096))
    d["w3t"] = w3t.reshape(32, 128, 8, 128).transpose(2, 1, 0, 3).reshape(8, 128, 4096)
    d["s1"] = A["L0_ffn_w1_s"].reshape(32, 128).T
    d["s2"] = A["L0_ffn_w2_s"].reshape(32, 128).T
    d["sh1"] = A["L0_ffn_h1_s"].reshape(32, 128).T
    d["sh2"] = A["L0_ffn_h2_s"].reshape(32, 128).T
    d["s3"] = A["L0_ffn_w3_s"].reshape(8, 128).T
    return {k: np.ascontiguousarray(v, dtype=np.float32) for k, v in d.items()}


def build_program(npass=L // PT, do_ffn=True, debug=()):
    nc = bass.Bass("TRN2", target_bir_lowering=False, detect_race_conditions=False)
    sc = Sched()

    def din(name, shape, dt=F32):
        return nc.dram_tensor(name, list(shape), dt, kind="ExternalInput").ap()

    x_attn = din("x_attn", (L, D))
    x_res = din("x_res", (TOK, D))
    wq, wk, wv = din("wq", (128, 2048)), din("wk", (128, 2048)), din("wv", (128, 2048))
    wb = din("wb", (128, 8))
    convw, firw = din("convw", (128, 24)), din("firw", (128, 68))
    gw1h, gw1b = din("gw1h", (128, 8 * GH)), din("gw1b", (128, 8 * GH))
    gw1s = din("gw1s", (128, 24 * GH))
    gb1, gw2, gb2 = din("gb1", (128, 4)), din("gw2", (128, 16)), din("gb2", (1, 4))
    temp, floor, onorm = din("temp", (1, 1)), din("floor", (1, 4)), din("onorm", (1, 256))
    wo = din("wo", (128, 2048))
    w1t, w2t = din("w1t", (8, 128, 4096)), din("w2t", (8, 128, 4096))
    h1t, h2t = din("h1t", (32, 128, 4096)), din("h2t", (32, 128, 4096))
    w3t = din("w3t", (8, 128, 4096))
    s1c, s2c = din("s1", (128, 32)), din("s2", (128, 32))
    sh1c, sh2c = din("sh1", (128, 32)), din("sh2", (128, 32))
    s3col = din("s3", (128, 8))

    out = nc.dram_tensor("out", [TOK, D], F32, kind="ExternalOutput").ap()
    dbg = {}
    for name, shape in debug:
        dbg[name] = nc.dram_tensor("dbg_" + name, list(shape), F32, kind="ExternalOutput").ap()

    cc_in = nc.dram_tensor("cc_in", [L, D], F32, kind="Internal")
    cc_rs = nc.dram_tensor("cc_rs", [TOK, D], F32, kind="Internal")
    x2stash = nc.dram_tensor("x2stash", [TOK, D], F32, kind="Internal").ap()

    st = ExitStack()
    E = st.enter_context
    for s in ("dG", "dY", "T", "V", "S", "G", "CC"):
        sc.h[s] = E(nc.semaphore("sm" + s))

    PS_S = E(nc.psum_tensor("ps_s", [128, 512], F32))
    PS_TA = E(nc.psum_tensor("ps_ta", [128, 1024], BF16))
    PS_TB = E(nc.psum_tensor("ps_tb", [128, 1024], BF16))
    PS_WA = E(nc.psum_tensor("ps_wa", [128, 512], F32))
    PS_WB = E(nc.psum_tensor("ps_wb", [128, 512], F32))
    PS_WC = E(nc.psum_tensor("ps_wc", [128, 512], F32))
    PS_B0 = E(nc.psum_tensor("ps_b0", [128, 512], F32))
    PS_B1 = E(nc.psum_tensor("ps_b1", [128, 512], F32))
    PSB = [PS_B0, PS_B1]

    IDb = E(nc.sbuf_tensor("idb", [128, 128], BF16))
    IDF = E(nc.sbuf_tensor("idf", [128, 128], F32))
    ONE2 = E(nc.sbuf_tensor("one2", [2, 128], BF16))
    CSTT = E(nc.sbuf_tensor("cstt", [128, 4], F32))
    ZRO = E(nc.sbuf_tensor("zro", [128, 512], BF16))
    nc.const_aps.register(F32, 1e-6, CSTT[:, 0:1])
    p1 = ExitStack()
    E1 = p1.enter_context
    MKU = E1(nc.sbuf_tensor("mku", [128, 128], BF16))
    MKUS = E1(nc.sbuf_tensor("mkus", [128, 128], BF16))
    ONESC = E1(nc.sbuf_tensor("onesc", [128, 1], BF16))
    WQ = E1(nc.sbuf_tensor("wqb", [128, 2048], BF16))
    WK = E1(nc.sbuf_tensor("wkb", [128, 2048], BF16))
    WV = E1(nc.sbuf_tensor("wvb", [128, 2048], BF16))
    WBc = E1(nc.sbuf_tensor("wbb", [128, 8], BF16))
    WOs = E1(nc.sbuf_tensor("wos", [128, 2048], BF16))
    CDIAG = E1(nc.sbuf_tensor("cdiag", [128, 24 * 128], BF16))
    FDIAG = E1(nc.sbuf_tensor("fdiag", [128, 68 * 128], BF16))
    CVW = E1(nc.sbuf_tensor("cvw", [128, 24], F32))
    FIW = E1(nc.sbuf_tensor("fiw", [128, 68], F32))
    GW1H = E1(nc.sbuf_tensor("gw1hb", [128, 8 * GH], BF16))
    GW1B = E1(nc.sbuf_tensor("gw1bb", [128, 8 * GH], BF16))
    GW2 = E1(nc.sbuf_tensor("gw2b", [128, 16], BF16))
    GB1 = E1(nc.sbuf_tensor("gb1b", [128, 4], F32))
    STL = E1(nc.sbuf_tensor("stl", [128, GH], BF16))
    GB2T = E1(nc.sbuf_tensor("gb2t", [128, 4], F32))
    FLT = E1(nc.sbuf_tensor("flt", [128, 4], F32))
    ONT = E1(nc.sbuf_tensor("ont", [128, 256], BF16))
    TIV = E1(nc.sbuf_tensor("tiv", [128, 1], F32))
    RR2 = E1(nc.sbuf_tensor("rr2", [2, 512], BF16))

    def _prev(v):
        v.memset(IDb[:, :], 0.0)
        v.memset(IDF[:, :], 0.0)
        v.memset(MKU[:, :], 0.0)
        v.memset(MKUS[:, :], 0.0)
        v.memset(ONE2[:, :], 0.0)
        v.memset(ONE2[0:1, :], 1.0)
        v.memset(RR2[:, :], 0.0)
        v.memset(ONT[:, :], 0.0)
        v.memset(CSTT[:, :], 0.0)
        v.memset(ZRO[:, :], 0.0)
        v.memset(CSTT[:, 0:1], 1e-6)
        return v.memset(ONESC[:, :], 1.0)
    sc.inc("V", "V", _prev)
    sc.wait("G", "V")
    def _pre(g):
        g.affine_select(out=IDb[:, :], in_=IDb[:, :], compare_op=ALU.not_equal,
                        fill=1.0, base=0, pattern=[[-1, 128]], channel_multiplier=1)
        g.affine_select(out=IDF[:, :], in_=IDF[:, :], compare_op=ALU.not_equal,
                        fill=1.0, base=0, pattern=[[-1, 128]], channel_multiplier=1)
        g.affine_select(out=MKU[:, :], in_=MKU[:, :], compare_op=ALU.is_gt,
                        fill=1.0, base=0, pattern=[[-1, 128]], channel_multiplier=1)
        return g.affine_select(out=MKUS[:, :], in_=MKUS[:, :], compare_op=ALU.is_ge,
                               fill=1.0, base=0, pattern=[[-1, 128]], channel_multiplier=1)
    sc.inc("G", "G", _pre)
    nG_pre = sc.n("G")

    def dmaG(dst, src):
        return sc.inc("G", "dG", lambda g: g.dma_start(dst, src), 16)

    def dmaY(dst, src, **kw):
        return sc.inc("Y", "dY", lambda y: y.dma_start(dst, src, **kw), 16)

    for dst, src in ((WQ, wq), (WK, wk), (WV, wv), (WBc, wb), (CVW, convw), (FIW, firw),
                     (GW1H, gw1h), (GW1B, gw1b), (GW2, gw2), (GB1, gb1), (WOs, wo)):
        dmaG(dst[:, :], src)
    n_wl = sc.n("dG")

    sc.wait("V", "dG", n_wl)
    sc.wait("V", "G", nG_pre)
    def _diags(v):
        for i in range(24):
            v.tensor_scalar(CDIAG[:, i * 128:(i + 1) * 128], IDb[:, :], CVW[:, i:i + 1],
                            None, ALU.mult)
        for i in range(67):
            v.tensor_scalar(FDIAG[:, i * 128:(i + 1) * 128], IDb[:, :], FIW[:, i:i + 1],
                            None, ALU.mult)
        return v.tensor_scalar(FDIAG[:, 67 * 128:68 * 128], IDb[:, :], FIW[:, 67:68],
                               None, ALU.mult)
    sc.inc("V", "V", _diags)
    nV_diag = sc.n("V")

    pgs = ExitStack()
    GS = pgs.enter_context(nc.sbuf_tensor("gsb0", [128, 24 * GH], BF16, side="right"))
    dmaG(GS[:, :], gw1s)
    dmaG(ONT[0:1, 0:256], onorm)
    n_gs = sc.n("dG")
    SRW = pgs.enter_context(nc.sbuf_tensor("srw", [1, 12 * GH], F32, side="right"))
    sc.wait("T", "dG", n_gs)
    sc.wait("T", "G", nG_pre)
    for s in range(12):
        def _mm(t, s=s):
            t.matmul(PS_WA[0:1, 0:512], ONESC[:, :], GS[:, (2 * s) * GH:(2 * s + 1) * GH],
                     start=True, stop=False)
            return t.matmul(PS_WA[0:1, 0:512], ONESC[:, :],
                            GS[:, (2 * s + 1) * GH:(2 * s + 2) * GH], start=False, stop=True)
        tk = sc.inc("T", "T", _mm)
        sc.wait("S", "T", tk)
        scale = (1.0 / 256.0) if s % 3 == 0 else 1.0
        sc.inc("S", "S", lambda sl, s=s, scale=scale: sl.activation(
            SRW[0:1, s * GH:(s + 1) * GH], PS_WA[0:1, 0:512], AF.Copy, scale=scale))
        sc.wait("T", "S")
    sc.inc("V", "V", lambda v: v.memset(STL[:, :], 0.0))
    sc.wait("G", "S")
    sc.wait("G", "V")
    for s in range(12):
        sc.inc("G", "dG", lambda g, s=s: g.dma_start(
            bass.AP(STL, s * GH, [[GH, 1], [1, GH]]),
            bass.AP(SRW, s * GH, [[12 * GH, 1], [1, GH]])), 16)

    SC1 = E1(nc.sbuf_tensor("sc1", [1, 32], F32))
    sc.inc("V", "V", lambda v: v.memset(SC1[:, :], 0.0))
    sc.wait("Y", "V")
    dmaY(SC1[0:1, 0:4], gb2)
    dmaY(SC1[0:1, 8:12], floor)
    dmaY(SC1[0:1, 16:17], temp)
    n_sc1 = sc.n("dY")
    sc.wait("S", "dY", n_sc1)
    sc.wait("S", "dG", n_gs)
    sc.wait("S", "G", nG_pre)
    sc.inc("S", "S", lambda sl: sl.copy(RR2[0:1, 0:32], SC1[0:1, 0:32]))
    sc.wait("T", "S")
    def _bc1(t):
        t.matmul(PS_WB[:, 0:4], ONE2[:, 0:128], RR2[0:2, 0:4], start=True, stop=True)
        t.matmul(PS_WB[:, 4:8], ONE2[:, 0:128], RR2[0:2, 8:12], start=True, stop=True)
        t.matmul(PS_WB[:, 8:9], ONE2[:, 0:128], RR2[0:2, 16:17], start=True, stop=True)
        return t.matmul(PS_WB[:, 16:272], ONE2[:, 0:128],
                        bass.AP(ONT, 0, [[256, 2], [1, 256]]), start=True, stop=True)
    tk = sc.inc("T", "T", _bc1)
    sc.wait("V", "T", tk)
    def _bc2(v):
        v.tensor_copy(GB2T[:, :], PS_WB[:, 0:4])
        v.tensor_scalar(FLT[:, :], PS_WB[:, 4:8], 1e-7, 0.1, ALU.max, ALU.min)
        v.tensor_scalar(TIV[:, :], PS_WB[:, 8:9], 0.2, 10.0, ALU.max, ALU.min)
        v.tensor_copy(TIV[:, 0:1], PS_WB[:, 8:9])
        v.reciprocal(TIV[:, :], TIV[:, :])
        v.tensor_scalar(FLT[:, :], PS_WB[:, 4:8], 1e-7, 0.1, ALU.max, ALU.min)
        return v.tensor_copy(ONT[:, :], PS_WB[:, 16:272])
    sc.inc("V", "V", _bc2)
    sc.wait("T", "V")

    pgs.close()
    sc.wait("G", "T")
    XT = [E1(nc.sbuf_tensor(f"xt{i}", [128, 4 * D], F32)) for i in range(2)]
    XLNT = E1(nc.sbuf_tensor("xlnt", [128, D], BF16))
    XLN = E1(nc.sbuf_tensor("xln", [128, 8 * PT], BF16))
    PQ = E1(nc.sbuf_tensor("pq", [128, 2 * 515], BF16))
    PK = E1(nc.sbuf_tensor("pk", [128, 2 * 515], BF16))
    PV = E1(nc.sbuf_tensor("pv", [128, 2 * 515], BF16))
    QF = E1(nc.sbuf_tensor("qf", [128, 2 * PT], BF16))
    KF = E1(nc.sbuf_tensor("kf", [128, 2 * PT], BF16))
    VF = E1(nc.sbuf_tensor("vf", [128, 2 * 542], BF16))
    KBF = E1(nc.sbuf_tensor("kbf", [128, 2 * PT], BF16))
    FSF = E1(nc.sbuf_tensor("fsf", [128, 2 * PT], BF16))
    FLF = E1(nc.sbuf_tensor("flf", [128, 2 * PT], BF16))
    DOF = E1(nc.sbuf_tensor("dof", [128, 2 * PT], BF16))
    OPF = E1(nc.sbuf_tensor("opf", [128, 2 * PT], BF16))
    OPART = E1(nc.sbuf_tensor("opart", [128, 4 * D], F32))
    SQF = E1(nc.sbuf_tensor("sqf", [128, 2 * PT], BF16))
    VTM = E1(nc.sbuf_tensor("vtm", [128, NCH * 256], BF16))
    FSTM = E1(nc.sbuf_tensor("fstm", [128, NCH * 256], BF16))
    FLTM = E1(nc.sbuf_tensor("fltm", [128, NCH * 256], BF16))
    DOTM = E1(nc.sbuf_tensor("dotm", [128, NCH * 256], BF16))
    KNTM = E1(nc.sbuf_tensor("kntm", [128, 256], BF16))
    KBTM = E1(nc.sbuf_tensor("kbtm", [128, 256], BF16))
    VBTM = E1(nc.sbuf_tensor("vbtm", [128, 256], BF16))
    UP = E1(nc.sbuf_tensor("upb", [128, 256], BF16))
    SB = E1(nc.sbuf_tensor("sbb", [128, 512], BF16))
    SFP = E1(nc.sbuf_tensor("sfp", [128, 512], F32))
    XPI = E1(nc.sbuf_tensor("xpi", [128, 128], BF16))
    XCt = E1(nc.sbuf_tensor("xcb", [128, 128], BF16))
    YCt = E1(nc.sbuf_tensor("ycb", [128, 128], BF16))
    PCt = E1(nc.sbuf_tensor("pcb", [128, 128], BF16))
    PTt = E1(nc.sbuf_tensor("ptcb", [128, 128], BF16))
    ATT = E1(nc.sbuf_tensor("attb", [128, 128], BF16))
    WNEG = E1(nc.sbuf_tensor("wneg", [128, 256], BF16))
    GSBt = E1(nc.sbuf_tensor("gsbt", [128, 4 * PT], BF16))
    STATS = E1(nc.sbuf_tensor("stats", [128, GH], BF16))
    SPK = E1(nc.sbuf_tensor("spk", [128, 16], BF16))
    WGt = E1(nc.sbuf_tensor("wgt", [128, 4], F32))
    OT1 = E1(nc.sbuf_tensor("ot1", [128, 256], BF16))
    OT2 = E1(nc.sbuf_tensor("ot2", [128, 256], BF16))
    OTB = E1(nc.sbuf_tensor("otb", [128, 256], BF16))
    ROWS = E1(nc.sbuf_tensor("rows", [128, 512], BF16))
    RRF = E1(nc.sbuf_tensor("rrf", [128, 1024], F32))
    COLS = E1(nc.sbuf_tensor("colsb", [128, 64], F32))
    SCR = E1(nc.sbuf_tensor("scrb", [128, 1024], BF16))
    SCF = E1(nc.sbuf_tensor("scfb", [128, 16], F32))
    ETM = E1(nc.sbuf_tensor("etm", [128, 4], F32))

    def _z1(v):
        v.memset(PQ[:, :], 0.0)
        v.memset(PK[:, :], 0.0)
        v.memset(PV[:, :], 0.0)
        v.memset(VF[:, :], 0.0)
        v.memset(ROWS[:, :], 0.0)
        v.memset(SPK[:, :], 0.0)
        v.memset(SB[:, :], 0.0)
        v.memset(SFP[:, :], 0.0)
        return v.memset(STATS[:, :], 0.0)
    sc.inc("V", "V", _z1)

    def load_x(p):
        buf = XT[p % 2]
        for t in range(4):
            dmaG(buf[:, t * D:(t + 1) * D],
                 x_attn[p * PT + t * 128:p * PT + (t + 1) * 128, :])
    load_x(0)

    sc.wait("T", "G", sc.n("G"))
    sc.wait("T", "V", nV_diag)
    sc.wait("S", "G", sc.n("G"))
    CB, CSK, CSQ, CBSK, CNSK = 0, 4, 8, 12, 16

    for p in range(npass):
        sc.wait("S", "V")
        sc.wait("T", "V")
        sc.wait("G", "S")
        sc.wait("S", "dY")
        buf = XT[p % 2]
        nx = sc.n("dG")
        if p + 1 < npass:
            load_x(p + 1)

        sc.wait("S", "dG", nx)
        for t in range(4):
            xt = buf[:, t * D:(t + 1) * D]
            def _lnacc(sl, xt=xt):
                sl.activation(SCR[:, 0:1024], xt, AF.Copy, accum_out=SCF[:, 0:1])
                sl.activation(SCR[:, 0:1024], xt, AF.Square, accum_out=SCF[:, 1:2])
                return sl.copy(SCF[:, 15:16], CSTT[:, 0:1])
            sc.inc("S", "S", _lnacc)
            sc.wait("V", "S")
            def _lncols(v):
                v.tensor_scalar(SCF[:, 2:3], SCF[:, 0:1], 1.0 / D, None, ALU.mult)
                v.tensor_scalar(SCF[:, 3:4], SCF[:, 1:2], 1.0 / D, None, ALU.mult)
                v.tensor_copy(SCF[:, 9:10], ZRO[:, 0:1])
                v.tensor_mul(SCF[:, 4:5], SCF[:, 2:3], SCF[:, 2:3])
                v.tensor_copy(SCF[:, 9:10], ZRO[:, 0:1])
                v.tensor_sub(SCF[:, 3:4], SCF[:, 3:4], SCF[:, 4:5])
                v.tensor_copy(SCF[:, 9:10], ZRO[:, 0:1])
                return v.tensor_scalar(SCF[:, 3:4], SCF[:, 3:4], 1e-5, None, ALU.add)
            sc.inc("V", "V", _lncols)
            sc.wait("S", "V")
            sc.inc("S", "S", lambda sl: sl.sqrt(SCF[:, 3:4], SCF[:, 3:4]))
            sc.wait("V", "S")
            def _lninv(v):
                v.reciprocal(SCF[:, 3:4], SCF[:, 3:4])
                v.tensor_copy(SCF[:, 5:6], SCF[:, 2:3])
                v.tensor_mul(SCF[:, 4:5], SCF[:, 2:3], SCF[:, 3:4])
                v.tensor_copy(SCF[:, 9:10], ZRO[:, 0:1])
                return v.tensor_scalar(SCF[:, 4:5], SCF[:, 4:5], -1.0, None, ALU.mult)
            sc.inc("V", "V", _lninv)
            sc.wait("S", "V")
            sc.inc("S", "S", lambda sl, xt=xt: sl.activation(
                XLNT[:, :], xt, AF.Identity, bias=SCF[:, 4:5], scale=SCF[:, 3:4]))
            nS_ln = sc.n("S")
            if p == npass - 1 and t == 3 and "xlnt" in dbg:
                sc.wait("G", "S", nS_ln)
                sc.wait("G", "V")
                dmaG(dbg["xlnt"], XLNT[:, :])
                dmaG(dbg["lncols"], SCF[:, 0:8])
                sc.wait("S", "dG")
                sc.wait("T", "dG")
                sc.wait("V", "dG")
            sc.wait("T", "S", nS_ln)
            def _trx(tt):
                for k in range(7):
                    tt.transpose(PS_TA[:, k * 128:(k + 1) * 128],
                                 XLNT[:, k * 128:(k + 1) * 128], IDb[:, :])
                return tt.transpose(PS_TA[:, 896:1024], XLNT[:, 896:1024], IDb[:, :])
            tk = sc.inc("T", "T", _trx)
            sc.wait("S", "T", tk)
            sc.inc("S", "S", lambda sl, t=t: sl.copy(
                bass.AP(XLN, t * 128, [[8 * PT, 128], [PT, 8], [1, 128]]),
                bass.AP(PS_TA, 0, [[1024, 128], [128, 8], [1, 128]])))
            sc.wait("T", "S")

        nS_xln = sc.n("S")

        sc.wait("T", "S", nS_xln)
        for W, P, pi in ((WQ, PQ, 0), (WK, PK, 1), (WV, PV, 2)):
            for ch in range(2):
                bank = PSB[(pi * 2 + ch) % 2]
                sc.wait("T", "S")
                def _proj(t, W=W, ch=ch, bank=bank):
                    for k in range(7):
                        t.matmul(bank[:, 0:512], W[:, k * 256 + ch * 128:k * 256 + ch * 128 + 128],
                                 XLN[:, k * PT:(k + 1) * PT], start=(k == 0), stop=False)
                    return t.matmul(bank[:, 0:512], W[:, 7 * 256 + ch * 128:7 * 256 + ch * 128 + 128],
                                    XLN[:, 7 * PT:8 * PT], start=False, stop=True)
                tk = sc.inc("T", "T", _proj)
                sc.wait("S", "T", tk)
                sc.inc("S", "S", lambda sl, P=P, ch=ch, bank=bank: sl.copy(
                    P[:, ch * 515 + 3:ch * 515 + 515], bank[:, 0:512]))
        def _betamm(t):
            for k in range(7):
                t.matmul(PS_WA[0:1, 0:512], WBc[:, k:k + 1], XLN[:, k * PT:(k + 1) * PT],
                         start=(k == 0), stop=False)
            return t.matmul(PS_WA[0:1, 0:512], WBc[:, 7:8], XLN[:, 7 * PT:8 * PT],
                            start=False, stop=True)
        sc.inc("T", "T", _betamm)
        nS_proj = sc.n("S")
        sc.wait("T", "S", nS_proj)
        for P, O, pi in ((PQ, QF, 0), (PK, KF, 1), (PV, VF, 2)):
            for ch in range(2):
                bank = PSB[(pi * 2 + ch) % 2]
                sc.wait("T", "S")
                def _conv(t, P=P, pi=pi, ch=ch, bank=bank):
                    for j in range(3):
                        t.matmul(bank[:, 0:512],
                                 CDIAG[:, (pi * 8 + ch * 4 + j) * 128:(pi * 8 + ch * 4 + j + 1) * 128],
                                 P[:, ch * 515 + j:ch * 515 + j + 512], start=(j == 0), stop=False)
                    return t.matmul(bank[:, 0:512],
                                    CDIAG[:, (pi * 8 + ch * 4 + 3) * 128:(pi * 8 + ch * 4 + 4) * 128],
                                    P[:, ch * 515 + 3:ch * 515 + 3 + 512], start=False, stop=True)
                tk = sc.inc("T", "T", _conv)
                sc.wait("S", "T", tk)
                off = (ch * 542 + 30) if O is VF else (ch * PT)
                sc.inc("S", "S", lambda sl, O=O, off=off, bank=bank: sl.activation(
                    O[:, off:off + 512], bank[:, 0:512], AF.Silu))
        nS_conv = sc.n("S")

        for ch in range(2):
            sc.inc("S", "S", lambda sl, ch=ch: sl.activation(
                SQF[:, ch * PT:(ch + 1) * PT], KF[:, ch * PT:(ch + 1) * PT], AF.Square))
        nS_ksq = sc.n("S")
        sc.wait("T", "S", nS_ksq)
        def _ksum(t):
            t.matmul(PS_WA[32:33, 0:512], ONESC[:, :], SQF[:, 0:PT], start=True, stop=False)
            return t.matmul(PS_WA[32:33, 0:512], ONESC[:, :], SQF[:, PT:2 * PT],
                            start=False, stop=True)
        tk_k = sc.inc("T", "T", _ksum)
        sc.wait("S", "T", tk_k)
        for ch in range(2):
            sc.inc("S", "S", lambda sl, ch=ch: sl.activation(
                SQF[:, ch * PT:(ch + 1) * PT], QF[:, ch * PT:(ch + 1) * PT], AF.Square))
        nS_qsq = sc.n("S")
        sc.wait("T", "S", nS_qsq)
        def _qsum(t):
            t.matmul(PS_WA[64:65, 0:512], ONESC[:, :], SQF[:, 0:PT], start=True, stop=False)
            return t.matmul(PS_WA[64:65, 0:512], ONESC[:, :], SQF[:, PT:2 * PT],
                            start=False, stop=True)
        tk_q = sc.inc("T", "T", _qsum)
        sc.wait("S", "T", tk_q)
        def _rows(sl):
            sl.activation(RRF[0:1, 0:512], PS_WA[0:1, 0:512], AF.Sigmoid)
            sl.activation(RRF[32:33, 0:512], PS_WA[32:33, 0:512], AF.Sqrt, bias=1e-6)
            return sl.activation(RRF[64:65, 0:512], PS_WA[64:65, 0:512], AF.Sqrt, bias=1e-6)
        sc.inc("S", "S", _rows)
        sc.wait("V", "S")
        def _rows2(v):
            v.reciprocal(RRF[32:33, 0:512], RRF[32:33, 0:512])
            v.tensor_copy(SCF[:, 9:10], ZRO[:, 0:1])
            v.reciprocal(RRF[64:65, 0:512], RRF[64:65, 0:512])
            v.tensor_copy(SCF[:, 9:10], ZRO[:, 0:1])
            v.tensor_copy(RRF[0:1, 512:1024], RRF[32:33, 0:512])
            v.tensor_copy(SCF[:, 9:10], ZRO[:, 0:1])
            return v.tensor_mul(RRF[0:1, 512:1024], RRF[0:1, 512:1024], RRF[0:1, 0:512])
        sc.inc("V", "V", _rows2)
        sc.wait("T", "V")
        sc.wait("S", "V")
        def _rows3(sl):
            sl.copy(ROWS[0:1, 0:512], RRF[0:1, 0:512])
            sl.copy(ROWS[32:33, 0:512], RRF[32:33, 0:512])
            sl.copy(ROWS[64:65, 0:512], RRF[64:65, 0:512])
            return sl.copy(RR2[0:1, 0:512], RRF[0:1, 512:1024])
        sc.inc("S", "S", _rows3)
        nS_rows = sc.n("S")
        sc.wait("T", "S", nS_rows)
        def _bskmm(t):
            return t.matmul(PS_WC[:, 0:512], ONE2[:, 0:128], RR2[0:2, 0:512],
                            start=True, stop=True)
        tk = sc.inc("T", "T", _bskmm)
        sc.wait("V", "T", tk)
        def _kbf(v):
            v.tensor_mul(KBF[:, 0:PT], KF[:, 0:PT], PS_WC[:, 0:512])
            return v.tensor_mul(KBF[:, PT:2 * PT], KF[:, PT:2 * PT], PS_WC[:, 0:512])
        sc.inc("V", "V", _kbf)
        sc.wait("T", "V")
        for c in range(NCH):
            def _trr(t, c=c):
                return t.transpose(PS_TB[:, 0:128], ROWS[:, c * 128:(c + 1) * 128], IDb[:, :])
            tk = sc.inc("T", "T", _trr)
            sc.wait("V", "T", tk)
            def _colx(v, c=c):
                v.tensor_copy(COLS[:, CB + c:CB + c + 1], PS_TB[:, 0:1])
                v.tensor_copy(COLS[:, CSK + c:CSK + c + 1], PS_TB[:, 32:33])
                v.tensor_copy(COLS[:, CSQ + c:CSQ + c + 1], PS_TB[:, 64:65])
                v.tensor_copy(SCF[:, 9:10], ZRO[:, 0:1])
                v.tensor_mul(COLS[:, CBSK + c:CBSK + c + 1], COLS[:, CB + c:CB + c + 1],
                             COLS[:, CSK + c:CSK + c + 1])
                v.tensor_copy(SCF[:, 9:10], ZRO[:, 0:1])
                return v.tensor_scalar(COLS[:, CNSK + c:CNSK + c + 1],
                                       COLS[:, CSK + c:CSK + c + 1], -1.0, None, ALU.mult)
            sc.inc("V", "V", _colx)
            sc.wait("T", "V")

        for (O, base, ntap) in ((FSF, 0, 3), (FLF, 6, 31)):
            for ch in range(2):
                bank = PSB[ch]
                sc.wait("T", "S")
                def _fir(t, O=O, base=base, ntap=ntap, ch=ch, bank=bank):
                    for j in range(ntap - 1):
                        t.matmul(bank[:, 0:512],
                                 FDIAG[:, (base + ch * ntap + j) * 128:(base + ch * ntap + j + 1) * 128],
                                 VF[:, ch * 542 + (31 - ntap) + j:ch * 542 + (31 - ntap) + j + 512],
                                 start=(j == 0), stop=False)
                    j = ntap - 1
                    return t.matmul(bank[:, 0:512],
                                    FDIAG[:, (base + ch * ntap + j) * 128:(base + ch * ntap + j + 1) * 128],
                                    VF[:, ch * 542 + (31 - ntap) + j:ch * 542 + (31 - ntap) + j + 512],
                                    start=False, stop=True)
                tk = sc.inc("T", "T", _fir)
                sc.wait("S", "T", tk)
                sc.inc("S", "S", lambda sl, O=O, ch=ch, bank=bank: sl.copy(
                    O[:, ch * PT:(ch + 1) * PT], bank[:, 0:512]))
        nS_fir = sc.n("S")

        for c in range(NCH):
            gc = p * NCH + c
            t0 = c * 128
            sc.wait("T", "S", nS_fir)
            sc.wait("T", "S")
            def _trc(t, t0=t0):
                srcs = [KF[:, t0:t0 + 128], KF[:, PT + t0:PT + t0 + 128],
                        VF[:, 30 + t0:30 + t0 + 128], VF[:, 542 + 30 + t0:542 + 30 + t0 + 128],
                        FSF[:, t0:t0 + 128], FSF[:, PT + t0:PT + t0 + 128],
                        FLF[:, t0:t0 + 128], FLF[:, PT + t0:PT + t0 + 128]]
                for i, s_ in enumerate(srcs[:-1]):
                    t.transpose(PS_TA[:, i * 128:(i + 1) * 128], s_, IDb[:, :])
                return t.transpose(PS_TA[:, 896:1024], srcs[-1], IDb[:, :])
            tk = sc.inc("T", "T", _trc)
            sc.wait("S", "T", tk)
            def _evc(sl, c=c):
                sl.activation(KNTM[:, :], PS_TA[:, 0:256], AF.Copy,
                              scale=COLS[:, CSK + c:CSK + c + 1])
                sl.activation(KBTM[:, :], PS_TA[:, 0:256], AF.Copy,
                              scale=COLS[:, CBSK + c:CBSK + c + 1])
                sl.activation(VBTM[:, :], PS_TA[:, 256:512], AF.Copy,
                              scale=COLS[:, CB + c:CB + c + 1])
                sl.copy(VTM[:, c * 256:(c + 1) * 256], PS_TA[:, 256:512])
                sl.copy(FSTM[:, c * 256:(c + 1) * 256], PS_TA[:, 512:768])
                return sl.copy(FLTM[:, c * 256:(c + 1) * 256], PS_TA[:, 768:1024])
            sc.inc("S", "S", _evc)
            nS_evc = sc.n("S")

            sc.wait("T", "S", nS_evc)
            def _grA(t, t0=t0):
                t.matmul(PS_WA[:, 0:128], KF[:, t0:t0 + 128], KBF[:, t0:t0 + 128],
                         start=True, stop=False)
                return t.matmul(PS_WA[:, 0:128], KF[:, PT + t0:PT + t0 + 128],
                                KBF[:, PT + t0:PT + t0 + 128], start=False, stop=True)
            tk = sc.inc("T", "T", _grA)
            sc.wait("V", "T", tk)
            sc.inc("V", "V", lambda v: v.tensor_mul(SCR[:, 0:128], PS_WA[:, 0:128], MKUS[:, :]))
            sc.wait("S", "V")
            sc.inc("S", "S", lambda sl, c=c: sl.activation(
                XCt[:, :], SCR[:, 0:128], AF.Copy, scale=COLS[:, CNSK + c:CNSK + c + 1]))
            nS_x = sc.n("S")
            sc.wait("T", "S", nS_x)
            tk = sc.inc("T", "T", lambda t: t.transpose(PS_TB[:, 0:128], XCt[:, :], IDb[:, :]))
            sc.wait("S", "T", tk)
            sc.inc("S", "S", lambda sl: sl.copy(YCt[:, :], PS_TB[:, 0:128]))
            nS_y = sc.n("S")
            if p == 0 and c == 0 and "xc0" in dbg:
                sc.wait("G", "S", nS_y)
                dmaG(dbg["xc0"], XCt[:, :])
                dmaG(dbg["kntm"], KNTM[:, :])
                dmaG(dbg["kbtm"], KBTM[:, :])
                dmaG(dbg["vbtm"], VBTM[:, :])
                sc.wait("S", "dG")
                sc.wait("T", "dG")

            sc.wait("V", "S", nS_y)
            def _pinit(v):
                v.tensor_add(PCt[:, :], XCt[:, :], IDb[:, :])
                return v.tensor_add(PTt[:, :], YCt[:, :], IDb[:, :])
            sc.inc("V", "V", _pinit)
            sc.wait("T", "V")
            sc.wait("S", "V")
            for j in range(1, 7):
                def _sq2(t):
                    t.matmul(PS_WA[:, 128:256], YCt[:, :], XCt[:, :], start=True, stop=True)
                    return t.matmul(PS_WA[:, 256:384], XCt[:, :], YCt[:, :], start=True, stop=True)
                tk = sc.inc("T", "T", _sq2)
                sc.wait("S", "T", tk)
                def _sqe(sl):
                    sl.copy(XCt[:, :], PS_WA[:, 128:256])
                    return sl.copy(YCt[:, :], PS_WA[:, 256:384])
                sc.inc("S", "S", _sqe)
                sc.wait("V", "S")
                sc.inc("V", "V", lambda v: v.tensor_add(XPI[:, :], XCt[:, :], IDb[:, :]))
                sc.wait("T", "V")
                def _upd(t):
                    t.matmul(PS_WB[:, 0:128], PTt[:, :], XPI[:, :], start=True, stop=True)
                    return t.matmul(PS_WB[:, 128:256], XPI[:, :], PTt[:, :], start=True, stop=True)
                tk = sc.inc("T", "T", _upd)
                sc.wait("S", "T", tk)
                def _pce(sl):
                    sl.copy(PCt[:, :], PS_WB[:, 0:128])
                    return sl.copy(PTt[:, :], PS_WB[:, 128:256])
                sc.inc("S", "S", _pce)
            nS_chain = sc.n("S")
            if p == 0 and c == 0 and "pc0" in dbg:
                sc.wait("G", "S", nS_chain)
                dmaG(dbg["pc0"], PCt[:, :])
                dmaG(dbg["ptc0"], PTt[:, :])
                sc.wait("S", "dG")
                sc.wait("T", "dG")

            sc.wait("T", "S", nS_chain)
            def _wmm(t):
                t.matmul(PS_WA[:, 0:128], KBTM[:, 0:128], PCt[:, :], start=True, stop=True)
                return t.matmul(PS_WA[:, 384:512], KBTM[:, 128:256], PCt[:, :],
                                start=True, stop=True)
            tk = sc.inc("T", "T", _wmm)
            sc.wait("S", "T", tk)
            def _wng(sl):
                sl.mul(WNEG[:, 0:128], PS_WA[:, 0:128], -1.0)
                return sl.mul(WNEG[:, 128:256], PS_WA[:, 384:512], -1.0)
            sc.inc("S", "S", _wng)
            nS_w = sc.n("S")
            sc.wait("T", "S", nS_w)
            def _ups(t):
                t.matmul(PS_WC[:, 0:256], PCt[:, :], VBTM[:, :], start=True, stop=False)
                t.matmul(PS_WC[:, 0:256], WNEG[:, 0:128], SB[:, 0:256], start=False, stop=False)
                return t.matmul(PS_WC[:, 0:256], WNEG[:, 128:256], SB[:, 256:512],
                                start=False, stop=True)
            tk = sc.inc("T", "T", _ups)
            sc.wait("S", "T", tk)
            sc.inc("S", "S", lambda sl: sl.copy(UP[:, :], PS_WC[:, 0:256]))
            nS_up = sc.n("S")
            if p == 0 and c == 0 and "up0" in dbg:
                sc.wait("G", "S", nS_up)
                dmaG(dbg["up0"], UP[:, :])
                dmaG(dbg["wneg0"], WNEG[:, :])
                sc.wait("S", "dG")
                sc.wait("T", "dG")

            def _grq(t, t0=t0):
                t.matmul(PS_WA[:, 128:256], KF[:, t0:t0 + 128], QF[:, t0:t0 + 128],
                         start=True, stop=False)
                return t.matmul(PS_WA[:, 128:256], KF[:, PT + t0:PT + t0 + 128],
                                QF[:, PT + t0:PT + t0 + 128], start=False, stop=True)
            tk = sc.inc("T", "T", _grq)
            sc.wait("V", "T", tk)
            sc.wait("V", "S", nS_up)
            sc.inc("V", "V", lambda v: v.tensor_mul(SCR[:, 128:256], PS_WA[:, 128:256], MKU[:, :]))
            sc.wait("S", "V")
            sc.inc("S", "S", lambda sl, c=c: sl.activation(
                ATT[:, :], SCR[:, 128:256], AF.Copy, scale=COLS[:, CSK + c:CSK + c + 1]))
            nS_att = sc.n("S")
            if p == 0 and c == 0 and "att0" in dbg:
                sc.wait("G", "S", nS_att)
                dmaG(dbg["att0"], ATT[:, :])
                sc.wait("S", "dG")
                sc.wait("T", "dG")
            sc.wait("T", "S", nS_att)
            def _o(t, t0=t0):
                t.matmul(PS_WC[:, 256:512], QF[:, t0:t0 + 128], SB[:, 0:256],
                         start=True, stop=False)
                t.matmul(PS_WC[:, 256:512], QF[:, PT + t0:PT + t0 + 128], SB[:, 256:512],
                         start=False, stop=False)
                return t.matmul(PS_WC[:, 256:512], ATT[:, :], UP[:, :], start=False, stop=True)
            tk = sc.inc("T", "T", _o)
            sc.wait("S", "T", tk)
            sc.inc("S", "S", lambda sl, c=c: sl.activation(
                DOTM[:, c * 256:(c + 1) * 256], PS_WC[:, 256:512], AF.Copy,
                scale=COLS[:, CSQ + c:CSQ + c + 1]))
            nS_do = sc.n("S")

            def _ds(t):
                t.matmul(PS_S[:, 0:256], KNTM[:, 0:128], UP[:, :], start=True, stop=True)
                return t.matmul(PS_S[:, 256:512], KNTM[:, 128:256], UP[:, :],
                                start=True, stop=True)
            tk = sc.inc("T", "T", _ds)
            sc.wait("V", "T", tk)
            def _sacc(v):
                v.tensor_add(SFP[:, 0:256], SFP[:, 0:256], PS_S[:, 0:256])
                return v.tensor_add(SFP[:, 256:512], SFP[:, 256:512], PS_S[:, 256:512])
            sc.inc("V", "V", _sacc)
            sc.wait("S", "V")
            sc.inc("S", "S", lambda sl: sl.copy(SB[:, :], SFP[:, :]))
            sc.wait("T", "V")

            sc.wait("T", "S")
            def _trd(t, c=c):
                t.transpose(PS_TB[:, 128:256], DOTM[:, c * 256:c * 256 + 128], IDb[:, :])
                return t.transpose(PS_TB[:, 256:384], DOTM[:, c * 256 + 128:c * 256 + 256],
                                   IDb[:, :])
            tk = sc.inc("T", "T", _trd)
            sc.wait("S", "T", tk)
            def _dof(sl, t0=t0):
                sl.copy(DOF[:, t0:t0 + 128], PS_TB[:, 128:256])
                return sl.copy(DOF[:, PT + t0:PT + t0 + 128], PS_TB[:, 256:384])
            sc.inc("S", "S", _dof)

            for (BR, bi) in ((FSTM, 0), (FLTM, 1), (DOTM, 2), (VTM, 3)):
                sl_ = BR[:, c * 256:(c + 1) * 256]
                def _redm(v, sl_=sl_, bi=bi):
                    with nc.allow_low_precision("stat cols feed bf16 matmul operands"):
                        v.reduce_sum(SPK[:, 3 * bi:3 * bi + 1], sl_, axis=AX.X)
                        return v.reduce_max(SPK[:, 3 * bi + 2:3 * bi + 3], sl_, axis=AX.X)
                sc.inc("V", "V", _redm)
                sc.inc("S", "S", lambda sl, sl_=sl_, bi=bi: sl.activation(
                    SCR[:, 0:256], sl_, AF.Square, accum_out=SCF[:, 8 + bi:9 + bi]))
            sc.inc("S", "S", lambda sl: sl.copy(SCF[:, 15:16], CSTT[:, 0:1]))
            sc.wait("V", "S")
            def _rstat(v):
                for bi in range(3):
                    v.tensor_scalar(SCF[:, 12 + bi:13 + bi], SCF[:, 8 + bi:9 + bi],
                                    1.0 / 256.0, 1e-8, ALU.mult, ALU.max)
                return v.tensor_scalar(SCF[:, 15:16], SCF[:, 11:12],
                                       1.0 / 256.0, 1e-8, ALU.mult, ALU.max)
            sc.inc("V", "V", _rstat)
            sc.wait("S", "V")
            def _rsq(sl):
                for bi in range(3):
                    sl.activation(SPK[:, 3 * bi + 1:3 * bi + 2], SCF[:, 12 + bi:13 + bi], AF.Sqrt)
                return sl.activation(SPK[:, 10:11], SCF[:, 15:16], AF.Sqrt)
            sc.inc("S", "S", _rsq)
            nS_spk = sc.n("S")
            sc.wait("T", "S", nS_spk)
            sc.wait("T", "V")
            tk = sc.inc("T", "T", lambda t: t.transpose(PS_TB[0:16, 384:512], SPK[:, 0:16],
                                                        IDb[:, :]))
            sc.wait("S", "T", tk)
            sc.inc("S", "S", lambda sl, c=c: sl.copy(
                STATS[0:16, c * 128:(c + 1) * 128], PS_TB[0:16, 384:512]))
            sc.wait("T", "S")

        def _halos(v):
            for P in (PQ, PK, PV):
                v.tensor_copy(P[:, 0:3], P[:, 512:515])
                v.tensor_copy(P[:, 515:518], P[:, 1027:1030])
            v.tensor_copy(VF[:, 0:30], VF[:, 512:542])
            return v.tensor_copy(VF[:, 542:572], VF[:, 1054:1084])
        sc.inc("V", "V", _halos)

        nS_br = sc.n("S")
        sc.wait("T", "S", nS_br)
        for m in range(4):
            bank = PSB[m % 2]
            sc.wait("T", "S")
            def _gate(t, m=m, bank=bank):
                for k in range(8):
                    t.matmul(bank[:, 0:512], GW1H[:, k * GH + m * 128:k * GH + m * 128 + 128],
                             XLN[:, k * PT:(k + 1) * PT], start=(k == 0), stop=False)
                for bi in range(4):
                    for c2 in range(2):
                        if bi == 0:
                            rhs = FSF[:, c2 * PT:(c2 + 1) * PT]
                        elif bi == 1:
                            rhs = FLF[:, c2 * PT:(c2 + 1) * PT]
                        elif bi == 2:
                            rhs = DOF[:, c2 * PT:(c2 + 1) * PT]
                        else:
                            rhs = VF[:, c2 * 542 + 30:c2 * 542 + 30 + 512]
                        t.matmul(bank[:, 0:512],
                                 GW1B[:, (bi * 2 + c2) * GH + m * 128:(bi * 2 + c2) * GH + m * 128 + 128],
                                 rhs, start=False, stop=False)
                return t.matmul(bank[:, 0:512], STL[0:16, m * 128:(m + 1) * 128],
                                STATS[0:16, :], start=False, stop=True)
            tk = sc.inc("T", "T", _gate)
            sc.wait("S", "T", tk)
            sc.inc("S", "S", lambda sl, m=m, bank=bank: sl.activation(
                GSBt[:, m * PT:(m + 1) * PT], bank[:, 0:512], AF.Gelu, bias=GB1[:, m:m + 1]))
        nS_gate = sc.n("S")

        sc.wait("T", "S", nS_gate)
        for c in range(NCH):
            t0 = c * 128
            def _lg(t, t0=t0):
                for mg in range(3):
                    t.matmul(PS_WB[:, 256:260], GSBt[:, mg * PT + t0:mg * PT + t0 + 128],
                             GW2[:, mg * 4:mg * 4 + 4], start=(mg == 0), stop=False)
                return t.matmul(PS_WB[:, 256:260], GSBt[:, 3 * PT + t0:3 * PT + t0 + 128],
                                GW2[:, 12:16], start=False, stop=True)
            tk = sc.inc("T", "T", _lg)
            sc.wait("V", "T", tk)
            def _sm1(v):
                v.tensor_add(ETM[:, :], PS_WB[:, 256:260], GB2T[:, :])
                v.tensor_copy(SCF[:, 9:10], ZRO[:, 0:1])
                v.tensor_scalar(ETM[:, :], ETM[:, :], TIV[:, 0:1], None, ALU.mult)
                v.tensor_copy(SCF[:, 9:10], ZRO[:, 0:1])
                v.reduce_max(SCF[:, 5:6], ETM[:, :], axis=AX.X)
                v.tensor_copy(SCF[:, 10:11], ZRO[:, 0:1])
                return v.tensor_scalar(SCF[:, 5:6], SCF[:, 5:6], -1.0, None, ALU.mult)
            sc.inc("V", "V", _sm1)
            sc.wait("T", "V")
            sc.wait("S", "V")
            sc.inc("S", "S", lambda sl: sl.activation(ETM[:, :], ETM[:, :], AF.Exp,
                                                      bias=SCF[:, 5:6]))
            sc.wait("V", "S")
            def _sm2(v, c=c):
                v.reduce_sum(SCF[:, 6:7], ETM[:, :], axis=AX.X)
                v.tensor_copy(SCF[:, 9:10], ZRO[:, 0:1])
                v.reciprocal(SCF[:, 6:7], SCF[:, 6:7])
                v.tensor_copy(SCF[:, 9:10], FLT[:, 0:1])
                v.tensor_scalar(WGt[:, :], ETM[:, :], SCF[:, 6:7], None, ALU.mult)
                v.tensor_copy(SCF[:, 9:10], ZRO[:, 0:1])
                v.tensor_max(WGt[:, :], WGt[:, :], FLT[:, :])
                v.tensor_copy(SCF[:, 9:10], ZRO[:, 0:1])
                v.reduce_sum(SCF[:, 6:7], WGt[:, :], axis=AX.X)
                v.tensor_copy(SCF[:, 9:10], ZRO[:, 0:1])
                v.reciprocal(SCF[:, 6:7], SCF[:, 6:7])
                v.tensor_copy(SCF[:, 9:10], FLT[:, 0:1])
                v.tensor_scalar(WGt[:, :], WGt[:, :], SCF[:, 6:7], None, ALU.mult)
                v.tensor_copy(SCF[:, 9:10], ZRO[:, 0:1])
                v.tensor_scalar(OT1[:, :], FSTM[:, c * 256:(c + 1) * 256], WGt[:, 0:1],
                                None, ALU.mult)
                v.tensor_copy(SCF[:, 9:10], ZRO[:, 0:1])
                v.scalar_tensor_tensor(OT2[:, :], FLTM[:, c * 256:(c + 1) * 256], WGt[:, 1:2],
                                       OT1[:, :], ALU.mult, ALU.add)
                v.tensor_copy(SCF[:, 9:10], ZRO[:, 0:1])
                v.scalar_tensor_tensor(OT1[:, :], DOTM[:, c * 256:(c + 1) * 256], WGt[:, 2:3],
                                       OT2[:, :], ALU.mult, ALU.add)
                v.tensor_copy(SCF[:, 9:10], ZRO[:, 0:1])
                return v.scalar_tensor_tensor(OT2[:, :], VTM[:, c * 256:(c + 1) * 256],
                                              WGt[:, 3:4], OT1[:, :], ALU.mult, ALU.add)
            sc.inc("V", "V", _sm2)
            sc.wait("S", "V")
            def _rmsacc(sl):
                sl.activation(SCR[:, 0:256], OT2[:, :], AF.Square, accum_out=SCF[:, 7:8])
                return sl.copy(SCF[:, 15:16], CSTT[:, 0:1])
            sc.inc("S", "S", _rmsacc)
            sc.wait("V", "S")
            sc.inc("V", "V", lambda v: v.tensor_scalar(SCF[:, 7:8], SCF[:, 7:8],
                                                       1.0 / 256.0, 1e-5, ALU.mult, ALU.add))
            sc.wait("S", "V")
            sc.inc("S", "S", lambda sl: sl.sqrt(SCF[:, 7:8], SCF[:, 7:8]))
            sc.wait("V", "S")
            sc.inc("V", "V", lambda v: v.reciprocal(SCF[:, 7:8], SCF[:, 7:8]))
            sc.wait("S", "V")
            sc.inc("S", "S", lambda sl: sl.activation(OT1[:, :], OT2[:, :], AF.Copy,
                                                      scale=SCF[:, 7:8]))
            sc.wait("V", "S")
            sc.inc("V", "V", lambda v: v.tensor_mul(OTB[:, :], OT1[:, :], ONT[:, :]))
            nV_o = sc.n("V")
            sc.wait("T", "V", nV_o)
            sc.wait("T", "S")
            def _tro(t):
                t.transpose(PS_TB[:, 512:640], OTB[:, 0:128], IDb[:, :])
                return t.transpose(PS_TB[:, 640:768], OTB[:, 128:256], IDb[:, :])
            tk = sc.inc("T", "T", _tro)
            sc.wait("S", "T", tk)
            def _opf(sl, t0=t0):
                sl.copy(OPF[:, t0:t0 + 128], PS_TB[:, 512:640])
                return sl.copy(OPF[:, PT + t0:PT + t0 + 128], PS_TB[:, 640:768])
            sc.inc("S", "S", _opf)
            sc.wait("T", "S")

        nS_opf = sc.n("S")
        sc.wait("T", "S", nS_opf)
        for tt in range(4):
            t0 = tt * 128
            for n_ in range(2):
                bank = PSB[n_]
                sc.wait("T", "S")
                def _wo(t, t0=t0, n_=n_, bank=bank):
                    t.matmul(bank[:, 0:512], OPF[:, t0:t0 + 128],
                             WOs[:, n_ * 512:n_ * 512 + 512], start=True, stop=False)
                    return t.matmul(bank[:, 0:512], OPF[:, PT + t0:PT + t0 + 128],
                                    WOs[:, 1024 + n_ * 512:1024 + n_ * 512 + 512],
                                    start=False, stop=True)
                tk = sc.inc("T", "T", _wo)
                sc.wait("S", "T", tk)
                sc.inc("S", "S", lambda sl, tt=tt, n_=n_, bank=bank: sl.copy(
                    OPART[:, tt * D + n_ * 512:tt * D + n_ * 512 + 512], bank[:, 0:512]))
        nS_op = sc.n("S")
        sc.wait("Y", "S", nS_op)
        def _ccw(y, p=p):
            return y.dma_start(
                bass.AP(cc_in, p * PT * D, [[D, 128], [128 * D, 4], [1, D]]),
                bass.AP(OPART, 0, [[4 * D, 128], [D, 4], [1, D]]))
        sc.inc("Y", "dY", _ccw, 16)

    if dbg:
        sc.wait("G", "S")
        sc.wait("G", "V")
        for name, ap_src in (("qf", QF), ("kf", KF), ("vf", VF), ("xln", XLN),
                             ("fsf", FSF), ("flf", FLF), ("dof", DOF), ("gsb", GSBt),
                             ("opf", OPF), ("opart", OPART), ("kbf", KBF),
                             ("vtm", VTM), ("dotm", DOTM), ("spk", SPK), ("wgt", WGt)):
            if name in dbg:
                dmaG(dbg[name], ap_src[:, :])
        if "beta" in dbg:
            dmaG(dbg["beta"], RRF[0:1, 0:512])
        if "xt" in dbg:
            dmaG(dbg["xt"], XT[(npass - 1) % 2][:, :])
        if "rows" in dbg:
            dmaG(dbg["rows"], RRF[:, :])
        if "stats" in dbg:
            dmaG(dbg["stats"], STATS[0:16, :])
        if "sb" in dbg:
            dmaG(dbg["sb"], SB[:, :])
        sc.wait("Y", "dG")

    n_cc = sc.n("dY")
    sc.wait("G", "dY", n_cc)
    sc.wait("G", "dG")
    def _rs(g):
        return g.collective_compute(
            "ReduceScatter", ALU.add,
            replica_groups=[[0, 1, 2, 3], [4, 5, 6, 7]],
            ins=[cc_in.ap().opt()], outs=[cc_rs.ap().opt()])
    sc.inc("G", "CC", _rs)
    p1.close()

    if not do_ffn:
        sc.wait("Y", "CC")
        AOT = E(nc.sbuf_tensor("aot", [128, D], F32))
        AOF = E(nc.sbuf_tensor("aof", [128, D], F32))
        for tt in range(TOK // 128):
            dmaY(AOT[:, :], cc_rs.ap()[tt * 128:(tt + 1) * 128, :])
            sc.wait("V", "dY")
            sc.inc("V", "V", lambda v: v.tensor_copy(AOF[:, :], AOT[:, :]))
            sc.wait("Y", "V")
            dmaY(out[tt * 128:(tt + 1) * 128, :], AOF[:, :])
        sc.wait("Y", "dY")
        nc_final(nc, sc)
        st.close()
        return nc, dbg

    p2 = ExitStack()
    E2 = p2.enter_context
    XRES = E2(nc.sbuf_tensor("xres", [128, D], F32))
    AOT = E2(nc.sbuf_tensor("aot", [128, D], F32))
    X2 = E2(nc.sbuf_tensor("x2b", [128, D], F32))
    X2LNT = E2(nc.sbuf_tensor("x2lnt", [128, D], BF16))
    X2LN = E2(nc.sbuf_tensor("x2ln", [128, 8 * 1024], BF16))
    SCF2 = E2(nc.sbuf_tensor("scf2", [128, 16], F32))
    S1C = E2(nc.sbuf_tensor("s1cb", [128, 32], F32))
    S2C = E2(nc.sbuf_tensor("s2cb", [128, 32], F32))
    SH1C = E2(nc.sbuf_tensor("sh1cb", [128, 32], F32))
    SH2C = E2(nc.sbuf_tensor("sh2cb", [128, 32], F32))
    S3C = E2(nc.sbuf_tensor("s3cb", [128, 8], F32))
    HT = E2(nc.sbuf_tensor("ht", [128, 32 * 1024], BF16))
    G1 = E2(nc.sbuf_tensor("g1", [128, 32 * 1024], BF16))
    RING = [E2(nc.sbuf_tensor(f"ring{i}", [128, 4096], BF16)) for i in range(3)]
    OASM = E2(nc.sbuf_tensor("oasm", [128, 8 * 1024], BF16))
    TM1 = E2(nc.sbuf_tensor("tm1", [128, 512], BF16))
    TM2 = E2(nc.sbuf_tensor("tm2", [128, 512], BF16))
    OUTT = E2(nc.sbuf_tensor("outt", [128, 1024], F32))

    dmaY(S1C[:, :], s1c)
    dmaY(S2C[:, :], s2c)
    dmaY(SH1C[:, :], sh1c)
    dmaY(SH2C[:, :], sh2c)
    dmaY(S3C[:, :], s3col)
    n_scl = sc.n("dY")

    sc.wait("G", "CC")
    ring_free = [0, 0, 0]

    for hh in range(2):
        tok0 = hh * 1024
        sc.wait("S", "dY", n_scl)
        sc.wait("V", "CC")
        sc.wait("Y", "CC")
        for tt in range(8):
            r0 = tok0 + tt * 128
            nlx = dmaY(XRES[:, :], x_res[r0:r0 + 128, :])
            nla = dmaY(AOT[:, :], cc_rs.ap()[r0:r0 + 128, :])
            sc.wait("V", "dY")
            sc.wait("V", "S")
            sc.inc("V", "V", lambda v: v.tensor_add(X2[:, :], XRES[:, :], AOT[:, :]))
            nV_x2 = sc.n("V")
            sc.wait("Y", "V", nV_x2)
            dmaY(x2stash[r0:r0 + 128, :], X2[:, :])
            sc.wait("S", "V", nV_x2)
            def _ln2acc(sl):
                sl.activation(OASM[:, 0:1024], X2[:, :], AF.Copy, accum_out=SCF2[:, 0:1])
                sl.activation(OASM[:, 0:1024], X2[:, :], AF.Square, accum_out=SCF2[:, 1:2])
                return sl.copy(SCF2[:, 15:16], CSTT[:, 0:1])
            sc.inc("S", "S", _ln2acc)
            sc.wait("V", "S")
            def _ln2c(v):
                v.tensor_scalar(SCF2[:, 2:3], SCF2[:, 0:1], 1.0 / D, None, ALU.mult)
                v.tensor_scalar(SCF2[:, 3:4], SCF2[:, 1:2], 1.0 / D, None, ALU.mult)
                v.tensor_copy(SCF2[:, 9:10], ZRO[:, 0:1])
                v.tensor_mul(SCF2[:, 4:5], SCF2[:, 2:3], SCF2[:, 2:3])
                v.tensor_copy(SCF2[:, 9:10], ZRO[:, 0:1])
                v.tensor_sub(SCF2[:, 3:4], SCF2[:, 3:4], SCF2[:, 4:5])
                v.tensor_copy(SCF2[:, 9:10], ZRO[:, 0:1])
                return v.tensor_scalar(SCF2[:, 3:4], SCF2[:, 3:4], 1e-5, None, ALU.add)
            sc.inc("V", "V", _ln2c)
            sc.wait("S", "V")
            sc.inc("S", "S", lambda sl: sl.sqrt(SCF2[:, 3:4], SCF2[:, 3:4]))
            sc.wait("V", "S")
            def _ln2i(v):
                v.reciprocal(SCF2[:, 3:4], SCF2[:, 3:4])
                v.tensor_copy(SCF2[:, 5:6], SCF2[:, 2:3])
                v.tensor_mul(SCF2[:, 4:5], SCF2[:, 2:3], SCF2[:, 3:4])
                v.tensor_copy(SCF2[:, 9:10], ZRO[:, 0:1])
                return v.tensor_scalar(SCF2[:, 4:5], SCF2[:, 4:5], -1.0, None, ALU.mult)
            sc.inc("V", "V", _ln2i)
            sc.wait("S", "V")
            sc.inc("S", "S", lambda sl: sl.activation(X2LNT[:, :], X2[:, :], AF.Identity,
                                                      bias=SCF2[:, 4:5], scale=SCF2[:, 3:4]))
            nS_l2 = sc.n("S")
            sc.wait("T", "S", nS_l2)
            def _trx2(t):
                for k in range(7):
                    t.transpose(PS_TA[:, k * 128:(k + 1) * 128],
                                X2LNT[:, k * 128:(k + 1) * 128], IDb[:, :])
                return t.transpose(PS_TA[:, 896:1024], X2LNT[:, 896:1024], IDb[:, :])
            tk = sc.inc("T", "T", _trx2)
            sc.wait("S", "T", tk)
            sc.inc("S", "S", lambda sl, tt=tt: sl.copy(
                bass.AP(X2LN, tt * 128, [[8 * 1024, 128], [1024, 8], [1, 128]]),
                bass.AP(PS_TA, 0, [[1024, 128], [128, 8], [1, 128]])))
            sc.wait("T", "S")

        nS_x2ln = sc.n("S")
        sc.wait("T", "S", nS_x2ln)
        if do_ffn == "x":
            continue

        for mg in range(8):
            sa, sb = (2 * mg) % 3, (2 * mg + 1) % 3
            sc.wait("G", "T", ring_free[sa])
            n1 = dmaG(RING[sa][:, :], w1t[mg])
            sc.wait("G", "T", ring_free[sb])
            n2 = dmaG(RING[sb][:, :], w2t[mg])
            sc.wait("T", "dG", n2)
            for m in range(4):
                gm = mg * 4 + m
                for t2 in range(2):
                    b1 = PSB[0] if t2 == 0 else PSB[1]
                    b2 = PS_WA if t2 == 0 else PS_WB
                    sc.wait("T", "S")
                    def _a(t, sa=sa, sb=sb, m=m, t2=t2, b1=b1, b2=b2):
                        for k in range(8):
                            t.matmul(b1[:, 0:512],
                                     RING[sa][:, k * 512 + m * 128:k * 512 + m * 128 + 128],
                                     X2LN[:, k * 1024 + t2 * 512:k * 1024 + t2 * 512 + 512],
                                     start=(k == 0), stop=(k == 7))
                        for k in range(7):
                            t.matmul(b2[:, 0:512],
                                     RING[sb][:, k * 512 + m * 128:k * 512 + m * 128 + 128],
                                     X2LN[:, k * 1024 + t2 * 512:k * 1024 + t2 * 512 + 512],
                                     start=(k == 0), stop=False)
                        return t.matmul(b2[:, 0:512],
                                        RING[sb][:, 7 * 512 + m * 128:7 * 512 + m * 128 + 128],
                                        X2LN[:, 7 * 1024 + t2 * 512:7 * 1024 + t2 * 512 + 512],
                                        start=False, stop=True)
                    tk = sc.inc("T", "T", _a)
                    sc.wait("S", "T", tk)
                    def _ha(sl, gm=gm, b1=b1, b2=b2):
                        sl.activation(TM1[:, :], b1[:, 0:512], AF.Silu, scale=S1C[:, gm:gm + 1])
                        return sl.activation(TM2[:, :], b2[:, 0:512], AF.Copy,
                                             scale=S2C[:, gm:gm + 1])
                    sc.inc("S", "S", _ha)
                    sc.wait("V", "S")
                    sc.inc("V", "V", lambda v, gm=gm, t2=t2: v.tensor_mul(
                        HT[:, gm * 1024 + t2 * 512:gm * 1024 + t2 * 512 + 512],
                        TM1[:, :], TM2[:, :]))
                    sc.wait("S", "V")
            ring_free[sa] = sc.n("T")
            ring_free[sb] = sc.n("T")

        nV_h = sc.n("V")
        sc.wait("T", "V", nV_h)
        if do_ffn == "a":
            continue

        def big_stage(wsrc, SRC, DST, SCL, act):
            banks = [PSB[0], PSB[1], PS_WA, PS_WB]
            for mg in range(16):
                sa, sb = (2 * mg) % 3, (2 * mg + 1) % 3
                sc.wait("G", "T", ring_free[sa])
                n1 = dmaG(RING[sa][:, :], wsrc[2 * mg])
                sc.wait("G", "T", ring_free[sb])
                n2 = dmaG(RING[sb][:, :], wsrc[2 * mg + 1])
                sc.wait("T", "dG", n2)
                sc.wait("T", "S")
                import os
                if os.environ.get("BSTAGE") == "dma":
                    continue
                def _grp(t, sa=sa, sb=sb):
                    last = None
                    for gi, (m, t2) in enumerate(((0, 0), (0, 1), (1, 0), (1, 1))):
                        for k in range(32):
                            slot, kk = (sa, k) if k < 16 else (sb, k - 16)
                            last = t.matmul(banks[gi][:, 0:512],
                                     RING[slot][:, kk * 256 + m * 128:kk * 256 + m * 128 + 128],
                                     SRC[:, k * 1024 + t2 * 512:k * 1024 + t2 * 512 + 512],
                                     start=(k == 0), stop=(k == 31))
                    return last
                tk2 = sc.inc("T", "T", _grp)
                ring_free[sa] = sc.n("T")
                ring_free[sb] = sc.n("T")
                if os.environ.get("BSTAGE") == "mm":
                    continue
                sc.wait("S", "T", tk2)
                def _ev(sl, mg=mg):
                    mode = os.environ.get("BSTAGE", "")
                    for gi, (m, t2) in enumerate(((0, 0), (0, 1), (1, 0), (1, 1))):
                        gm = mg * 2 + m
                        dst = DST[:, gm * 1024 + t2 * 512:gm * 1024 + t2 * 512 + 512]
                        if mode == "evcopy":
                            sl.copy(dst, banks[gi][:, 0:512])
                        elif mode == "evtm":
                            sl.copy(TM1[:, :], banks[gi][:, 0:512])
                        elif mode == "evsb":
                            sl.copy(dst, ZRO[:, 0:512])
                        elif mode == "evns":
                            sl.activation(dst, banks[gi][:, 0:512], act)
                        else:
                            sl.activation(dst, banks[gi][:, 0:512], act, scale=SCL[:, gm:gm + 1])
                    return sl.nop()
                sc.inc("S", "S", _ev)

        big_stage(h1t, HT, G1, SH1C, AF.Gelu)

        nS_g1 = sc.n("S")
        sc.wait("T", "S", nS_g1)
        if do_ffn == "b":
            continue
        big_stage(h2t, G1, HT, SH2C, AF.Gelu)

        nS_g2 = sc.n("S")
        sc.wait("T", "S", nS_g2)
        for m in range(8):
            slot = m % 3
            sc.wait("G", "T", ring_free[slot])
            nw = dmaG(RING[slot][:, :], w3t[m])
            sc.wait("T", "dG", nw)
            sc.wait("T", "S")
            def _d(t, slot=slot):
                last = None
                for n_ in range(2):
                    for k in range(32):
                        last = t.matmul(PSB[n_][:, 0:512],
                                 RING[slot][:, k * 128:(k + 1) * 128],
                                 HT[:, k * 1024 + n_ * 512:k * 1024 + n_ * 512 + 512],
                                 start=(k == 0), stop=(k == 31))
                return last
            tk = sc.inc("T", "T", _d)
            ring_free[slot] = sc.n("T")
            sc.wait("S", "T", tk)
            def _dev(sl, m=m):
                sl.activation(OASM[:, m * 1024:m * 1024 + 512], PSB[0][:, 0:512], AF.Copy,
                              scale=S3C[:, m:m + 1])
                return sl.activation(OASM[:, m * 1024 + 512:(m + 1) * 1024], PSB[1][:, 0:512],
                                     AF.Copy, scale=S3C[:, m:m + 1])
            sc.inc("S", "S", _dev)
        nS_ffo = sc.n("S")
        sc.wait("T", "S", nS_ffo)

        for tt in range(8):
            r0 = tok0 + tt * 128
            nrx = dmaY(XRES[:, :], x2stash[r0:r0 + 128, :])
            sc.wait("T", "V")
            def _trf(t, tt=tt):
                for m in range(7):
                    t.transpose(PS_TA[:, m * 128:(m + 1) * 128],
                                OASM[:, m * 1024 + tt * 128:m * 1024 + tt * 128 + 128], IDb[:, :])
                return t.transpose(PS_TA[:, 896:1024],
                                   OASM[:, 7 * 1024 + tt * 128:7 * 1024 + tt * 128 + 128],
                                   IDb[:, :])
            tk = sc.inc("T", "T", _trf)
            sc.wait("V", "T", tk)
            sc.wait("V", "dY")
            def _fadd(v):
                v.tensor_add(OUTT[:, 0:512], XRES[:, 0:512], PS_TA[:, 0:512])
                return v.tensor_add(OUTT[:, 512:1024], XRES[:, 512:1024], PS_TA[:, 512:1024])
            sc.inc("V", "V", _fadd)
            nV_out = sc.n("V")
            sc.wait("Y", "V", nV_out)
            dmaY(out[r0:r0 + 128, :], OUTT[:, :])
            sc.wait("T", "V", nV_out)

    if do_ffn is not True:
        sc.wait("V", "CC")
        sc.inc("V", "V", lambda v: v.memset(OUTT[:, :], 0.0))
        sc.wait("Y", "V")
        for tt in range(TOK // 128):
            dmaY(out[tt * 128:(tt + 1) * 128, :], OUTT[:, :])
    sc.wait("Y", "dY")
    nc_final(nc, sc)
    p2.close()
    st.close()
    return nc, dbg


def nc_final(nc, sc):
    with nc.Block() as block:
        def mk(key):
            def body(eng):
                for k, fn in sc.items:
                    if k == key:
                        fn(eng)
            return body
        block.tensor(mk("T"))
        block.vector(mk("V"))
        block.scalar(mk("S"))
        block.gpsimd(mk("G"))
        block.sync(mk("Y"))


_CACHE = {}


def kernel(**inputs):
    if "prog" not in _CACHE:
        _CACHE["prog"] = build_program()
    nc, _ = _CACHE["prog"]
    in_maps = [prepare_core_inputs(inputs, c) for c in range(8)]
    res = run_bass_kernel_spmd(nc, in_maps, core_ids=list(range(8)))
    outp = np.zeros((B, L, D), np.float32)
    for c in range(8):
        b, win = c // 4, c % 4
        outp[b, win * TOK:(win + 1) * TOK] = res.results[c]["out"]
    return outp


# version 05
# speedup vs baseline: 1.0230x; 1.0230x over previous
import numpy as np
import ml_dtypes
from contextlib import ExitStack

import concourse.bass as bass
import concourse.mybir as mybir
from concourse.bass_utils import run_bass_kernel_spmd

F32 = mybir.dt.float32
BF16 = mybir.dt.bfloat16
AF = mybir.ActivationFunctionType
ALU = mybir.AluOpType
AX = mybir.AxisListType

B, L, D = 2, 8192, 1024
CH, PT = 128, 512
NCH = PT // CH
TOK = 2048
GH = 512


class Sched:
    KEYS = ("T", "V", "S", "G", "Y")

    def __init__(self):
        self.items = []
        self.counts = {}
        self.h = {}

    def inc(self, key, sem, fn, n=1):
        self.counts[sem] = self.counts.get(sem, 0) + n

        def g(eng, fn=fn, sem=sem, n=n):
            fn(eng).then_inc(self.h[sem], n)
        self.items.append((key, g))
        return self.counts[sem]

    def run(self, key, fn):
        self.items.append((key, fn))

    def wait(self, key, sem, val=None):
        if val is None:
            val = self.counts.get(sem, 0)
        assert 0 <= val <= self.counts.get(sem, 0), (sem, val, self.counts.get(sem, 0))

        def g(eng, sem=sem, val=val):
            eng.wait_ge(self.h[sem], val)
        self.items.append((key, g))

    def n(self, sem):
        return self.counts.get(sem, 0)


def _lhsT(w):
    K, M = w.shape
    return np.ascontiguousarray(w.reshape(K // 128, 128, M).transpose(1, 0, 2).reshape(128, -1))


def prepare_core_inputs(A, core):
    b, h = core // 4, core % 4
    win = core % 4
    hs = slice(h * 256, (h + 1) * 256)
    d = {}
    d["x_attn"] = A["L0_attn_x"][b]
    xb = A["L0_attn_x"][b]
    d["x_res"] = np.concatenate([xb[win * 1024:win * 1024 + 1024],
                                 xb[4096 + win * 1024:4096 + win * 1024 + 1024]])
    d["wq"] = _lhsT(A["L0_attn_Wq"][:, hs])
    d["wk"] = _lhsT(A["L0_attn_Wk"][:, hs])
    d["wv"] = _lhsT(A["L0_attn_Wv"][:, hs])
    d["wb"] = _lhsT(A["L0_attn_Wb"][:, h:h + 1])
    cw = [A["L0_attn_" + nm][hs].reshape(2, 128, 4) for nm in ("qc_w", "kc_w", "vc_w")]
    d["convw"] = np.stack(cw, 0).reshape(6, 128, 4).transpose(1, 0, 2).reshape(128, 24)
    fs = A["L0_attn_fir_s"][h].reshape(2, 128, 3).transpose(1, 0, 2).reshape(128, 6)
    fl = A["L0_attn_fir_l"][h].reshape(2, 128, 31).transpose(1, 0, 2).reshape(128, 62)
    d["firw"] = np.concatenate([fs, fl], 1)
    g = A["L0_attn_gW1"]
    d["gw1h"] = _lhsT(g[0:1024])
    d["gw1b"] = g[4096:5120].reshape(8, 128, GH).transpose(1, 0, 2).reshape(128, 8 * GH)
    d["gw1s"] = g[1024:4096].reshape(24, 128, GH).transpose(1, 0, 2).reshape(128, 24 * GH)
    d["gb1"] = A["L0_attn_gb1"].reshape(4, 128).T
    d["gw2"] = A["L0_attn_gW2"].reshape(4, 128, 4).transpose(1, 0, 2).reshape(128, 16)
    d["gb2"] = A["L0_attn_gb2"].reshape(1, 4)
    d["temp"] = A["L0_attn_temp"][h].reshape(1, 1)
    d["floor"] = A["L0_attn_eps_floor"][h].reshape(1, 4)
    d["onorm"] = A["L0_attn_onorm_w"].reshape(1, 256)
    d["wo"] = A["L0_attn_Wo"][hs].reshape(2, 128, 1024).transpose(1, 0, 2).reshape(128, 2048)
    w1t, w2t = A["L0_ffn_w1_w"].T, A["L0_ffn_w2_w"].T
    h1t, h2t, w3t = A["L0_ffn_h1_w"].T, A["L0_ffn_h2_w"].T, A["L0_ffn_w3_w"].T
    d["w1t"] = w1t.reshape(8, 128, 8, 512).transpose(2, 1, 0, 3).reshape(8, 128, 4096)
    d["w2t"] = w2t.reshape(8, 128, 8, 512).transpose(2, 1, 0, 3).reshape(8, 128, 4096)
    d["h1t"] = (h1t.reshape(32, 128, 16, 256).transpose(2, 1, 0, 3).reshape(16, 128, 2, 4096)
                .transpose(0, 2, 1, 3).reshape(32, 128, 4096))
    d["h2t"] = (h2t.reshape(32, 128, 16, 256).transpose(2, 1, 0, 3).reshape(16, 128, 2, 4096)
                .transpose(0, 2, 1, 3).reshape(32, 128, 4096))
    d["w3t"] = w3t.reshape(32, 128, 8, 128).transpose(2, 1, 0, 3).reshape(8, 128, 4096)
    d["s1"] = A["L0_ffn_w1_s"].reshape(32, 128).T
    d["s2"] = A["L0_ffn_w2_s"].reshape(32, 128).T
    d["sh1"] = A["L0_ffn_h1_s"].reshape(32, 128).T
    d["sh2"] = A["L0_ffn_h2_s"].reshape(32, 128).T
    d["s3"] = A["L0_ffn_w3_s"].reshape(8, 128).T
    return {k: np.ascontiguousarray(v, dtype=np.float32) for k, v in d.items()}


def build_program(npass=L // PT, do_ffn=True, debug=()):
    nc = bass.Bass("TRN2", target_bir_lowering=False, detect_race_conditions=False)
    sc = Sched()
    split = (npass == L // PT)

    def din(name, shape, dt=F32):
        return nc.dram_tensor(name, list(shape), dt, kind="ExternalInput").ap()

    x_attn = din("x_attn", (L, D))
    x_res = din("x_res", (TOK, D))
    wq, wk, wv = din("wq", (128, 2048)), din("wk", (128, 2048)), din("wv", (128, 2048))
    wb = din("wb", (128, 8))
    convw, firw = din("convw", (128, 24)), din("firw", (128, 68))
    gw1h, gw1b = din("gw1h", (128, 8 * GH)), din("gw1b", (128, 8 * GH))
    gw1s = din("gw1s", (128, 24 * GH))
    gb1, gw2, gb2 = din("gb1", (128, 4)), din("gw2", (128, 16)), din("gb2", (1, 4))
    temp, floor, onorm = din("temp", (1, 1)), din("floor", (1, 4)), din("onorm", (1, 256))
    wo = din("wo", (128, 2048))
    w1t, w2t = din("w1t", (8, 128, 4096)), din("w2t", (8, 128, 4096))
    h1t, h2t = din("h1t", (32, 128, 4096)), din("h2t", (32, 128, 4096))
    w3t = din("w3t", (8, 128, 4096))
    s1c, s2c = din("s1", (128, 32)), din("s2", (128, 32))
    sh1c, sh2c = din("sh1", (128, 32)), din("sh2", (128, 32))
    s3col = din("s3", (128, 8))

    out = nc.dram_tensor("out", [TOK, D], F32, kind="ExternalOutput").ap()
    dbg = {}
    for name, shape in debug:
        dbg[name] = nc.dram_tensor("dbg_" + name, list(shape), F32, kind="ExternalOutput").ap()

    cc_in = nc.dram_tensor("cc_in", [L, D], F32, kind="Internal")
    cc_rs = nc.dram_tensor("cc_rs", [TOK, D], F32, kind="Internal")
    x2stash = nc.dram_tensor("x2stash", [TOK, D], F32, kind="Internal").ap()

    st = ExitStack()
    E = st.enter_context
    for s in ("dG", "dY", "T", "V", "S", "G", "CC"):
        sc.h[s] = E(nc.semaphore("sm" + s))

    PS_S = E(nc.psum_tensor("ps_s", [128, 512], F32))
    PS_TA = E(nc.psum_tensor("ps_ta", [128, 1024], BF16))
    PS_TB = E(nc.psum_tensor("ps_tb", [128, 1024], BF16))
    PS_WA = E(nc.psum_tensor("ps_wa", [128, 512], F32))
    PS_WB = E(nc.psum_tensor("ps_wb", [128, 512], F32))
    PS_WC = E(nc.psum_tensor("ps_wc", [128, 512], F32))
    PS_B0 = E(nc.psum_tensor("ps_b0", [128, 512], F32))
    PS_B1 = E(nc.psum_tensor("ps_b1", [128, 512], F32))
    PSB = [PS_B0, PS_B1]

    IDb = E(nc.sbuf_tensor("idb", [128, 128], BF16))
    IDF = E(nc.sbuf_tensor("idf", [128, 128], F32))
    ONE2 = E(nc.sbuf_tensor("one2", [2, 128], BF16))
    CSTT = E(nc.sbuf_tensor("cstt", [128, 4], F32))
    ZRO = E(nc.sbuf_tensor("zro", [128, 512], BF16))
    nc.const_aps.register(F32, 1e-6, CSTT[:, 0:1])
    p1 = ExitStack()
    E1 = p1.enter_context
    MKU = E1(nc.sbuf_tensor("mku", [128, 128], BF16))
    MKUS = E1(nc.sbuf_tensor("mkus", [128, 128], BF16))
    ONESC = E1(nc.sbuf_tensor("onesc", [128, 1], BF16))
    WQ = E1(nc.sbuf_tensor("wqb", [128, 2048], BF16))
    WK = E1(nc.sbuf_tensor("wkb", [128, 2048], BF16))
    WV = E1(nc.sbuf_tensor("wvb", [128, 2048], BF16))
    WBc = E1(nc.sbuf_tensor("wbb", [128, 8], BF16))
    WOs = E1(nc.sbuf_tensor("wos", [128, 2048], BF16))
    CDIAG = E1(nc.sbuf_tensor("cdiag", [128, 24 * 128], BF16))
    FDIAG = E1(nc.sbuf_tensor("fdiag", [128, 68 * 128], BF16))
    CVW = E1(nc.sbuf_tensor("cvw", [128, 24], F32))
    FIW = E1(nc.sbuf_tensor("fiw", [128, 68], F32))
    GW1H = E1(nc.sbuf_tensor("gw1hb", [128, 8 * GH], BF16))
    GW1B = E1(nc.sbuf_tensor("gw1bb", [128, 8 * GH], BF16))
    GW2 = E1(nc.sbuf_tensor("gw2b", [128, 16], BF16))
    GB1 = E1(nc.sbuf_tensor("gb1b", [128, 4], F32))
    STL = E1(nc.sbuf_tensor("stl", [128, GH], BF16))
    GB2T = E1(nc.sbuf_tensor("gb2t", [128, 4], F32))
    FLT = E1(nc.sbuf_tensor("flt", [128, 4], F32))
    ONT = E1(nc.sbuf_tensor("ont", [128, 256], BF16))
    TIV = E1(nc.sbuf_tensor("tiv", [128, 1], F32))
    RR2 = E1(nc.sbuf_tensor("rr2", [2, 512], BF16))

    def _prev(v):
        v.memset(IDb[:, :], 0.0)
        v.memset(IDF[:, :], 0.0)
        v.memset(MKU[:, :], 0.0)
        v.memset(MKUS[:, :], 0.0)
        v.memset(ONE2[:, :], 0.0)
        v.memset(ONE2[0:1, :], 1.0)
        v.memset(RR2[:, :], 0.0)
        v.memset(ONT[:, :], 0.0)
        v.memset(CSTT[:, :], 0.0)
        v.memset(ZRO[:, :], 0.0)
        v.memset(CSTT[:, 0:1], 1e-6)
        return v.memset(ONESC[:, :], 1.0)
    sc.inc("V", "V", _prev)
    sc.wait("G", "V")
    def _pre(g):
        g.affine_select(out=IDb[:, :], in_=IDb[:, :], compare_op=ALU.not_equal,
                        fill=1.0, base=0, pattern=[[-1, 128]], channel_multiplier=1)
        g.affine_select(out=IDF[:, :], in_=IDF[:, :], compare_op=ALU.not_equal,
                        fill=1.0, base=0, pattern=[[-1, 128]], channel_multiplier=1)
        g.affine_select(out=MKU[:, :], in_=MKU[:, :], compare_op=ALU.is_gt,
                        fill=1.0, base=0, pattern=[[-1, 128]], channel_multiplier=1)
        return g.affine_select(out=MKUS[:, :], in_=MKUS[:, :], compare_op=ALU.is_ge,
                               fill=1.0, base=0, pattern=[[-1, 128]], channel_multiplier=1)
    sc.inc("G", "G", _pre)
    nG_pre = sc.n("G")

    def dmaG(dst, src):
        return sc.inc("G", "dG", lambda g: g.dma_start(dst, src), 16)

    def dmaY(dst, src, **kw):
        return sc.inc("Y", "dY", lambda y: y.dma_start(dst, src, **kw), 16)

    for dst, src in ((WQ, wq), (WK, wk), (WV, wv), (WBc, wb), (CVW, convw), (FIW, firw),
                     (GW1H, gw1h), (GW1B, gw1b), (GW2, gw2), (GB1, gb1), (WOs, wo)):
        dmaG(dst[:, :], src)
    n_wl = sc.n("dG")

    sc.wait("V", "dG", n_wl)
    sc.wait("V", "G", nG_pre)
    def _diags(v):
        for i in range(24):
            v.tensor_scalar(CDIAG[:, i * 128:(i + 1) * 128], IDb[:, :], CVW[:, i:i + 1],
                            None, ALU.mult)
        for i in range(67):
            v.tensor_scalar(FDIAG[:, i * 128:(i + 1) * 128], IDb[:, :], FIW[:, i:i + 1],
                            None, ALU.mult)
        return v.tensor_scalar(FDIAG[:, 67 * 128:68 * 128], IDb[:, :], FIW[:, 67:68],
                               None, ALU.mult)
    sc.inc("V", "V", _diags)
    nV_diag = sc.n("V")

    pgs = ExitStack()
    GS = pgs.enter_context(nc.sbuf_tensor("gsb0", [128, 24 * GH], BF16, side="right"))
    dmaG(GS[:, :], gw1s)
    dmaG(ONT[0:1, 0:256], onorm)
    n_gs = sc.n("dG")
    SRW = pgs.enter_context(nc.sbuf_tensor("srw", [1, 12 * GH], F32, side="right"))
    sc.wait("T", "dG", n_gs)
    sc.wait("T", "G", nG_pre)
    for s in range(12):
        def _mm(t, s=s):
            t.matmul(PS_WA[0:1, 0:512], ONESC[:, :], GS[:, (2 * s) * GH:(2 * s + 1) * GH],
                     start=True, stop=False)
            return t.matmul(PS_WA[0:1, 0:512], ONESC[:, :],
                            GS[:, (2 * s + 1) * GH:(2 * s + 2) * GH], start=False, stop=True)
        tk = sc.inc("T", "T", _mm)
        sc.wait("S", "T", tk)
        scale = (1.0 / 256.0) if s % 3 == 0 else 1.0
        sc.inc("S", "S", lambda sl, s=s, scale=scale: sl.activation(
            SRW[0:1, s * GH:(s + 1) * GH], PS_WA[0:1, 0:512], AF.Copy, scale=scale))
        sc.wait("T", "S")
    sc.inc("V", "V", lambda v: v.memset(STL[:, :], 0.0))
    sc.wait("G", "S")
    sc.wait("G", "V")
    for s in range(12):
        sc.inc("G", "dG", lambda g, s=s: g.dma_start(
            bass.AP(STL, s * GH, [[GH, 1], [1, GH]]),
            bass.AP(SRW, s * GH, [[12 * GH, 1], [1, GH]])), 16)

    SC1 = E1(nc.sbuf_tensor("sc1", [1, 32], F32))
    sc.inc("V", "V", lambda v: v.memset(SC1[:, :], 0.0))
    sc.wait("Y", "V")
    dmaY(SC1[0:1, 0:4], gb2)
    dmaY(SC1[0:1, 8:12], floor)
    dmaY(SC1[0:1, 16:17], temp)
    n_sc1 = sc.n("dY")
    sc.wait("S", "dY", n_sc1)
    sc.wait("S", "dG", n_gs)
    sc.wait("S", "G", nG_pre)
    sc.inc("S", "S", lambda sl: sl.copy(RR2[0:1, 0:32], SC1[0:1, 0:32]))
    sc.wait("T", "S")
    def _bc1(t):
        t.matmul(PS_WB[:, 0:4], ONE2[:, 0:128], RR2[0:2, 0:4], start=True, stop=True)
        t.matmul(PS_WB[:, 4:8], ONE2[:, 0:128], RR2[0:2, 8:12], start=True, stop=True)
        t.matmul(PS_WB[:, 8:9], ONE2[:, 0:128], RR2[0:2, 16:17], start=True, stop=True)
        return t.matmul(PS_WB[:, 16:272], ONE2[:, 0:128],
                        bass.AP(ONT, 0, [[256, 2], [1, 256]]), start=True, stop=True)
    tk = sc.inc("T", "T", _bc1)
    sc.wait("V", "T", tk)
    def _bc2(v):
        v.tensor_copy(GB2T[:, :], PS_WB[:, 0:4])
        v.tensor_scalar(FLT[:, :], PS_WB[:, 4:8], 1e-7, 0.1, ALU.max, ALU.min)
        v.tensor_scalar(TIV[:, :], PS_WB[:, 8:9], 0.2, 10.0, ALU.max, ALU.min)
        v.tensor_copy(TIV[:, 0:1], PS_WB[:, 8:9])
        v.reciprocal(TIV[:, :], TIV[:, :])
        v.tensor_scalar(FLT[:, :], PS_WB[:, 4:8], 1e-7, 0.1, ALU.max, ALU.min)
        return v.tensor_copy(ONT[:, :], PS_WB[:, 16:272])
    sc.inc("V", "V", _bc2)
    sc.wait("T", "V")

    pgs.close()
    sc.wait("G", "T")
    XT = [E1(nc.sbuf_tensor(f"xt{i}", [128, 4 * D], F32)) for i in range(2)]
    XLNT = E1(nc.sbuf_tensor("xlnt", [128, D], BF16))
    XLN = E1(nc.sbuf_tensor("xln", [128, 8 * PT], BF16))
    PQ = E1(nc.sbuf_tensor("pq", [128, 2 * 515], BF16))
    PK = E1(nc.sbuf_tensor("pk", [128, 2 * 515], BF16))
    PV = E1(nc.sbuf_tensor("pv", [128, 2 * 515], BF16))
    QF = E1(nc.sbuf_tensor("qf", [128, 2 * PT], BF16))
    KF = E1(nc.sbuf_tensor("kf", [128, 2 * PT], BF16))
    VF = E1(nc.sbuf_tensor("vf", [128, 2 * 542], BF16))
    KBF = E1(nc.sbuf_tensor("kbf", [128, 2 * PT], BF16))
    FSF = E1(nc.sbuf_tensor("fsf", [128, 2 * PT], BF16))
    FLF = E1(nc.sbuf_tensor("flf", [128, 2 * PT], BF16))
    DOF = E1(nc.sbuf_tensor("dof", [128, 2 * PT], BF16))
    OPF = E1(nc.sbuf_tensor("opf", [128, 2 * PT], BF16))
    OPART = E1(nc.sbuf_tensor("opart", [128, 4 * D], F32))
    SQF = E1(nc.sbuf_tensor("sqf", [128, 2 * PT], BF16))
    VTM = E1(nc.sbuf_tensor("vtm", [128, NCH * 256], BF16))
    FSTM = E1(nc.sbuf_tensor("fstm", [128, NCH * 256], BF16))
    FLTM = E1(nc.sbuf_tensor("fltm", [128, NCH * 256], BF16))
    DOTM = E1(nc.sbuf_tensor("dotm", [128, NCH * 256], BF16))
    KNTM = E1(nc.sbuf_tensor("kntm", [128, 256], BF16))
    KBTM = E1(nc.sbuf_tensor("kbtm", [128, 256], BF16))
    VBTM = E1(nc.sbuf_tensor("vbtm", [128, 256], BF16))
    UP = E1(nc.sbuf_tensor("upb", [128, 256], BF16))
    SB = E1(nc.sbuf_tensor("sbb", [128, 512], BF16))
    SFP = E1(nc.sbuf_tensor("sfp", [128, 512], F32))
    XPI = E1(nc.sbuf_tensor("xpi", [128, 128], BF16))
    XCt = E1(nc.sbuf_tensor("xcb", [128, 128], BF16))
    YCt = E1(nc.sbuf_tensor("ycb", [128, 128], BF16))
    PCt = E1(nc.sbuf_tensor("pcb", [128, 128], BF16))
    PTt = E1(nc.sbuf_tensor("ptcb", [128, 128], BF16))
    ATT = E1(nc.sbuf_tensor("attb", [128, 128], BF16))
    WNEG = E1(nc.sbuf_tensor("wneg", [128, 256], BF16))
    GSBt = E1(nc.sbuf_tensor("gsbt", [128, 4 * PT], BF16))
    STATS = E1(nc.sbuf_tensor("stats", [128, GH], BF16))
    SPK = E1(nc.sbuf_tensor("spk", [128, 16], BF16))
    WGt = E1(nc.sbuf_tensor("wgt", [128, 4], F32))
    OT1 = E1(nc.sbuf_tensor("ot1", [128, 256], BF16))
    OT2 = E1(nc.sbuf_tensor("ot2", [128, 256], BF16))
    OTB = E1(nc.sbuf_tensor("otb", [128, 256], BF16))
    ROWS = E1(nc.sbuf_tensor("rows", [128, 512], BF16))
    RRF = E1(nc.sbuf_tensor("rrf", [128, 1024], F32))
    COLS = E1(nc.sbuf_tensor("colsb", [128, 64], F32))
    SCR = E1(nc.sbuf_tensor("scrb", [128, 1024], BF16))
    SCF = E1(nc.sbuf_tensor("scfb", [128, 16], F32))
    ETM = E1(nc.sbuf_tensor("etm", [128, 4], F32))

    def _z1(v):
        v.memset(PQ[:, :], 0.0)
        v.memset(PK[:, :], 0.0)
        v.memset(PV[:, :], 0.0)
        v.memset(VF[:, :], 0.0)
        v.memset(ROWS[:, :], 0.0)
        v.memset(SPK[:, :], 0.0)
        v.memset(SB[:, :], 0.0)
        v.memset(SFP[:, :], 0.0)
        return v.memset(STATS[:, :], 0.0)
    sc.inc("V", "V", _z1)

    def load_x(p):
        buf = XT[p % 2]
        for t in range(4):
            dmaG(buf[:, t * D:(t + 1) * D],
                 x_attn[p * PT + t * 128:p * PT + (t + 1) * 128, :])
    load_x(0)

    sc.wait("T", "G", sc.n("G"))
    sc.wait("T", "V", nV_diag)
    sc.wait("S", "G", sc.n("G"))
    CB, CSK, CSQ, CBSK, CNSK = 0, 4, 8, 12, 16

    for p in range(npass):
        sc.wait("S", "V")
        sc.wait("T", "V")
        sc.wait("G", "S")
        sc.wait("S", "dY")
        buf = XT[p % 2]
        nx = sc.n("dG")
        if p + 1 < npass:
            load_x(p + 1)
        if split and p == 8:
            sc.wait("G", "dY")
            sc.wait("G", "dG")
            def _rs1(g):
                return g.collective_compute(
                    "ReduceScatter", ALU.add,
                    replica_groups=[[0, 1, 2, 3], [4, 5, 6, 7]],
                    ins=[cc_in.ap()[0:L // 2, :].opt()],
                    outs=[cc_rs.ap()[0:TOK // 2, :].opt()])
            sc.inc("G", "CC", _rs1)
            sc.wait("G", "CC")

        sc.wait("S", "dG", nx)
        for t in range(4):
            xt = buf[:, t * D:(t + 1) * D]
            def _lnacc(sl, xt=xt):
                sl.activation(SCR[:, 0:1024], xt, AF.Copy, accum_out=SCF[:, 0:1])
                sl.activation(SCR[:, 0:1024], xt, AF.Square, accum_out=SCF[:, 1:2])
                return sl.copy(SCF[:, 15:16], CSTT[:, 0:1])
            sc.inc("S", "S", _lnacc)
            sc.wait("V", "S")
            def _lncols(v):
                v.tensor_scalar(SCF[:, 2:3], SCF[:, 0:1], 1.0 / D, None, ALU.mult)
                v.tensor_scalar(SCF[:, 3:4], SCF[:, 1:2], 1.0 / D, None, ALU.mult)
                v.tensor_copy(SCF[:, 9:10], ZRO[:, 0:1])
                v.tensor_mul(SCF[:, 4:5], SCF[:, 2:3], SCF[:, 2:3])
                v.tensor_copy(SCF[:, 9:10], ZRO[:, 0:1])
                v.tensor_sub(SCF[:, 3:4], SCF[:, 3:4], SCF[:, 4:5])
                v.tensor_copy(SCF[:, 9:10], ZRO[:, 0:1])
                return v.tensor_scalar(SCF[:, 3:4], SCF[:, 3:4], 1e-5, None, ALU.add)
            sc.inc("V", "V", _lncols)
            sc.wait("S", "V")
            sc.inc("S", "S", lambda sl: sl.sqrt(SCF[:, 3:4], SCF[:, 3:4]))
            sc.wait("V", "S")
            def _lninv(v):
                v.reciprocal(SCF[:, 3:4], SCF[:, 3:4])
                v.tensor_copy(SCF[:, 5:6], SCF[:, 2:3])
                v.tensor_mul(SCF[:, 4:5], SCF[:, 2:3], SCF[:, 3:4])
                v.tensor_copy(SCF[:, 9:10], ZRO[:, 0:1])
                return v.tensor_scalar(SCF[:, 4:5], SCF[:, 4:5], -1.0, None, ALU.mult)
            sc.inc("V", "V", _lninv)
            sc.wait("S", "V")
            sc.inc("S", "S", lambda sl, xt=xt: sl.activation(
                XLNT[:, :], xt, AF.Identity, bias=SCF[:, 4:5], scale=SCF[:, 3:4]))
            nS_ln = sc.n("S")
            if p == npass - 1 and t == 3 and "xlnt" in dbg:
                sc.wait("G", "S", nS_ln)
                sc.wait("G", "V")
                dmaG(dbg["xlnt"], XLNT[:, :])
                dmaG(dbg["lncols"], SCF[:, 0:8])
                sc.wait("S", "dG")
                sc.wait("T", "dG")
                sc.wait("V", "dG")
            sc.wait("T", "S", nS_ln)
            def _trx(tt):
                for k in range(7):
                    tt.transpose(PS_TA[:, k * 128:(k + 1) * 128],
                                 XLNT[:, k * 128:(k + 1) * 128], IDb[:, :])
                return tt.transpose(PS_TA[:, 896:1024], XLNT[:, 896:1024], IDb[:, :])
            tk = sc.inc("T", "T", _trx)
            sc.wait("S", "T", tk)
            sc.inc("S", "S", lambda sl, t=t: sl.copy(
                bass.AP(XLN, t * 128, [[8 * PT, 128], [PT, 8], [1, 128]]),
                bass.AP(PS_TA, 0, [[1024, 128], [128, 8], [1, 128]])))
            sc.wait("T", "S")

        nS_xln = sc.n("S")

        sc.wait("T", "S", nS_xln)
        for W, P, pi in ((WQ, PQ, 0), (WK, PK, 1), (WV, PV, 2)):
            for ch in range(2):
                bank = PSB[(pi * 2 + ch) % 2]
                sc.wait("T", "S")
                def _proj(t, W=W, ch=ch, bank=bank):
                    for k in range(7):
                        t.matmul(bank[:, 0:512], W[:, k * 256 + ch * 128:k * 256 + ch * 128 + 128],
                                 XLN[:, k * PT:(k + 1) * PT], start=(k == 0), stop=False)
                    return t.matmul(bank[:, 0:512], W[:, 7 * 256 + ch * 128:7 * 256 + ch * 128 + 128],
                                    XLN[:, 7 * PT:8 * PT], start=False, stop=True)
                tk = sc.inc("T", "T", _proj)
                sc.wait("S", "T", tk)
                sc.inc("S", "S", lambda sl, P=P, ch=ch, bank=bank: sl.copy(
                    P[:, ch * 515 + 3:ch * 515 + 515], bank[:, 0:512]))
        def _betamm(t):
            for k in range(7):
                t.matmul(PS_WA[0:1, 0:512], WBc[:, k:k + 1], XLN[:, k * PT:(k + 1) * PT],
                         start=(k == 0), stop=False)
            return t.matmul(PS_WA[0:1, 0:512], WBc[:, 7:8], XLN[:, 7 * PT:8 * PT],
                            start=False, stop=True)
        sc.inc("T", "T", _betamm)
        nS_proj = sc.n("S")
        sc.wait("T", "S", nS_proj)
        for P, O, pi in ((PQ, QF, 0), (PK, KF, 1), (PV, VF, 2)):
            for ch in range(2):
                bank = PSB[(pi * 2 + ch) % 2]
                sc.wait("T", "S")
                def _conv(t, P=P, pi=pi, ch=ch, bank=bank):
                    for j in range(3):
                        t.matmul(bank[:, 0:512],
                                 CDIAG[:, (pi * 8 + ch * 4 + j) * 128:(pi * 8 + ch * 4 + j + 1) * 128],
                                 P[:, ch * 515 + j:ch * 515 + j + 512], start=(j == 0), stop=False)
                    return t.matmul(bank[:, 0:512],
                                    CDIAG[:, (pi * 8 + ch * 4 + 3) * 128:(pi * 8 + ch * 4 + 4) * 128],
                                    P[:, ch * 515 + 3:ch * 515 + 3 + 512], start=False, stop=True)
                tk = sc.inc("T", "T", _conv)
                sc.wait("S", "T", tk)
                off = (ch * 542 + 30) if O is VF else (ch * PT)
                sc.inc("S", "S", lambda sl, O=O, off=off, bank=bank: sl.activation(
                    O[:, off:off + 512], bank[:, 0:512], AF.Silu))
        nS_conv = sc.n("S")

        for ch in range(2):
            sc.inc("S", "S", lambda sl, ch=ch: sl.activation(
                SQF[:, ch * PT:(ch + 1) * PT], KF[:, ch * PT:(ch + 1) * PT], AF.Square))
        nS_ksq = sc.n("S")
        sc.wait("T", "S", nS_ksq)
        def _ksum(t):
            t.matmul(PS_WA[32:33, 0:512], ONESC[:, :], SQF[:, 0:PT], start=True, stop=False)
            return t.matmul(PS_WA[32:33, 0:512], ONESC[:, :], SQF[:, PT:2 * PT],
                            start=False, stop=True)
        tk_k = sc.inc("T", "T", _ksum)
        sc.wait("S", "T", tk_k)
        for ch in range(2):
            sc.inc("S", "S", lambda sl, ch=ch: sl.activation(
                SQF[:, ch * PT:(ch + 1) * PT], QF[:, ch * PT:(ch + 1) * PT], AF.Square))
        nS_qsq = sc.n("S")
        sc.wait("T", "S", nS_qsq)
        def _qsum(t):
            t.matmul(PS_WA[64:65, 0:512], ONESC[:, :], SQF[:, 0:PT], start=True, stop=False)
            return t.matmul(PS_WA[64:65, 0:512], ONESC[:, :], SQF[:, PT:2 * PT],
                            start=False, stop=True)
        tk_q = sc.inc("T", "T", _qsum)
        sc.wait("S", "T", tk_q)
        def _rows(sl):
            sl.activation(RRF[0:1, 0:512], PS_WA[0:1, 0:512], AF.Sigmoid)
            sl.activation(RRF[32:33, 0:512], PS_WA[32:33, 0:512], AF.Sqrt, bias=1e-6)
            return sl.activation(RRF[64:65, 0:512], PS_WA[64:65, 0:512], AF.Sqrt, bias=1e-6)
        sc.inc("S", "S", _rows)
        sc.wait("V", "S")
        def _rows2(v):
            v.reciprocal(RRF[32:33, 0:512], RRF[32:33, 0:512])
            v.tensor_copy(SCF[:, 9:10], ZRO[:, 0:1])
            v.reciprocal(RRF[64:65, 0:512], RRF[64:65, 0:512])
            v.tensor_copy(SCF[:, 9:10], ZRO[:, 0:1])
            v.tensor_copy(RRF[0:1, 512:1024], RRF[32:33, 0:512])
            v.tensor_copy(SCF[:, 9:10], ZRO[:, 0:1])
            return v.tensor_mul(RRF[0:1, 512:1024], RRF[0:1, 512:1024], RRF[0:1, 0:512])
        sc.inc("V", "V", _rows2)
        sc.wait("T", "V")
        sc.wait("S", "V")
        def _rows3(sl):
            sl.copy(ROWS[0:1, 0:512], RRF[0:1, 0:512])
            sl.copy(ROWS[32:33, 0:512], RRF[32:33, 0:512])
            sl.copy(ROWS[64:65, 0:512], RRF[64:65, 0:512])
            return sl.copy(RR2[0:1, 0:512], RRF[0:1, 512:1024])
        sc.inc("S", "S", _rows3)
        nS_rows = sc.n("S")
        sc.wait("T", "S", nS_rows)
        def _bskmm(t):
            return t.matmul(PS_WC[:, 0:512], ONE2[:, 0:128], RR2[0:2, 0:512],
                            start=True, stop=True)
        tk = sc.inc("T", "T", _bskmm)
        sc.wait("V", "T", tk)
        def _kbf(v):
            v.tensor_mul(KBF[:, 0:PT], KF[:, 0:PT], PS_WC[:, 0:512])
            return v.tensor_mul(KBF[:, PT:2 * PT], KF[:, PT:2 * PT], PS_WC[:, 0:512])
        sc.inc("V", "V", _kbf)
        sc.wait("T", "V")
        for c in range(NCH):
            def _trr(t, c=c):
                return t.transpose(PS_TB[:, 0:128], ROWS[:, c * 128:(c + 1) * 128], IDb[:, :])
            tk = sc.inc("T", "T", _trr)
            sc.wait("V", "T", tk)
            def _colx(v, c=c):
                v.tensor_copy(COLS[:, CB + c:CB + c + 1], PS_TB[:, 0:1])
                v.tensor_copy(COLS[:, CSK + c:CSK + c + 1], PS_TB[:, 32:33])
                v.tensor_copy(COLS[:, CSQ + c:CSQ + c + 1], PS_TB[:, 64:65])
                v.tensor_copy(SCF[:, 9:10], ZRO[:, 0:1])
                v.tensor_mul(COLS[:, CBSK + c:CBSK + c + 1], COLS[:, CB + c:CB + c + 1],
                             COLS[:, CSK + c:CSK + c + 1])
                v.tensor_copy(SCF[:, 9:10], ZRO[:, 0:1])
                return v.tensor_scalar(COLS[:, CNSK + c:CNSK + c + 1],
                                       COLS[:, CSK + c:CSK + c + 1], -1.0, None, ALU.mult)
            sc.inc("V", "V", _colx)
            sc.wait("T", "V")

        for (O, base, ntap) in ((FSF, 0, 3), (FLF, 6, 31)):
            for ch in range(2):
                bank = PSB[ch]
                sc.wait("T", "S")
                def _fir(t, O=O, base=base, ntap=ntap, ch=ch, bank=bank):
                    for j in range(ntap - 1):
                        t.matmul(bank[:, 0:512],
                                 FDIAG[:, (base + ch * ntap + j) * 128:(base + ch * ntap + j + 1) * 128],
                                 VF[:, ch * 542 + (31 - ntap) + j:ch * 542 + (31 - ntap) + j + 512],
                                 start=(j == 0), stop=False)
                    j = ntap - 1
                    return t.matmul(bank[:, 0:512],
                                    FDIAG[:, (base + ch * ntap + j) * 128:(base + ch * ntap + j + 1) * 128],
                                    VF[:, ch * 542 + (31 - ntap) + j:ch * 542 + (31 - ntap) + j + 512],
                                    start=False, stop=True)
                tk = sc.inc("T", "T", _fir)
                sc.wait("S", "T", tk)
                sc.inc("S", "S", lambda sl, O=O, ch=ch, bank=bank: sl.copy(
                    O[:, ch * PT:(ch + 1) * PT], bank[:, 0:512]))
        nS_fir = sc.n("S")

        for c in range(NCH):
            gc = p * NCH + c
            t0 = c * 128
            sc.wait("T", "S", nS_fir)
            sc.wait("T", "S")
            def _trc(t, t0=t0):
                srcs = [KF[:, t0:t0 + 128], KF[:, PT + t0:PT + t0 + 128],
                        VF[:, 30 + t0:30 + t0 + 128], VF[:, 542 + 30 + t0:542 + 30 + t0 + 128],
                        FSF[:, t0:t0 + 128], FSF[:, PT + t0:PT + t0 + 128],
                        FLF[:, t0:t0 + 128], FLF[:, PT + t0:PT + t0 + 128]]
                for i, s_ in enumerate(srcs[:-1]):
                    t.transpose(PS_TA[:, i * 128:(i + 1) * 128], s_, IDb[:, :])
                return t.transpose(PS_TA[:, 896:1024], srcs[-1], IDb[:, :])
            tk = sc.inc("T", "T", _trc)
            sc.wait("S", "T", tk)
            def _evc(sl, c=c):
                sl.activation(KNTM[:, :], PS_TA[:, 0:256], AF.Copy,
                              scale=COLS[:, CSK + c:CSK + c + 1])
                sl.activation(KBTM[:, :], PS_TA[:, 0:256], AF.Copy,
                              scale=COLS[:, CBSK + c:CBSK + c + 1])
                sl.activation(VBTM[:, :], PS_TA[:, 256:512], AF.Copy,
                              scale=COLS[:, CB + c:CB + c + 1])
                sl.copy(VTM[:, c * 256:(c + 1) * 256], PS_TA[:, 256:512])
                sl.copy(FSTM[:, c * 256:(c + 1) * 256], PS_TA[:, 512:768])
                return sl.copy(FLTM[:, c * 256:(c + 1) * 256], PS_TA[:, 768:1024])
            sc.inc("S", "S", _evc)
            nS_evc = sc.n("S")

            sc.wait("T", "S", nS_evc)
            def _grA(t, t0=t0):
                t.matmul(PS_WA[:, 0:128], KF[:, t0:t0 + 128], KBF[:, t0:t0 + 128],
                         start=True, stop=False)
                return t.matmul(PS_WA[:, 0:128], KF[:, PT + t0:PT + t0 + 128],
                                KBF[:, PT + t0:PT + t0 + 128], start=False, stop=True)
            tk = sc.inc("T", "T", _grA)
            sc.wait("V", "T", tk)
            sc.inc("V", "V", lambda v: v.tensor_mul(SCR[:, 0:128], PS_WA[:, 0:128], MKUS[:, :]))
            sc.wait("S", "V")
            sc.inc("S", "S", lambda sl, c=c: sl.activation(
                XCt[:, :], SCR[:, 0:128], AF.Copy, scale=COLS[:, CNSK + c:CNSK + c + 1]))
            nS_x = sc.n("S")
            sc.wait("T", "S", nS_x)
            tk = sc.inc("T", "T", lambda t: t.transpose(PS_TB[:, 0:128], XCt[:, :], IDb[:, :]))
            sc.wait("S", "T", tk)
            sc.inc("S", "S", lambda sl: sl.copy(YCt[:, :], PS_TB[:, 0:128]))
            nS_y = sc.n("S")
            if p == 0 and c == 0 and "xc0" in dbg:
                sc.wait("G", "S", nS_y)
                dmaG(dbg["xc0"], XCt[:, :])
                dmaG(dbg["kntm"], KNTM[:, :])
                dmaG(dbg["kbtm"], KBTM[:, :])
                dmaG(dbg["vbtm"], VBTM[:, :])
                sc.wait("S", "dG")
                sc.wait("T", "dG")

            sc.wait("V", "S", nS_y)
            def _pinit(v):
                v.tensor_add(PCt[:, :], XCt[:, :], IDb[:, :])
                return v.tensor_add(PTt[:, :], YCt[:, :], IDb[:, :])
            sc.inc("V", "V", _pinit)
            sc.wait("T", "V")
            sc.wait("S", "V")
            for j in range(1, 7):
                def _sq2(t):
                    t.matmul(PS_WA[:, 128:256], YCt[:, :], XCt[:, :], start=True, stop=True)
                    return t.matmul(PS_WA[:, 256:384], XCt[:, :], YCt[:, :], start=True, stop=True)
                tk = sc.inc("T", "T", _sq2)
                sc.wait("S", "T", tk)
                def _sqe(sl):
                    sl.copy(XCt[:, :], PS_WA[:, 128:256])
                    return sl.copy(YCt[:, :], PS_WA[:, 256:384])
                sc.inc("S", "S", _sqe)
                sc.wait("V", "S")
                sc.inc("V", "V", lambda v: v.tensor_add(XPI[:, :], XCt[:, :], IDb[:, :]))
                sc.wait("T", "V")
                def _upd(t):
                    t.matmul(PS_WB[:, 0:128], PTt[:, :], XPI[:, :], start=True, stop=True)
                    return t.matmul(PS_WB[:, 128:256], XPI[:, :], PTt[:, :], start=True, stop=True)
                tk = sc.inc("T", "T", _upd)
                sc.wait("S", "T", tk)
                def _pce(sl):
                    sl.copy(PCt[:, :], PS_WB[:, 0:128])
                    return sl.copy(PTt[:, :], PS_WB[:, 128:256])
                sc.inc("S", "S", _pce)
            nS_chain = sc.n("S")
            if p == 0 and c == 0 and "pc0" in dbg:
                sc.wait("G", "S", nS_chain)
                dmaG(dbg["pc0"], PCt[:, :])
                dmaG(dbg["ptc0"], PTt[:, :])
                sc.wait("S", "dG")
                sc.wait("T", "dG")

            sc.wait("T", "S", nS_chain)
            def _wmm(t):
                t.matmul(PS_WA[:, 0:128], KBTM[:, 0:128], PCt[:, :], start=True, stop=True)
                return t.matmul(PS_WA[:, 384:512], KBTM[:, 128:256], PCt[:, :],
                                start=True, stop=True)
            tk = sc.inc("T", "T", _wmm)
            sc.wait("S", "T", tk)
            def _wng(sl):
                sl.mul(WNEG[:, 0:128], PS_WA[:, 0:128], -1.0)
                return sl.mul(WNEG[:, 128:256], PS_WA[:, 384:512], -1.0)
            sc.inc("S", "S", _wng)
            nS_w = sc.n("S")
            sc.wait("T", "S", nS_w)
            def _ups(t):
                t.matmul(PS_WC[:, 0:256], PCt[:, :], VBTM[:, :], start=True, stop=False)
                t.matmul(PS_WC[:, 0:256], WNEG[:, 0:128], SB[:, 0:256], start=False, stop=False)
                return t.matmul(PS_WC[:, 0:256], WNEG[:, 128:256], SB[:, 256:512],
                                start=False, stop=True)
            tk = sc.inc("T", "T", _ups)
            sc.wait("S", "T", tk)
            sc.inc("S", "S", lambda sl: sl.copy(UP[:, :], PS_WC[:, 0:256]))
            nS_up = sc.n("S")
            if p == 0 and c == 0 and "up0" in dbg:
                sc.wait("G", "S", nS_up)
                dmaG(dbg["up0"], UP[:, :])
                dmaG(dbg["wneg0"], WNEG[:, :])
                sc.wait("S", "dG")
                sc.wait("T", "dG")

            def _grq(t, t0=t0):
                t.matmul(PS_WA[:, 128:256], KF[:, t0:t0 + 128], QF[:, t0:t0 + 128],
                         start=True, stop=False)
                return t.matmul(PS_WA[:, 128:256], KF[:, PT + t0:PT + t0 + 128],
                                QF[:, PT + t0:PT + t0 + 128], start=False, stop=True)
            tk = sc.inc("T", "T", _grq)
            sc.wait("V", "T", tk)
            sc.wait("V", "S", nS_up)
            sc.inc("V", "V", lambda v: v.tensor_mul(SCR[:, 128:256], PS_WA[:, 128:256], MKU[:, :]))
            sc.wait("S", "V")
            sc.inc("S", "S", lambda sl, c=c: sl.activation(
                ATT[:, :], SCR[:, 128:256], AF.Copy, scale=COLS[:, CSK + c:CSK + c + 1]))
            nS_att = sc.n("S")
            if p == 0 and c == 0 and "att0" in dbg:
                sc.wait("G", "S", nS_att)
                dmaG(dbg["att0"], ATT[:, :])
                sc.wait("S", "dG")
                sc.wait("T", "dG")
            sc.wait("T", "S", nS_att)
            def _o(t, t0=t0):
                t.matmul(PS_WC[:, 256:512], QF[:, t0:t0 + 128], SB[:, 0:256],
                         start=True, stop=False)
                t.matmul(PS_WC[:, 256:512], QF[:, PT + t0:PT + t0 + 128], SB[:, 256:512],
                         start=False, stop=False)
                return t.matmul(PS_WC[:, 256:512], ATT[:, :], UP[:, :], start=False, stop=True)
            tk = sc.inc("T", "T", _o)
            sc.wait("S", "T", tk)
            sc.inc("S", "S", lambda sl, c=c: sl.activation(
                DOTM[:, c * 256:(c + 1) * 256], PS_WC[:, 256:512], AF.Copy,
                scale=COLS[:, CSQ + c:CSQ + c + 1]))
            nS_do = sc.n("S")

            def _ds(t):
                t.matmul(PS_S[:, 0:256], KNTM[:, 0:128], UP[:, :], start=True, stop=True)
                return t.matmul(PS_S[:, 256:512], KNTM[:, 128:256], UP[:, :],
                                start=True, stop=True)
            tk = sc.inc("T", "T", _ds)
            sc.wait("V", "T", tk)
            def _sacc(v):
                v.tensor_add(SFP[:, 0:256], SFP[:, 0:256], PS_S[:, 0:256])
                return v.tensor_add(SFP[:, 256:512], SFP[:, 256:512], PS_S[:, 256:512])
            sc.inc("V", "V", _sacc)
            sc.wait("S", "V")
            sc.inc("S", "S", lambda sl: sl.copy(SB[:, :], SFP[:, :]))
            sc.wait("T", "V")

            sc.wait("T", "S")
            def _trd(t, c=c):
                t.transpose(PS_TB[:, 128:256], DOTM[:, c * 256:c * 256 + 128], IDb[:, :])
                return t.transpose(PS_TB[:, 256:384], DOTM[:, c * 256 + 128:c * 256 + 256],
                                   IDb[:, :])
            tk = sc.inc("T", "T", _trd)
            sc.wait("S", "T", tk)
            def _dof(sl, t0=t0):
                sl.copy(DOF[:, t0:t0 + 128], PS_TB[:, 128:256])
                return sl.copy(DOF[:, PT + t0:PT + t0 + 128], PS_TB[:, 256:384])
            sc.inc("S", "S", _dof)

            for (BR, bi) in ((FSTM, 0), (FLTM, 1), (DOTM, 2), (VTM, 3)):
                sl_ = BR[:, c * 256:(c + 1) * 256]
                def _redm(v, sl_=sl_, bi=bi):
                    with nc.allow_low_precision("stat cols feed bf16 matmul operands"):
                        v.reduce_sum(SPK[:, 3 * bi:3 * bi + 1], sl_, axis=AX.X)
                        return v.reduce_max(SPK[:, 3 * bi + 2:3 * bi + 3], sl_, axis=AX.X)
                sc.inc("V", "V", _redm)
                sc.inc("S", "S", lambda sl, sl_=sl_, bi=bi: sl.activation(
                    SCR[:, 0:256], sl_, AF.Square, accum_out=SCF[:, 8 + bi:9 + bi]))
            sc.inc("S", "S", lambda sl: sl.copy(SCF[:, 15:16], CSTT[:, 0:1]))
            sc.wait("V", "S")
            def _rstat(v):
                for bi in range(3):
                    v.tensor_scalar(SCF[:, 12 + bi:13 + bi], SCF[:, 8 + bi:9 + bi],
                                    1.0 / 256.0, 1e-8, ALU.mult, ALU.max)
                return v.tensor_scalar(SCF[:, 15:16], SCF[:, 11:12],
                                       1.0 / 256.0, 1e-8, ALU.mult, ALU.max)
            sc.inc("V", "V", _rstat)
            sc.wait("S", "V")
            def _rsq(sl):
                for bi in range(3):
                    sl.activation(SPK[:, 3 * bi + 1:3 * bi + 2], SCF[:, 12 + bi:13 + bi], AF.Sqrt)
                return sl.activation(SPK[:, 10:11], SCF[:, 15:16], AF.Sqrt)
            sc.inc("S", "S", _rsq)
            nS_spk = sc.n("S")
            sc.wait("T", "S", nS_spk)
            sc.wait("T", "V")
            tk = sc.inc("T", "T", lambda t: t.transpose(PS_TB[0:16, 384:512], SPK[:, 0:16],
                                                        IDb[:, :]))
            sc.wait("S", "T", tk)
            sc.inc("S", "S", lambda sl, c=c: sl.copy(
                STATS[0:16, c * 128:(c + 1) * 128], PS_TB[0:16, 384:512]))
            sc.wait("T", "S")

        def _halos(v):
            for P in (PQ, PK, PV):
                v.tensor_copy(P[:, 0:3], P[:, 512:515])
                v.tensor_copy(P[:, 515:518], P[:, 1027:1030])
            v.tensor_copy(VF[:, 0:30], VF[:, 512:542])
            return v.tensor_copy(VF[:, 542:572], VF[:, 1054:1084])
        sc.inc("V", "V", _halos)

        nS_br = sc.n("S")
        sc.wait("T", "S", nS_br)
        for m in range(4):
            bank = PSB[m % 2]
            sc.wait("T", "S")
            def _gate(t, m=m, bank=bank):
                for k in range(8):
                    t.matmul(bank[:, 0:512], GW1H[:, k * GH + m * 128:k * GH + m * 128 + 128],
                             XLN[:, k * PT:(k + 1) * PT], start=(k == 0), stop=False)
                for bi in range(4):
                    for c2 in range(2):
                        if bi == 0:
                            rhs = FSF[:, c2 * PT:(c2 + 1) * PT]
                        elif bi == 1:
                            rhs = FLF[:, c2 * PT:(c2 + 1) * PT]
                        elif bi == 2:
                            rhs = DOF[:, c2 * PT:(c2 + 1) * PT]
                        else:
                            rhs = VF[:, c2 * 542 + 30:c2 * 542 + 30 + 512]
                        t.matmul(bank[:, 0:512],
                                 GW1B[:, (bi * 2 + c2) * GH + m * 128:(bi * 2 + c2) * GH + m * 128 + 128],
                                 rhs, start=False, stop=False)
                return t.matmul(bank[:, 0:512], STL[0:16, m * 128:(m + 1) * 128],
                                STATS[0:16, :], start=False, stop=True)
            tk = sc.inc("T", "T", _gate)
            sc.wait("S", "T", tk)
            sc.inc("S", "S", lambda sl, m=m, bank=bank: sl.activation(
                GSBt[:, m * PT:(m + 1) * PT], bank[:, 0:512], AF.Gelu, bias=GB1[:, m:m + 1]))
        nS_gate = sc.n("S")

        sc.wait("T", "S", nS_gate)
        for c in range(NCH):
            t0 = c * 128
            def _lg(t, t0=t0):
                for mg in range(3):
                    t.matmul(PS_WB[:, 256:260], GSBt[:, mg * PT + t0:mg * PT + t0 + 128],
                             GW2[:, mg * 4:mg * 4 + 4], start=(mg == 0), stop=False)
                return t.matmul(PS_WB[:, 256:260], GSBt[:, 3 * PT + t0:3 * PT + t0 + 128],
                                GW2[:, 12:16], start=False, stop=True)
            tk = sc.inc("T", "T", _lg)
            sc.wait("V", "T", tk)
            def _sm1(v):
                v.tensor_add(ETM[:, :], PS_WB[:, 256:260], GB2T[:, :])
                v.tensor_copy(SCF[:, 9:10], ZRO[:, 0:1])
                v.tensor_scalar(ETM[:, :], ETM[:, :], TIV[:, 0:1], None, ALU.mult)
                v.tensor_copy(SCF[:, 9:10], ZRO[:, 0:1])
                v.reduce_max(SCF[:, 5:6], ETM[:, :], axis=AX.X)
                v.tensor_copy(SCF[:, 10:11], ZRO[:, 0:1])
                return v.tensor_scalar(SCF[:, 5:6], SCF[:, 5:6], -1.0, None, ALU.mult)
            sc.inc("V", "V", _sm1)
            sc.wait("T", "V")
            sc.wait("S", "V")
            sc.inc("S", "S", lambda sl: sl.activation(ETM[:, :], ETM[:, :], AF.Exp,
                                                      bias=SCF[:, 5:6]))
            sc.wait("V", "S")
            def _sm2(v, c=c):
                v.reduce_sum(SCF[:, 6:7], ETM[:, :], axis=AX.X)
                v.tensor_copy(SCF[:, 9:10], ZRO[:, 0:1])
                v.reciprocal(SCF[:, 6:7], SCF[:, 6:7])
                v.tensor_copy(SCF[:, 9:10], FLT[:, 0:1])
                v.tensor_scalar(WGt[:, :], ETM[:, :], SCF[:, 6:7], None, ALU.mult)
                v.tensor_copy(SCF[:, 9:10], ZRO[:, 0:1])
                v.tensor_max(WGt[:, :], WGt[:, :], FLT[:, :])
                v.tensor_copy(SCF[:, 9:10], ZRO[:, 0:1])
                v.reduce_sum(SCF[:, 6:7], WGt[:, :], axis=AX.X)
                v.tensor_copy(SCF[:, 9:10], ZRO[:, 0:1])
                v.reciprocal(SCF[:, 6:7], SCF[:, 6:7])
                v.tensor_copy(SCF[:, 9:10], FLT[:, 0:1])
                v.tensor_scalar(WGt[:, :], WGt[:, :], SCF[:, 6:7], None, ALU.mult)
                v.tensor_copy(SCF[:, 9:10], ZRO[:, 0:1])
                v.tensor_scalar(OT1[:, :], FSTM[:, c * 256:(c + 1) * 256], WGt[:, 0:1],
                                None, ALU.mult)
                v.tensor_copy(SCF[:, 9:10], ZRO[:, 0:1])
                v.scalar_tensor_tensor(OT2[:, :], FLTM[:, c * 256:(c + 1) * 256], WGt[:, 1:2],
                                       OT1[:, :], ALU.mult, ALU.add)
                v.tensor_copy(SCF[:, 9:10], ZRO[:, 0:1])
                v.scalar_tensor_tensor(OT1[:, :], DOTM[:, c * 256:(c + 1) * 256], WGt[:, 2:3],
                                       OT2[:, :], ALU.mult, ALU.add)
                v.tensor_copy(SCF[:, 9:10], ZRO[:, 0:1])
                return v.scalar_tensor_tensor(OT2[:, :], VTM[:, c * 256:(c + 1) * 256],
                                              WGt[:, 3:4], OT1[:, :], ALU.mult, ALU.add)
            sc.inc("V", "V", _sm2)
            sc.wait("S", "V")
            def _rmsacc(sl):
                sl.activation(SCR[:, 0:256], OT2[:, :], AF.Square, accum_out=SCF[:, 7:8])
                return sl.copy(SCF[:, 15:16], CSTT[:, 0:1])
            sc.inc("S", "S", _rmsacc)
            sc.wait("V", "S")
            sc.inc("V", "V", lambda v: v.tensor_scalar(SCF[:, 7:8], SCF[:, 7:8],
                                                       1.0 / 256.0, 1e-5, ALU.mult, ALU.add))
            sc.wait("S", "V")
            sc.inc("S", "S", lambda sl: sl.sqrt(SCF[:, 7:8], SCF[:, 7:8]))
            sc.wait("V", "S")
            sc.inc("V", "V", lambda v: v.reciprocal(SCF[:, 7:8], SCF[:, 7:8]))
            sc.wait("S", "V")
            sc.inc("S", "S", lambda sl: sl.activation(OT1[:, :], OT2[:, :], AF.Copy,
                                                      scale=SCF[:, 7:8]))
            sc.wait("V", "S")
            sc.inc("V", "V", lambda v: v.tensor_mul(OTB[:, :], OT1[:, :], ONT[:, :]))
            nV_o = sc.n("V")
            sc.wait("T", "V", nV_o)
            sc.wait("T", "S")
            def _tro(t):
                t.transpose(PS_TB[:, 512:640], OTB[:, 0:128], IDb[:, :])
                return t.transpose(PS_TB[:, 640:768], OTB[:, 128:256], IDb[:, :])
            tk = sc.inc("T", "T", _tro)
            sc.wait("S", "T", tk)
            def _opf(sl, t0=t0):
                sl.copy(OPF[:, t0:t0 + 128], PS_TB[:, 512:640])
                return sl.copy(OPF[:, PT + t0:PT + t0 + 128], PS_TB[:, 640:768])
            sc.inc("S", "S", _opf)
            sc.wait("T", "S")

        nS_opf = sc.n("S")
        sc.wait("T", "S", nS_opf)
        for tt in range(4):
            t0 = tt * 128
            for n_ in range(2):
                bank = PSB[n_]
                sc.wait("T", "S")
                def _wo(t, t0=t0, n_=n_, bank=bank):
                    t.matmul(bank[:, 0:512], OPF[:, t0:t0 + 128],
                             WOs[:, n_ * 512:n_ * 512 + 512], start=True, stop=False)
                    return t.matmul(bank[:, 0:512], OPF[:, PT + t0:PT + t0 + 128],
                                    WOs[:, 1024 + n_ * 512:1024 + n_ * 512 + 512],
                                    start=False, stop=True)
                tk = sc.inc("T", "T", _wo)
                sc.wait("S", "T", tk)
                sc.inc("S", "S", lambda sl, tt=tt, n_=n_, bank=bank: sl.copy(
                    OPART[:, tt * D + n_ * 512:tt * D + n_ * 512 + 512], bank[:, 0:512]))
        nS_op = sc.n("S")
        sc.wait("Y", "S", nS_op)
        def _ccw(y, p=p):
            return y.dma_start(
                bass.AP(cc_in, p * PT * D, [[D, 128], [128 * D, 4], [1, D]]),
                bass.AP(OPART, 0, [[4 * D, 128], [D, 4], [1, D]]))
        sc.inc("Y", "dY", _ccw, 16)

    if dbg:
        sc.wait("G", "S")
        sc.wait("G", "V")
        for name, ap_src in (("qf", QF), ("kf", KF), ("vf", VF), ("xln", XLN),
                             ("fsf", FSF), ("flf", FLF), ("dof", DOF), ("gsb", GSBt),
                             ("opf", OPF), ("opart", OPART), ("kbf", KBF),
                             ("vtm", VTM), ("dotm", DOTM), ("spk", SPK), ("wgt", WGt)):
            if name in dbg:
                dmaG(dbg[name], ap_src[:, :])
        if "beta" in dbg:
            dmaG(dbg["beta"], RRF[0:1, 0:512])
        if "xt" in dbg:
            dmaG(dbg["xt"], XT[(npass - 1) % 2][:, :])
        if "rows" in dbg:
            dmaG(dbg["rows"], RRF[:, :])
        if "stats" in dbg:
            dmaG(dbg["stats"], STATS[0:16, :])
        if "sb" in dbg:
            dmaG(dbg["sb"], SB[:, :])
        sc.wait("Y", "dG")

    n_cc = sc.n("dY")
    sc.wait("G", "dY", n_cc)
    sc.wait("G", "dG")
    def _rs(g):
        if split:
            return g.collective_compute(
                "ReduceScatter", ALU.add,
                replica_groups=[[0, 1, 2, 3], [4, 5, 6, 7]],
                ins=[cc_in.ap()[L // 2:L, :].opt()],
                outs=[cc_rs.ap()[TOK // 2:TOK, :].opt()])
        return g.collective_compute(
            "ReduceScatter", ALU.add,
            replica_groups=[[0, 1, 2, 3], [4, 5, 6, 7]],
            ins=[cc_in.ap().opt()], outs=[cc_rs.ap().opt()])
    sc.inc("G", "CC", _rs)
    p1.close()

    if not do_ffn:
        sc.wait("Y", "CC")
        AOT = E(nc.sbuf_tensor("aot", [128, D], F32))
        AOF = E(nc.sbuf_tensor("aof", [128, D], F32))
        for tt in range(TOK // 128):
            dmaY(AOT[:, :], cc_rs.ap()[tt * 128:(tt + 1) * 128, :])
            sc.wait("V", "dY")
            sc.inc("V", "V", lambda v: v.tensor_copy(AOF[:, :], AOT[:, :]))
            sc.wait("Y", "V")
            dmaY(out[tt * 128:(tt + 1) * 128, :], AOF[:, :])
        sc.wait("Y", "dY")
        nc_final(nc, sc)
        st.close()
        return nc, dbg

    p2 = ExitStack()
    E2 = p2.enter_context
    XRES = E2(nc.sbuf_tensor("xres", [128, D], F32))
    AOT = E2(nc.sbuf_tensor("aot", [128, D], F32))
    X2 = E2(nc.sbuf_tensor("x2b", [128, D], F32))
    X2LNT = E2(nc.sbuf_tensor("x2lnt", [128, D], BF16))
    X2LN = E2(nc.sbuf_tensor("x2ln", [128, 8 * 1024], BF16))
    SCF2 = E2(nc.sbuf_tensor("scf2", [128, 16], F32))
    S1C = E2(nc.sbuf_tensor("s1cb", [128, 32], F32))
    S2C = E2(nc.sbuf_tensor("s2cb", [128, 32], F32))
    SH1C = E2(nc.sbuf_tensor("sh1cb", [128, 32], F32))
    SH2C = E2(nc.sbuf_tensor("sh2cb", [128, 32], F32))
    S3C = E2(nc.sbuf_tensor("s3cb", [128, 8], F32))
    HT = E2(nc.sbuf_tensor("ht", [128, 32 * 1024], BF16))
    G1 = E2(nc.sbuf_tensor("g1", [128, 32 * 1024], BF16))
    RING = [E2(nc.sbuf_tensor(f"ring{i}", [128, 4096], BF16)) for i in range(3)]
    OASM = E2(nc.sbuf_tensor("oasm", [128, 8 * 1024], BF16))
    TM1 = E2(nc.sbuf_tensor("tm1", [128, 512], BF16))
    TM2 = E2(nc.sbuf_tensor("tm2", [128, 512], BF16))
    OUTT = E2(nc.sbuf_tensor("outt", [128, 1024], F32))

    dmaY(S1C[:, :], s1c)
    dmaY(S2C[:, :], s2c)
    dmaY(SH1C[:, :], sh1c)
    dmaY(SH2C[:, :], sh2c)
    dmaY(S3C[:, :], s3col)
    n_scl = sc.n("dY")

    sc.wait("G", "CC")
    ring_free = [0, 0, 0]

    for hh in range(2):
        tok0 = hh * 1024
        sc.wait("S", "dY", n_scl)
        sc.wait("V", "CC")
        sc.wait("Y", "CC")
        for tt in range(8):
            r0 = tok0 + tt * 128
            nlx = dmaY(XRES[:, :], x_res[r0:r0 + 128, :])
            nla = dmaY(AOT[:, :], cc_rs.ap()[r0:r0 + 128, :])
            sc.wait("V", "dY")
            sc.wait("V", "S")
            sc.inc("V", "V", lambda v: v.tensor_add(X2[:, :], XRES[:, :], AOT[:, :]))
            nV_x2 = sc.n("V")
            sc.wait("Y", "V", nV_x2)
            dmaY(x2stash[r0:r0 + 128, :], X2[:, :])
            sc.wait("S", "V", nV_x2)
            def _ln2acc(sl):
                sl.activation(OASM[:, 0:1024], X2[:, :], AF.Copy, accum_out=SCF2[:, 0:1])
                sl.activation(OASM[:, 0:1024], X2[:, :], AF.Square, accum_out=SCF2[:, 1:2])
                return sl.copy(SCF2[:, 15:16], CSTT[:, 0:1])
            sc.inc("S", "S", _ln2acc)
            sc.wait("V", "S")
            def _ln2c(v):
                v.tensor_scalar(SCF2[:, 2:3], SCF2[:, 0:1], 1.0 / D, None, ALU.mult)
                v.tensor_scalar(SCF2[:, 3:4], SCF2[:, 1:2], 1.0 / D, None, ALU.mult)
                v.tensor_copy(SCF2[:, 9:10], ZRO[:, 0:1])
                v.tensor_mul(SCF2[:, 4:5], SCF2[:, 2:3], SCF2[:, 2:3])
                v.tensor_copy(SCF2[:, 9:10], ZRO[:, 0:1])
                v.tensor_sub(SCF2[:, 3:4], SCF2[:, 3:4], SCF2[:, 4:5])
                v.tensor_copy(SCF2[:, 9:10], ZRO[:, 0:1])
                return v.tensor_scalar(SCF2[:, 3:4], SCF2[:, 3:4], 1e-5, None, ALU.add)
            sc.inc("V", "V", _ln2c)
            sc.wait("S", "V")
            sc.inc("S", "S", lambda sl: sl.sqrt(SCF2[:, 3:4], SCF2[:, 3:4]))
            sc.wait("V", "S")
            def _ln2i(v):
                v.reciprocal(SCF2[:, 3:4], SCF2[:, 3:4])
                v.tensor_copy(SCF2[:, 5:6], SCF2[:, 2:3])
                v.tensor_mul(SCF2[:, 4:5], SCF2[:, 2:3], SCF2[:, 3:4])
                v.tensor_copy(SCF2[:, 9:10], ZRO[:, 0:1])
                return v.tensor_scalar(SCF2[:, 4:5], SCF2[:, 4:5], -1.0, None, ALU.mult)
            sc.inc("V", "V", _ln2i)
            sc.wait("S", "V")
            sc.inc("S", "S", lambda sl: sl.activation(X2LNT[:, :], X2[:, :], AF.Identity,
                                                      bias=SCF2[:, 4:5], scale=SCF2[:, 3:4]))
            nS_l2 = sc.n("S")
            sc.wait("T", "S", nS_l2)
            def _trx2(t):
                for k in range(7):
                    t.transpose(PS_TA[:, k * 128:(k + 1) * 128],
                                X2LNT[:, k * 128:(k + 1) * 128], IDb[:, :])
                return t.transpose(PS_TA[:, 896:1024], X2LNT[:, 896:1024], IDb[:, :])
            tk = sc.inc("T", "T", _trx2)
            sc.wait("S", "T", tk)
            sc.inc("S", "S", lambda sl, tt=tt: sl.copy(
                bass.AP(X2LN, tt * 128, [[8 * 1024, 128], [1024, 8], [1, 128]]),
                bass.AP(PS_TA, 0, [[1024, 128], [128, 8], [1, 128]])))
            sc.wait("T", "S")

        nS_x2ln = sc.n("S")
        sc.wait("T", "S", nS_x2ln)
        if do_ffn == "x":
            continue

        for mg in range(8):
            sa, sb = (2 * mg) % 3, (2 * mg + 1) % 3
            sc.wait("G", "T", ring_free[sa])
            n1 = dmaG(RING[sa][:, :], w1t[mg])
            sc.wait("G", "T", ring_free[sb])
            n2 = dmaG(RING[sb][:, :], w2t[mg])
            sc.wait("T", "dG", n2)
            for m in range(4):
                gm = mg * 4 + m
                for t2 in range(2):
                    b1 = PSB[0] if t2 == 0 else PSB[1]
                    b2 = PS_WA if t2 == 0 else PS_WB
                    sc.wait("T", "S")
                    def _a(t, sa=sa, sb=sb, m=m, t2=t2, b1=b1, b2=b2):
                        for k in range(8):
                            t.matmul(b1[:, 0:512],
                                     RING[sa][:, k * 512 + m * 128:k * 512 + m * 128 + 128],
                                     X2LN[:, k * 1024 + t2 * 512:k * 1024 + t2 * 512 + 512],
                                     start=(k == 0), stop=(k == 7))
                        for k in range(7):
                            t.matmul(b2[:, 0:512],
                                     RING[sb][:, k * 512 + m * 128:k * 512 + m * 128 + 128],
                                     X2LN[:, k * 1024 + t2 * 512:k * 1024 + t2 * 512 + 512],
                                     start=(k == 0), stop=False)
                        return t.matmul(b2[:, 0:512],
                                        RING[sb][:, 7 * 512 + m * 128:7 * 512 + m * 128 + 128],
                                        X2LN[:, 7 * 1024 + t2 * 512:7 * 1024 + t2 * 512 + 512],
                                        start=False, stop=True)
                    tk = sc.inc("T", "T", _a)
                    sc.wait("S", "T", tk)
                    def _ha(sl, gm=gm, b1=b1, b2=b2):
                        sl.activation(TM1[:, :], b1[:, 0:512], AF.Silu, scale=S1C[:, gm:gm + 1])
                        return sl.activation(TM2[:, :], b2[:, 0:512], AF.Copy,
                                             scale=S2C[:, gm:gm + 1])
                    sc.inc("S", "S", _ha)
                    sc.wait("V", "S")
                    sc.inc("V", "V", lambda v, gm=gm, t2=t2: v.tensor_mul(
                        HT[:, gm * 1024 + t2 * 512:gm * 1024 + t2 * 512 + 512],
                        TM1[:, :], TM2[:, :]))
                    sc.wait("S", "V")
            ring_free[sa] = sc.n("T")
            ring_free[sb] = sc.n("T")

        nV_h = sc.n("V")
        sc.wait("T", "V", nV_h)
        if do_ffn == "a":
            continue

        def big_stage(wsrc, SRC, DST, SCL, act):
            banks = [PSB[0], PSB[1], PS_WA, PS_WB]
            for mg in range(16):
                sa, sb = (2 * mg) % 3, (2 * mg + 1) % 3
                sc.wait("G", "T", ring_free[sa])
                n1 = dmaG(RING[sa][:, :], wsrc[2 * mg])
                sc.wait("G", "T", ring_free[sb])
                n2 = dmaG(RING[sb][:, :], wsrc[2 * mg + 1])
                sc.wait("T", "dG", n2)
                sc.wait("T", "S")
                import os
                if os.environ.get("BSTAGE") == "dma":
                    continue
                def _grp(t, sa=sa, sb=sb):
                    last = None
                    for gi, (m, t2) in enumerate(((0, 0), (0, 1), (1, 0), (1, 1))):
                        for k in range(32):
                            slot, kk = (sa, k) if k < 16 else (sb, k - 16)
                            last = t.matmul(banks[gi][:, 0:512],
                                     RING[slot][:, kk * 256 + m * 128:kk * 256 + m * 128 + 128],
                                     SRC[:, k * 1024 + t2 * 512:k * 1024 + t2 * 512 + 512],
                                     start=(k == 0), stop=(k == 31))
                    return last
                tk2 = sc.inc("T", "T", _grp)
                ring_free[sa] = sc.n("T")
                ring_free[sb] = sc.n("T")
                if os.environ.get("BSTAGE") == "mm":
                    continue
                sc.wait("S", "T", tk2)
                def _ev(sl, mg=mg):
                    mode = os.environ.get("BSTAGE", "")
                    for gi, (m, t2) in enumerate(((0, 0), (0, 1), (1, 0), (1, 1))):
                        gm = mg * 2 + m
                        dst = DST[:, gm * 1024 + t2 * 512:gm * 1024 + t2 * 512 + 512]
                        if mode == "evcopy":
                            sl.copy(dst, banks[gi][:, 0:512])
                        elif mode == "evtm":
                            sl.copy(TM1[:, :], banks[gi][:, 0:512])
                        elif mode == "evsb":
                            sl.copy(dst, ZRO[:, 0:512])
                        elif mode == "evns":
                            sl.activation(dst, banks[gi][:, 0:512], act)
                        else:
                            sl.activation(dst, banks[gi][:, 0:512], act, scale=SCL[:, gm:gm + 1])
                    return sl.nop()
                sc.inc("S", "S", _ev)

        big_stage(h1t, HT, G1, SH1C, AF.Gelu)

        nS_g1 = sc.n("S")
        sc.wait("T", "S", nS_g1)
        if do_ffn == "b":
            continue
        big_stage(h2t, G1, HT, SH2C, AF.Gelu)

        nS_g2 = sc.n("S")
        sc.wait("T", "S", nS_g2)
        for m in range(8):
            slot = m % 3
            sc.wait("G", "T", ring_free[slot])
            nw = dmaG(RING[slot][:, :], w3t[m])
            sc.wait("T", "dG", nw)
            sc.wait("T", "S")
            def _d(t, slot=slot):
                last = None
                for n_ in range(2):
                    for k in range(32):
                        last = t.matmul(PSB[n_][:, 0:512],
                                 RING[slot][:, k * 128:(k + 1) * 128],
                                 HT[:, k * 1024 + n_ * 512:k * 1024 + n_ * 512 + 512],
                                 start=(k == 0), stop=(k == 31))
                return last
            tk = sc.inc("T", "T", _d)
            ring_free[slot] = sc.n("T")
            sc.wait("S", "T", tk)
            def _dev(sl, m=m):
                sl.activation(OASM[:, m * 1024:m * 1024 + 512], PSB[0][:, 0:512], AF.Copy,
                              scale=S3C[:, m:m + 1])
                return sl.activation(OASM[:, m * 1024 + 512:(m + 1) * 1024], PSB[1][:, 0:512],
                                     AF.Copy, scale=S3C[:, m:m + 1])
            sc.inc("S", "S", _dev)
        nS_ffo = sc.n("S")
        sc.wait("T", "S", nS_ffo)

        for tt in range(8):
            r0 = tok0 + tt * 128
            nrx = dmaY(XRES[:, :], x2stash[r0:r0 + 128, :])
            sc.wait("T", "V")
            def _trf(t, tt=tt):
                for m in range(7):
                    t.transpose(PS_TA[:, m * 128:(m + 1) * 128],
                                OASM[:, m * 1024 + tt * 128:m * 1024 + tt * 128 + 128], IDb[:, :])
                return t.transpose(PS_TA[:, 896:1024],
                                   OASM[:, 7 * 1024 + tt * 128:7 * 1024 + tt * 128 + 128],
                                   IDb[:, :])
            tk = sc.inc("T", "T", _trf)
            sc.wait("V", "T", tk)
            sc.wait("V", "dY")
            def _fadd(v):
                v.tensor_add(OUTT[:, 0:512], XRES[:, 0:512], PS_TA[:, 0:512])
                return v.tensor_add(OUTT[:, 512:1024], XRES[:, 512:1024], PS_TA[:, 512:1024])
            sc.inc("V", "V", _fadd)
            nV_out = sc.n("V")
            sc.wait("Y", "V", nV_out)
            dmaY(out[r0:r0 + 128, :], OUTT[:, :])
            sc.wait("T", "V", nV_out)

    if do_ffn is not True:
        sc.wait("V", "CC")
        sc.inc("V", "V", lambda v: v.memset(OUTT[:, :], 0.0))
        sc.wait("Y", "V")
        for tt in range(TOK // 128):
            dmaY(out[tt * 128:(tt + 1) * 128, :], OUTT[:, :])
    sc.wait("Y", "dY")
    nc_final(nc, sc)
    p2.close()
    st.close()
    return nc, dbg


def nc_final(nc, sc):
    with nc.Block() as block:
        def mk(key):
            def body(eng):
                for k, fn in sc.items:
                    if k == key:
                        fn(eng)
            return body
        block.tensor(mk("T"))
        block.vector(mk("V"))
        block.scalar(mk("S"))
        block.gpsimd(mk("G"))
        block.sync(mk("Y"))


_CACHE = {}


def kernel(**inputs):
    if "prog" not in _CACHE:
        _CACHE["prog"] = build_program()
    nc, _ = _CACHE["prog"]
    in_maps = [prepare_core_inputs(inputs, c) for c in range(8)]
    res = run_bass_kernel_spmd(nc, in_maps, core_ids=list(range(8)))
    outp = np.zeros((B, L, D), np.float32)
    for c in range(8):
        b, win = c // 4, c % 4
        o = res.results[c]["out"]
        outp[b, win * 1024:win * 1024 + 1024] = o[0:1024]
        outp[b, 4096 + win * 1024:4096 + win * 1024 + 1024] = o[1024:2048]
    return outp


# version 06 (final)
# speedup vs baseline: 1.0289x; 1.0057x over previous
import numpy as np
import ml_dtypes
from contextlib import ExitStack

import concourse.bass as bass
import concourse.mybir as mybir
from concourse.bass_utils import run_bass_kernel_spmd

F32 = mybir.dt.float32
BF16 = mybir.dt.bfloat16
AF = mybir.ActivationFunctionType
ALU = mybir.AluOpType
AX = mybir.AxisListType

B, L, D = 2, 8192, 1024
CH, PT = 128, 512
NCH = PT // CH
TOK = 2048
GH = 512


class Sched:
    KEYS = ("T", "V", "S", "G", "Y")

    def __init__(self):
        self.items = []
        self.counts = {}
        self.h = {}

    def inc(self, key, sem, fn, n=1):
        self.counts[sem] = self.counts.get(sem, 0) + n

        def g(eng, fn=fn, sem=sem, n=n):
            fn(eng).then_inc(self.h[sem], n)
        self.items.append((key, g))
        return self.counts[sem]

    def run(self, key, fn):
        self.items.append((key, fn))

    def wait(self, key, sem, val=None):
        if val is None:
            val = self.counts.get(sem, 0)
        assert 0 <= val <= self.counts.get(sem, 0), (sem, val, self.counts.get(sem, 0))

        def g(eng, sem=sem, val=val):
            eng.wait_ge(self.h[sem], val)
        self.items.append((key, g))

    def n(self, sem):
        return self.counts.get(sem, 0)


def _lhsT(w):
    K, M = w.shape
    return np.ascontiguousarray(w.reshape(K // 128, 128, M).transpose(1, 0, 2).reshape(128, -1))


def prepare_core_inputs(A, core):
    b, h = core // 4, core % 4
    win = core % 4
    hs = slice(h * 256, (h + 1) * 256)
    d = {}
    d["x_attn"] = A["L0_attn_x"][b]
    xb = A["L0_attn_x"][b]
    d["x_res"] = np.concatenate([xb[win * 1024:win * 1024 + 1024],
                                 xb[4096 + win * 1024:4096 + win * 1024 + 1024]])
    d["wq"] = _lhsT(A["L0_attn_Wq"][:, hs])
    d["wk"] = _lhsT(A["L0_attn_Wk"][:, hs])
    d["wv"] = _lhsT(A["L0_attn_Wv"][:, hs])
    d["wb"] = _lhsT(A["L0_attn_Wb"][:, h:h + 1])
    cw = [A["L0_attn_" + nm][hs].reshape(2, 128, 4) for nm in ("qc_w", "kc_w", "vc_w")]
    d["convw"] = np.stack(cw, 0).reshape(6, 128, 4).transpose(1, 0, 2).reshape(128, 24)
    fs = A["L0_attn_fir_s"][h].reshape(2, 128, 3).transpose(1, 0, 2).reshape(128, 6)
    fl = A["L0_attn_fir_l"][h].reshape(2, 128, 31).transpose(1, 0, 2).reshape(128, 62)
    d["firw"] = np.concatenate([fs, fl], 1)
    g = A["L0_attn_gW1"]
    d["gw1h"] = _lhsT(g[0:1024])
    d["gw1b"] = g[4096:5120].reshape(8, 128, GH).transpose(1, 0, 2).reshape(128, 8 * GH)
    d["gw1s"] = g[1024:4096].reshape(24, 128, GH).transpose(1, 0, 2).reshape(128, 24 * GH)
    d["gb1"] = A["L0_attn_gb1"].reshape(4, 128).T
    d["gw2"] = A["L0_attn_gW2"].reshape(4, 128, 4).transpose(1, 0, 2).reshape(128, 16)
    d["gb2"] = A["L0_attn_gb2"].reshape(1, 4)
    d["temp"] = A["L0_attn_temp"][h].reshape(1, 1)
    d["floor"] = A["L0_attn_eps_floor"][h].reshape(1, 4)
    d["onorm"] = A["L0_attn_onorm_w"].reshape(1, 256)
    d["wo"] = A["L0_attn_Wo"][hs].reshape(2, 128, 1024).transpose(1, 0, 2).reshape(128, 2048)
    w1t, w2t = A["L0_ffn_w1_w"].T, A["L0_ffn_w2_w"].T
    h1t, h2t, w3t = A["L0_ffn_h1_w"].T, A["L0_ffn_h2_w"].T, A["L0_ffn_w3_w"].T
    d["w1t"] = w1t.reshape(8, 128, 8, 512).transpose(2, 1, 0, 3).reshape(8, 128, 4096)
    d["w2t"] = w2t.reshape(8, 128, 8, 512).transpose(2, 1, 0, 3).reshape(8, 128, 4096)
    d["h1t"] = (h1t.reshape(32, 128, 16, 256).transpose(2, 1, 0, 3).reshape(16, 128, 2, 4096)
                .transpose(0, 2, 1, 3).reshape(32, 128, 4096))
    d["h2t"] = (h2t.reshape(32, 128, 16, 256).transpose(2, 1, 0, 3).reshape(16, 128, 2, 4096)
                .transpose(0, 2, 1, 3).reshape(32, 128, 4096))
    d["w3t"] = w3t.reshape(32, 128, 8, 128).transpose(2, 1, 0, 3).reshape(8, 128, 4096)
    d["s1"] = A["L0_ffn_w1_s"].reshape(32, 128).T
    d["s2"] = A["L0_ffn_w2_s"].reshape(32, 128).T
    d["sh1"] = A["L0_ffn_h1_s"].reshape(32, 128).T
    d["sh2"] = A["L0_ffn_h2_s"].reshape(32, 128).T
    d["s3"] = A["L0_ffn_w3_s"].reshape(8, 128).T
    return {k: np.ascontiguousarray(v, dtype=np.float32) for k, v in d.items()}


def build_program(npass=L // PT, do_ffn=True, debug=()):
    nc = bass.Bass("TRN2", target_bir_lowering=False, detect_race_conditions=False)
    sc = Sched()
    split = (npass == L // PT)

    def din(name, shape, dt=F32):
        return nc.dram_tensor(name, list(shape), dt, kind="ExternalInput").ap()

    x_attn = din("x_attn", (L, D))
    x_res = din("x_res", (TOK, D))
    wq, wk, wv = din("wq", (128, 2048)), din("wk", (128, 2048)), din("wv", (128, 2048))
    wb = din("wb", (128, 8))
    convw, firw = din("convw", (128, 24)), din("firw", (128, 68))
    gw1h, gw1b = din("gw1h", (128, 8 * GH)), din("gw1b", (128, 8 * GH))
    gw1s = din("gw1s", (128, 24 * GH))
    gb1, gw2, gb2 = din("gb1", (128, 4)), din("gw2", (128, 16)), din("gb2", (1, 4))
    temp, floor, onorm = din("temp", (1, 1)), din("floor", (1, 4)), din("onorm", (1, 256))
    wo = din("wo", (128, 2048))
    w1t, w2t = din("w1t", (8, 128, 4096)), din("w2t", (8, 128, 4096))
    h1t, h2t = din("h1t", (32, 128, 4096)), din("h2t", (32, 128, 4096))
    w3t = din("w3t", (8, 128, 4096))
    s1c, s2c = din("s1", (128, 32)), din("s2", (128, 32))
    sh1c, sh2c = din("sh1", (128, 32)), din("sh2", (128, 32))
    s3col = din("s3", (128, 8))

    out = nc.dram_tensor("out", [TOK, D], F32, kind="ExternalOutput").ap()
    dbg = {}
    for name, shape in debug:
        dbg[name] = nc.dram_tensor("dbg_" + name, list(shape), F32, kind="ExternalOutput").ap()

    cc_in = nc.dram_tensor("cc_in", [L, D], F32, kind="Internal")
    cc_rs = nc.dram_tensor("cc_rs", [TOK, D], F32, kind="Internal")
    x2stash = nc.dram_tensor("x2stash", [TOK, D], F32, kind="Internal").ap()

    st = ExitStack()
    E = st.enter_context
    for s in ("dG", "dY", "T", "V", "S", "G", "CC"):
        sc.h[s] = E(nc.semaphore("sm" + s))

    PS_S = E(nc.psum_tensor("ps_s", [128, 512], F32))
    PS_TA = E(nc.psum_tensor("ps_ta", [128, 1024], BF16))
    PS_TB = E(nc.psum_tensor("ps_tb", [128, 1024], BF16))
    PS_WA = E(nc.psum_tensor("ps_wa", [128, 512], F32))
    PS_WB = E(nc.psum_tensor("ps_wb", [128, 512], F32))
    PS_WC = E(nc.psum_tensor("ps_wc", [128, 512], F32))
    PS_B0 = E(nc.psum_tensor("ps_b0", [128, 512], F32))
    PS_B1 = E(nc.psum_tensor("ps_b1", [128, 512], F32))
    PSB = [PS_B0, PS_B1]

    IDb = E(nc.sbuf_tensor("idb", [128, 128], BF16))
    IDF = E(nc.sbuf_tensor("idf", [128, 128], F32))
    ONE2 = E(nc.sbuf_tensor("one2", [2, 128], BF16))
    CSTT = E(nc.sbuf_tensor("cstt", [128, 4], F32))
    ZRO = E(nc.sbuf_tensor("zro", [128, 512], BF16))
    nc.const_aps.register(F32, 1e-6, CSTT[:, 0:1])
    p1 = ExitStack()
    E1 = p1.enter_context
    MKU = E1(nc.sbuf_tensor("mku", [128, 128], BF16))
    MKUS = E1(nc.sbuf_tensor("mkus", [128, 128], BF16))
    ONESC = E1(nc.sbuf_tensor("onesc", [128, 1], BF16))
    WQ = E1(nc.sbuf_tensor("wqb", [128, 2048], BF16))
    WK = E1(nc.sbuf_tensor("wkb", [128, 2048], BF16))
    WV = E1(nc.sbuf_tensor("wvb", [128, 2048], BF16))
    WBc = E1(nc.sbuf_tensor("wbb", [128, 8], BF16))
    WOs = E1(nc.sbuf_tensor("wos", [128, 2048], BF16))
    CDIAG = E1(nc.sbuf_tensor("cdiag", [128, 24 * 128], BF16))
    FDIAG = E1(nc.sbuf_tensor("fdiag", [128, 68 * 128], BF16))
    CVW = E1(nc.sbuf_tensor("cvw", [128, 24], F32))
    FIW = E1(nc.sbuf_tensor("fiw", [128, 68], F32))
    GW1H = E1(nc.sbuf_tensor("gw1hb", [128, 8 * GH], BF16))
    GW1B = E1(nc.sbuf_tensor("gw1bb", [128, 8 * GH], BF16))
    GW2 = E1(nc.sbuf_tensor("gw2b", [128, 16], BF16))
    GB1 = E1(nc.sbuf_tensor("gb1b", [128, 4], F32))
    STL = E1(nc.sbuf_tensor("stl", [128, GH], BF16))
    GB2T = E1(nc.sbuf_tensor("gb2t", [128, 4], F32))
    FLT = E1(nc.sbuf_tensor("flt", [128, 4], F32))
    ONT = E1(nc.sbuf_tensor("ont", [128, 256], BF16))
    TIV = E1(nc.sbuf_tensor("tiv", [128, 1], F32))
    RR2 = E1(nc.sbuf_tensor("rr2", [2, 512], BF16))

    def _prev(v):
        v.memset(IDb[:, :], 0.0)
        v.memset(IDF[:, :], 0.0)
        v.memset(MKU[:, :], 0.0)
        v.memset(MKUS[:, :], 0.0)
        v.memset(ONE2[:, :], 0.0)
        v.memset(ONE2[0:1, :], 1.0)
        v.memset(RR2[:, :], 0.0)
        v.memset(ONT[:, :], 0.0)
        v.memset(CSTT[:, :], 0.0)
        v.memset(ZRO[:, :], 0.0)
        v.memset(CSTT[:, 0:1], 1e-6)
        return v.memset(ONESC[:, :], 1.0)
    sc.inc("V", "V", _prev)
    sc.wait("G", "V")
    def _pre(g):
        g.affine_select(out=IDb[:, :], in_=IDb[:, :], compare_op=ALU.not_equal,
                        fill=1.0, base=0, pattern=[[-1, 128]], channel_multiplier=1)
        g.affine_select(out=IDF[:, :], in_=IDF[:, :], compare_op=ALU.not_equal,
                        fill=1.0, base=0, pattern=[[-1, 128]], channel_multiplier=1)
        g.affine_select(out=MKU[:, :], in_=MKU[:, :], compare_op=ALU.is_gt,
                        fill=1.0, base=0, pattern=[[-1, 128]], channel_multiplier=1)
        return g.affine_select(out=MKUS[:, :], in_=MKUS[:, :], compare_op=ALU.is_ge,
                               fill=1.0, base=0, pattern=[[-1, 128]], channel_multiplier=1)
    sc.inc("G", "G", _pre)
    nG_pre = sc.n("G")

    def dmaG(dst, src):
        return sc.inc("G", "dG", lambda g: g.dma_start(dst, src), 16)

    def dmaY(dst, src, **kw):
        return sc.inc("Y", "dY", lambda y: y.dma_start(dst, src, **kw), 16)

    for dst, src in ((WQ, wq), (WK, wk), (WV, wv), (WBc, wb), (CVW, convw), (FIW, firw),
                     (GW1H, gw1h), (GW1B, gw1b), (GW2, gw2), (GB1, gb1), (WOs, wo)):
        dmaG(dst[:, :], src)
    n_wl = sc.n("dG")

    sc.wait("V", "dG", n_wl)
    sc.wait("V", "G", nG_pre)
    def _diags(v):
        for i in range(24):
            v.tensor_scalar(CDIAG[:, i * 128:(i + 1) * 128], IDb[:, :], CVW[:, i:i + 1],
                            None, ALU.mult)
        for i in range(67):
            v.tensor_scalar(FDIAG[:, i * 128:(i + 1) * 128], IDb[:, :], FIW[:, i:i + 1],
                            None, ALU.mult)
        return v.tensor_scalar(FDIAG[:, 67 * 128:68 * 128], IDb[:, :], FIW[:, 67:68],
                               None, ALU.mult)
    sc.inc("V", "V", _diags)
    nV_diag = sc.n("V")

    pgs = ExitStack()
    GS = pgs.enter_context(nc.sbuf_tensor("gsb0", [128, 24 * GH], BF16, side="right"))
    dmaG(GS[:, :], gw1s)
    dmaG(ONT[0:1, 0:256], onorm)
    n_gs = sc.n("dG")
    SRW = pgs.enter_context(nc.sbuf_tensor("srw", [1, 12 * GH], F32, side="right"))
    sc.wait("T", "dG", n_gs)
    sc.wait("T", "G", nG_pre)
    for s in range(12):
        def _mm(t, s=s):
            t.matmul(PS_WA[0:1, 0:512], ONESC[:, :], GS[:, (2 * s) * GH:(2 * s + 1) * GH],
                     start=True, stop=False)
            return t.matmul(PS_WA[0:1, 0:512], ONESC[:, :],
                            GS[:, (2 * s + 1) * GH:(2 * s + 2) * GH], start=False, stop=True)
        tk = sc.inc("T", "T", _mm)
        sc.wait("S", "T", tk)
        scale = (1.0 / 256.0) if s % 3 == 0 else 1.0
        sc.inc("S", "S", lambda sl, s=s, scale=scale: sl.activation(
            SRW[0:1, s * GH:(s + 1) * GH], PS_WA[0:1, 0:512], AF.Copy, scale=scale))
        sc.wait("T", "S")
    sc.inc("V", "V", lambda v: v.memset(STL[:, :], 0.0))
    sc.wait("G", "S")
    sc.wait("G", "V")
    for s in range(12):
        sc.inc("G", "dG", lambda g, s=s: g.dma_start(
            bass.AP(STL, s * GH, [[GH, 1], [1, GH]]),
            bass.AP(SRW, s * GH, [[12 * GH, 1], [1, GH]])), 16)

    SC1 = E1(nc.sbuf_tensor("sc1", [1, 32], F32))
    sc.inc("V", "V", lambda v: v.memset(SC1[:, :], 0.0))
    sc.wait("Y", "V")
    dmaY(SC1[0:1, 0:4], gb2)
    dmaY(SC1[0:1, 8:12], floor)
    dmaY(SC1[0:1, 16:17], temp)
    n_sc1 = sc.n("dY")
    sc.wait("S", "dY", n_sc1)
    sc.wait("S", "dG", n_gs)
    sc.wait("S", "G", nG_pre)
    sc.inc("S", "S", lambda sl: sl.copy(RR2[0:1, 0:32], SC1[0:1, 0:32]))
    sc.wait("T", "S")
    def _bc1(t):
        t.matmul(PS_WB[:, 0:4], ONE2[:, 0:128], RR2[0:2, 0:4], start=True, stop=True)
        t.matmul(PS_WB[:, 4:8], ONE2[:, 0:128], RR2[0:2, 8:12], start=True, stop=True)
        t.matmul(PS_WB[:, 8:9], ONE2[:, 0:128], RR2[0:2, 16:17], start=True, stop=True)
        return t.matmul(PS_WB[:, 16:272], ONE2[:, 0:128],
                        bass.AP(ONT, 0, [[256, 2], [1, 256]]), start=True, stop=True)
    tk = sc.inc("T", "T", _bc1)
    sc.wait("V", "T", tk)
    def _bc2(v):
        v.tensor_copy(GB2T[:, :], PS_WB[:, 0:4])
        v.tensor_scalar(FLT[:, :], PS_WB[:, 4:8], 1e-7, 0.1, ALU.max, ALU.min)
        v.tensor_scalar(TIV[:, :], PS_WB[:, 8:9], 0.2, 10.0, ALU.max, ALU.min)
        v.tensor_copy(TIV[:, 0:1], PS_WB[:, 8:9])
        v.reciprocal(TIV[:, :], TIV[:, :])
        v.tensor_scalar(FLT[:, :], PS_WB[:, 4:8], 1e-7, 0.1, ALU.max, ALU.min)
        return v.tensor_copy(ONT[:, :], PS_WB[:, 16:272])
    sc.inc("V", "V", _bc2)
    sc.wait("T", "V")

    pgs.close()
    sc.wait("G", "T")
    XT = [E1(nc.sbuf_tensor(f"xt{i}", [128, 4 * D], F32)) for i in range(2)]
    XLNT = E1(nc.sbuf_tensor("xlnt", [128, D], BF16))
    XLN = E1(nc.sbuf_tensor("xln", [128, 8 * PT], BF16))
    PQ = E1(nc.sbuf_tensor("pq", [128, 2 * 515], BF16))
    PK = E1(nc.sbuf_tensor("pk", [128, 2 * 515], BF16))
    PV = E1(nc.sbuf_tensor("pv", [128, 2 * 515], BF16))
    QF = E1(nc.sbuf_tensor("qf", [128, 2 * PT], BF16))
    KF = E1(nc.sbuf_tensor("kf", [128, 2 * PT], BF16))
    VF = E1(nc.sbuf_tensor("vf", [128, 2 * 542], BF16))
    KBF = E1(nc.sbuf_tensor("kbf", [128, 2 * PT], BF16))
    FSF = E1(nc.sbuf_tensor("fsf", [128, 2 * PT], BF16))
    FLF = E1(nc.sbuf_tensor("flf", [128, 2 * PT], BF16))
    DOF = E1(nc.sbuf_tensor("dof", [128, 2 * PT], BF16))
    OPF = E1(nc.sbuf_tensor("opf", [128, 2 * PT], BF16))
    OPART = E1(nc.sbuf_tensor("opart", [128, 4 * D], F32))
    SQF = E1(nc.sbuf_tensor("sqf", [128, 2 * PT], BF16))
    VTM = E1(nc.sbuf_tensor("vtm", [128, NCH * 256], BF16))
    FSTM = E1(nc.sbuf_tensor("fstm", [128, NCH * 256], BF16))
    FLTM = E1(nc.sbuf_tensor("fltm", [128, NCH * 256], BF16))
    DOTM = E1(nc.sbuf_tensor("dotm", [128, NCH * 256], BF16))
    KNTM = E1(nc.sbuf_tensor("kntm", [128, 256], BF16))
    KBTM = E1(nc.sbuf_tensor("kbtm", [128, 256], BF16))
    VBTM = E1(nc.sbuf_tensor("vbtm", [128, 256], BF16))
    UP = E1(nc.sbuf_tensor("upb", [128, 256], BF16))
    SB = E1(nc.sbuf_tensor("sbb", [128, 512], BF16))
    SFP = E1(nc.sbuf_tensor("sfp", [128, 512], F32))
    XPI = E1(nc.sbuf_tensor("xpi", [128, 128], BF16))
    XCt = E1(nc.sbuf_tensor("xcb", [128, 128], BF16))
    YCt = E1(nc.sbuf_tensor("ycb", [128, 128], BF16))
    PCt = E1(nc.sbuf_tensor("pcb", [128, 128], BF16))
    PTt = E1(nc.sbuf_tensor("ptcb", [128, 128], BF16))
    ATT = E1(nc.sbuf_tensor("attb", [128, 128], BF16))
    WNEG = E1(nc.sbuf_tensor("wneg", [128, 256], BF16))
    GSBt = E1(nc.sbuf_tensor("gsbt", [128, 4 * PT], BF16))
    STATS = E1(nc.sbuf_tensor("stats", [128, GH], BF16))
    SPK = E1(nc.sbuf_tensor("spk", [128, 16], BF16))
    WGt = E1(nc.sbuf_tensor("wgt", [128, 4], F32))
    OT1 = E1(nc.sbuf_tensor("ot1", [128, 256], BF16))
    OT2 = E1(nc.sbuf_tensor("ot2", [128, 256], BF16))
    OTB = E1(nc.sbuf_tensor("otb", [128, 256], BF16))
    ROWS = E1(nc.sbuf_tensor("rows", [128, 512], BF16))
    RRF = E1(nc.sbuf_tensor("rrf", [128, 1024], F32))
    COLS = E1(nc.sbuf_tensor("colsb", [128, 64], F32))
    SCR = E1(nc.sbuf_tensor("scrb", [128, 1024], BF16))
    SCF = E1(nc.sbuf_tensor("scfb", [128, 16], F32))
    ETM = E1(nc.sbuf_tensor("etm", [128, 4], F32))

    def _z1(v):
        v.memset(PQ[:, :], 0.0)
        v.memset(PK[:, :], 0.0)
        v.memset(PV[:, :], 0.0)
        v.memset(VF[:, :], 0.0)
        v.memset(ROWS[:, :], 0.0)
        v.memset(SPK[:, :], 0.0)
        v.memset(SB[:, :], 0.0)
        v.memset(SFP[:, :], 0.0)
        return v.memset(STATS[:, :], 0.0)
    sc.inc("V", "V", _z1)

    def load_x(p):
        buf = XT[p % 2]
        for t in range(4):
            dmaG(buf[:, t * D:(t + 1) * D],
                 x_attn[p * PT + t * 128:p * PT + (t + 1) * 128, :])
    load_x(0)

    sc.wait("T", "G", sc.n("G"))
    sc.wait("T", "V", nV_diag)
    sc.wait("S", "G", sc.n("G"))
    CB, CSK, CSQ, CBSK, CNSK = 0, 4, 8, 12, 16

    for p in range(npass):
        sc.wait("S", "V")
        sc.wait("T", "V")
        sc.wait("G", "S")
        sc.wait("S", "dY")
        buf = XT[p % 2]
        nx = sc.n("dG")
        if p + 1 < npass:
            load_x(p + 1)
        if split and p == 8:
            sc.wait("G", "dY")
            sc.wait("G", "dG")
            def _rs1(g):
                return g.collective_compute(
                    "ReduceScatter", ALU.add,
                    replica_groups=[[0, 1, 2, 3], [4, 5, 6, 7]],
                    ins=[cc_in.ap()[0:L // 2, :].opt()],
                    outs=[cc_rs.ap()[0:TOK // 2, :].opt()])
            sc.inc("G", "CC", _rs1)
            sc.wait("G", "CC")

        sc.wait("S", "dG", nx)
        for t in range(4):
            xt = buf[:, t * D:(t + 1) * D]
            def _lnacc(sl, xt=xt):
                sl.activation(SCR[:, 0:1024], xt, AF.Copy, accum_out=SCF[:, 0:1])
                sl.activation(SCR[:, 0:1024], xt, AF.Square, accum_out=SCF[:, 1:2])
                return sl.copy(SCF[:, 15:16], CSTT[:, 0:1])
            sc.inc("S", "S", _lnacc)
            sc.wait("V", "S")
            def _lncols(v):
                v.tensor_scalar(SCF[:, 2:3], SCF[:, 0:1], 1.0 / D, None, ALU.mult)
                v.tensor_scalar(SCF[:, 3:4], SCF[:, 1:2], 1.0 / D, None, ALU.mult)
                v.tensor_copy(SCF[:, 9:10], ZRO[:, 0:1])
                v.tensor_mul(SCF[:, 4:5], SCF[:, 2:3], SCF[:, 2:3])
                v.tensor_copy(SCF[:, 9:10], ZRO[:, 0:1])
                v.tensor_sub(SCF[:, 3:4], SCF[:, 3:4], SCF[:, 4:5])
                v.tensor_copy(SCF[:, 9:10], ZRO[:, 0:1])
                return v.tensor_scalar(SCF[:, 3:4], SCF[:, 3:4], 1e-5, None, ALU.add)
            sc.inc("V", "V", _lncols)
            sc.wait("S", "V")
            sc.inc("S", "S", lambda sl: sl.sqrt(SCF[:, 3:4], SCF[:, 3:4]))
            sc.wait("V", "S")
            def _lninv(v):
                v.reciprocal(SCF[:, 3:4], SCF[:, 3:4])
                v.tensor_copy(SCF[:, 5:6], SCF[:, 2:3])
                v.tensor_mul(SCF[:, 4:5], SCF[:, 2:3], SCF[:, 3:4])
                v.tensor_copy(SCF[:, 9:10], ZRO[:, 0:1])
                return v.tensor_scalar(SCF[:, 4:5], SCF[:, 4:5], -1.0, None, ALU.mult)
            sc.inc("V", "V", _lninv)
            sc.wait("S", "V")
            sc.inc("S", "S", lambda sl, xt=xt: sl.activation(
                XLNT[:, :], xt, AF.Identity, bias=SCF[:, 4:5], scale=SCF[:, 3:4]))
            nS_ln = sc.n("S")
            if p == npass - 1 and t == 3 and "xlnt" in dbg:
                sc.wait("G", "S", nS_ln)
                sc.wait("G", "V")
                dmaG(dbg["xlnt"], XLNT[:, :])
                dmaG(dbg["lncols"], SCF[:, 0:8])
                sc.wait("S", "dG")
                sc.wait("T", "dG")
                sc.wait("V", "dG")
            sc.wait("T", "S", nS_ln)
            def _trx(tt):
                for k in range(7):
                    tt.transpose(PS_TA[:, k * 128:(k + 1) * 128],
                                 XLNT[:, k * 128:(k + 1) * 128], IDb[:, :])
                return tt.transpose(PS_TA[:, 896:1024], XLNT[:, 896:1024], IDb[:, :])
            tk = sc.inc("T", "T", _trx)
            sc.wait("S", "T", tk)
            sc.inc("S", "S", lambda sl, t=t: sl.copy(
                bass.AP(XLN, t * 128, [[8 * PT, 128], [PT, 8], [1, 128]]),
                bass.AP(PS_TA, 0, [[1024, 128], [128, 8], [1, 128]])))
            sc.wait("T", "S")

        nS_xln = sc.n("S")

        sc.wait("T", "S", nS_xln)
        for W, P, pi in ((WQ, PQ, 0), (WK, PK, 1), (WV, PV, 2)):
            for ch in range(2):
                bank = PSB[(pi * 2 + ch) % 2]
                sc.wait("T", "S")
                def _proj(t, W=W, ch=ch, bank=bank):
                    for k in range(7):
                        t.matmul(bank[:, 0:512], W[:, k * 256 + ch * 128:k * 256 + ch * 128 + 128],
                                 XLN[:, k * PT:(k + 1) * PT], start=(k == 0), stop=False)
                    return t.matmul(bank[:, 0:512], W[:, 7 * 256 + ch * 128:7 * 256 + ch * 128 + 128],
                                    XLN[:, 7 * PT:8 * PT], start=False, stop=True)
                tk = sc.inc("T", "T", _proj)
                sc.wait("S", "T", tk)
                sc.inc("S", "S", lambda sl, P=P, ch=ch, bank=bank: sl.copy(
                    P[:, ch * 515 + 3:ch * 515 + 515], bank[:, 0:512]))
        def _betamm(t):
            for k in range(7):
                t.matmul(PS_WA[0:1, 0:512], WBc[:, k:k + 1], XLN[:, k * PT:(k + 1) * PT],
                         start=(k == 0), stop=False)
            return t.matmul(PS_WA[0:1, 0:512], WBc[:, 7:8], XLN[:, 7 * PT:8 * PT],
                            start=False, stop=True)
        sc.inc("T", "T", _betamm)
        nS_proj = sc.n("S")
        sc.wait("T", "S", nS_proj)
        for P, O, pi in ((PQ, QF, 0), (PK, KF, 1), (PV, VF, 2)):
            for ch in range(2):
                bank = PSB[(pi * 2 + ch) % 2]
                sc.wait("T", "S")
                def _conv(t, P=P, pi=pi, ch=ch, bank=bank):
                    for j in range(3):
                        t.matmul(bank[:, 0:512],
                                 CDIAG[:, (pi * 8 + ch * 4 + j) * 128:(pi * 8 + ch * 4 + j + 1) * 128],
                                 P[:, ch * 515 + j:ch * 515 + j + 512], start=(j == 0), stop=False)
                    return t.matmul(bank[:, 0:512],
                                    CDIAG[:, (pi * 8 + ch * 4 + 3) * 128:(pi * 8 + ch * 4 + 4) * 128],
                                    P[:, ch * 515 + 3:ch * 515 + 3 + 512], start=False, stop=True)
                tk = sc.inc("T", "T", _conv)
                sc.wait("S", "T", tk)
                off = (ch * 542 + 30) if O is VF else (ch * PT)
                sc.inc("S", "S", lambda sl, O=O, off=off, bank=bank: sl.activation(
                    O[:, off:off + 512], bank[:, 0:512], AF.Silu))
        nS_conv = sc.n("S")

        for ch in range(2):
            sc.inc("S", "S", lambda sl, ch=ch: sl.activation(
                SQF[:, ch * PT:(ch + 1) * PT], KF[:, ch * PT:(ch + 1) * PT], AF.Square))
        nS_ksq = sc.n("S")
        sc.wait("T", "S", nS_ksq)
        def _ksum(t):
            t.matmul(PS_WA[32:33, 0:512], ONESC[:, :], SQF[:, 0:PT], start=True, stop=False)
            return t.matmul(PS_WA[32:33, 0:512], ONESC[:, :], SQF[:, PT:2 * PT],
                            start=False, stop=True)
        tk_k = sc.inc("T", "T", _ksum)
        sc.wait("S", "T", tk_k)
        for ch in range(2):
            sc.inc("S", "S", lambda sl, ch=ch: sl.activation(
                SQF[:, ch * PT:(ch + 1) * PT], QF[:, ch * PT:(ch + 1) * PT], AF.Square))
        nS_qsq = sc.n("S")
        sc.wait("T", "S", nS_qsq)
        def _qsum(t):
            t.matmul(PS_WA[64:65, 0:512], ONESC[:, :], SQF[:, 0:PT], start=True, stop=False)
            return t.matmul(PS_WA[64:65, 0:512], ONESC[:, :], SQF[:, PT:2 * PT],
                            start=False, stop=True)
        tk_q = sc.inc("T", "T", _qsum)
        sc.wait("S", "T", tk_q)
        def _rows(sl):
            sl.activation(RRF[0:1, 0:512], PS_WA[0:1, 0:512], AF.Sigmoid)
            sl.activation(RRF[32:33, 0:512], PS_WA[32:33, 0:512], AF.Sqrt, bias=1e-6)
            return sl.activation(RRF[64:65, 0:512], PS_WA[64:65, 0:512], AF.Sqrt, bias=1e-6)
        sc.inc("S", "S", _rows)
        sc.wait("V", "S")
        def _rows2(v):
            v.reciprocal(RRF[32:33, 0:512], RRF[32:33, 0:512])
            v.tensor_copy(SCF[:, 9:10], ZRO[:, 0:1])
            v.reciprocal(RRF[64:65, 0:512], RRF[64:65, 0:512])
            v.tensor_copy(SCF[:, 9:10], ZRO[:, 0:1])
            v.tensor_copy(RRF[0:1, 512:1024], RRF[32:33, 0:512])
            v.tensor_copy(SCF[:, 9:10], ZRO[:, 0:1])
            return v.tensor_mul(RRF[0:1, 512:1024], RRF[0:1, 512:1024], RRF[0:1, 0:512])
        sc.inc("V", "V", _rows2)
        sc.wait("T", "V")
        sc.wait("S", "V")
        def _rows3(sl):
            sl.copy(ROWS[0:1, 0:512], RRF[0:1, 0:512])
            sl.copy(ROWS[32:33, 0:512], RRF[32:33, 0:512])
            sl.copy(ROWS[64:65, 0:512], RRF[64:65, 0:512])
            return sl.copy(RR2[0:1, 0:512], RRF[0:1, 512:1024])
        sc.inc("S", "S", _rows3)
        nS_rows = sc.n("S")
        sc.wait("T", "S", nS_rows)
        def _bskmm(t):
            return t.matmul(PS_WC[:, 0:512], ONE2[:, 0:128], RR2[0:2, 0:512],
                            start=True, stop=True)
        tk = sc.inc("T", "T", _bskmm)
        sc.wait("V", "T", tk)
        def _kbf(v):
            v.tensor_mul(KBF[:, 0:PT], KF[:, 0:PT], PS_WC[:, 0:512])
            return v.tensor_mul(KBF[:, PT:2 * PT], KF[:, PT:2 * PT], PS_WC[:, 0:512])
        sc.inc("V", "V", _kbf)
        sc.wait("T", "V")
        for c in range(NCH):
            def _trr(t, c=c):
                return t.transpose(PS_TB[:, 0:128], ROWS[:, c * 128:(c + 1) * 128], IDb[:, :])
            tk = sc.inc("T", "T", _trr)
            sc.wait("V", "T", tk)
            def _colx(v, c=c):
                v.tensor_copy(COLS[:, CB + c:CB + c + 1], PS_TB[:, 0:1])
                v.tensor_copy(COLS[:, CSK + c:CSK + c + 1], PS_TB[:, 32:33])
                v.tensor_copy(COLS[:, CSQ + c:CSQ + c + 1], PS_TB[:, 64:65])
                v.tensor_copy(SCF[:, 9:10], ZRO[:, 0:1])
                v.tensor_mul(COLS[:, CBSK + c:CBSK + c + 1], COLS[:, CB + c:CB + c + 1],
                             COLS[:, CSK + c:CSK + c + 1])
                v.tensor_copy(SCF[:, 9:10], ZRO[:, 0:1])
                return v.tensor_scalar(COLS[:, CNSK + c:CNSK + c + 1],
                                       COLS[:, CSK + c:CSK + c + 1], -1.0, None, ALU.mult)
            sc.inc("V", "V", _colx)
            sc.wait("T", "V")

        for (O, base, ntap) in ((FSF, 0, 3), (FLF, 6, 31)):
            for ch in range(2):
                bank = PSB[ch]
                sc.wait("T", "S")
                def _fir(t, O=O, base=base, ntap=ntap, ch=ch, bank=bank):
                    for j in range(ntap - 1):
                        t.matmul(bank[:, 0:512],
                                 FDIAG[:, (base + ch * ntap + j) * 128:(base + ch * ntap + j + 1) * 128],
                                 VF[:, ch * 542 + (31 - ntap) + j:ch * 542 + (31 - ntap) + j + 512],
                                 start=(j == 0), stop=False)
                    j = ntap - 1
                    return t.matmul(bank[:, 0:512],
                                    FDIAG[:, (base + ch * ntap + j) * 128:(base + ch * ntap + j + 1) * 128],
                                    VF[:, ch * 542 + (31 - ntap) + j:ch * 542 + (31 - ntap) + j + 512],
                                    start=False, stop=True)
                tk = sc.inc("T", "T", _fir)
                sc.wait("S", "T", tk)
                sc.inc("S", "S", lambda sl, O=O, ch=ch, bank=bank: sl.copy(
                    O[:, ch * PT:(ch + 1) * PT], bank[:, 0:512]))
        nS_fir = sc.n("S")

        for c in range(NCH):
            gc = p * NCH + c
            t0 = c * 128
            sc.wait("T", "S", nS_fir)
            sc.wait("T", "S")
            def _trc(t, t0=t0):
                srcs = [KF[:, t0:t0 + 128], KF[:, PT + t0:PT + t0 + 128],
                        VF[:, 30 + t0:30 + t0 + 128], VF[:, 542 + 30 + t0:542 + 30 + t0 + 128],
                        FSF[:, t0:t0 + 128], FSF[:, PT + t0:PT + t0 + 128],
                        FLF[:, t0:t0 + 128], FLF[:, PT + t0:PT + t0 + 128]]
                for i, s_ in enumerate(srcs[:-1]):
                    t.transpose(PS_TA[:, i * 128:(i + 1) * 128], s_, IDb[:, :])
                return t.transpose(PS_TA[:, 896:1024], srcs[-1], IDb[:, :])
            tk = sc.inc("T", "T", _trc)
            sc.wait("S", "T", tk)
            def _evc(sl, c=c):
                sl.activation(KNTM[:, :], PS_TA[:, 0:256], AF.Copy,
                              scale=COLS[:, CSK + c:CSK + c + 1])
                sl.activation(KBTM[:, :], PS_TA[:, 0:256], AF.Copy,
                              scale=COLS[:, CBSK + c:CBSK + c + 1])
                sl.activation(VBTM[:, :], PS_TA[:, 256:512], AF.Copy,
                              scale=COLS[:, CB + c:CB + c + 1])
                sl.copy(VTM[:, c * 256:(c + 1) * 256], PS_TA[:, 256:512])
                sl.copy(FSTM[:, c * 256:(c + 1) * 256], PS_TA[:, 512:768])
                return sl.copy(FLTM[:, c * 256:(c + 1) * 256], PS_TA[:, 768:1024])
            sc.inc("S", "S", _evc)
            nS_evc = sc.n("S")

            sc.wait("T", "S", nS_evc)
            def _grA(t, t0=t0):
                t.matmul(PS_WA[:, 0:128], KF[:, t0:t0 + 128], KBF[:, t0:t0 + 128],
                         start=True, stop=False)
                return t.matmul(PS_WA[:, 0:128], KF[:, PT + t0:PT + t0 + 128],
                                KBF[:, PT + t0:PT + t0 + 128], start=False, stop=True)
            tk = sc.inc("T", "T", _grA)
            sc.wait("V", "T", tk)
            sc.inc("V", "V", lambda v: v.tensor_mul(SCR[:, 0:128], PS_WA[:, 0:128], MKUS[:, :]))
            sc.wait("S", "V")
            sc.inc("S", "S", lambda sl, c=c: sl.activation(
                XCt[:, :], SCR[:, 0:128], AF.Copy, scale=COLS[:, CNSK + c:CNSK + c + 1]))
            nS_x = sc.n("S")
            sc.wait("T", "S", nS_x)
            tk = sc.inc("T", "T", lambda t: t.transpose(PS_TB[:, 0:128], XCt[:, :], IDb[:, :]))
            sc.wait("S", "T", tk)
            sc.inc("S", "S", lambda sl: sl.copy(YCt[:, :], PS_TB[:, 0:128]))
            nS_y = sc.n("S")
            if p == 0 and c == 0 and "xc0" in dbg:
                sc.wait("G", "S", nS_y)
                dmaG(dbg["xc0"], XCt[:, :])
                dmaG(dbg["kntm"], KNTM[:, :])
                dmaG(dbg["kbtm"], KBTM[:, :])
                dmaG(dbg["vbtm"], VBTM[:, :])
                sc.wait("S", "dG")
                sc.wait("T", "dG")

            sc.wait("V", "S", nS_y)
            def _pinit(v):
                v.tensor_add(PCt[:, :], XCt[:, :], IDb[:, :])
                return v.tensor_add(PTt[:, :], YCt[:, :], IDb[:, :])
            sc.inc("V", "V", _pinit)
            sc.wait("T", "V")
            sc.wait("S", "V")
            for j in range(1, 7):
                def _sq2(t):
                    t.matmul(PS_WA[:, 128:256], YCt[:, :], XCt[:, :], start=True, stop=True)
                    return t.matmul(PS_WA[:, 256:384], XCt[:, :], YCt[:, :], start=True, stop=True)
                tk = sc.inc("T", "T", _sq2)
                sc.wait("S", "T", tk)
                def _sqe(sl):
                    sl.copy(XCt[:, :], PS_WA[:, 128:256])
                    return sl.copy(YCt[:, :], PS_WA[:, 256:384])
                sc.inc("S", "S", _sqe)
                sc.wait("V", "S")
                sc.inc("V", "V", lambda v: v.tensor_add(XPI[:, :], XCt[:, :], IDb[:, :]))
                sc.wait("T", "V")
                def _upd(t):
                    t.matmul(PS_WB[:, 0:128], PTt[:, :], XPI[:, :], start=True, stop=True)
                    return t.matmul(PS_WB[:, 128:256], XPI[:, :], PTt[:, :], start=True, stop=True)
                tk = sc.inc("T", "T", _upd)
                sc.wait("S", "T", tk)
                def _pce(sl):
                    sl.copy(PCt[:, :], PS_WB[:, 0:128])
                    return sl.copy(PTt[:, :], PS_WB[:, 128:256])
                sc.inc("S", "S", _pce)
            nS_chain = sc.n("S")
            if p == 0 and c == 0 and "pc0" in dbg:
                sc.wait("G", "S", nS_chain)
                dmaG(dbg["pc0"], PCt[:, :])
                dmaG(dbg["ptc0"], PTt[:, :])
                sc.wait("S", "dG")
                sc.wait("T", "dG")

            sc.wait("T", "S", nS_chain)
            def _wmm(t):
                t.matmul(PS_WA[:, 0:128], KBTM[:, 0:128], PCt[:, :], start=True, stop=True)
                return t.matmul(PS_WA[:, 384:512], KBTM[:, 128:256], PCt[:, :],
                                start=True, stop=True)
            tk = sc.inc("T", "T", _wmm)
            sc.wait("S", "T", tk)
            def _wng(sl):
                sl.mul(WNEG[:, 0:128], PS_WA[:, 0:128], -1.0)
                return sl.mul(WNEG[:, 128:256], PS_WA[:, 384:512], -1.0)
            sc.inc("S", "S", _wng)
            nS_w = sc.n("S")
            sc.wait("T", "S", nS_w)
            def _ups(t):
                t.matmul(PS_WC[:, 0:256], PCt[:, :], VBTM[:, :], start=True, stop=False)
                t.matmul(PS_WC[:, 0:256], WNEG[:, 0:128], SB[:, 0:256], start=False, stop=False)
                return t.matmul(PS_WC[:, 0:256], WNEG[:, 128:256], SB[:, 256:512],
                                start=False, stop=True)
            tk = sc.inc("T", "T", _ups)
            sc.wait("S", "T", tk)
            sc.inc("S", "S", lambda sl: sl.copy(UP[:, :], PS_WC[:, 0:256]))
            nS_up = sc.n("S")
            if p == 0 and c == 0 and "up0" in dbg:
                sc.wait("G", "S", nS_up)
                dmaG(dbg["up0"], UP[:, :])
                dmaG(dbg["wneg0"], WNEG[:, :])
                sc.wait("S", "dG")
                sc.wait("T", "dG")

            def _grq(t, t0=t0):
                t.matmul(PS_WA[:, 128:256], KF[:, t0:t0 + 128], QF[:, t0:t0 + 128],
                         start=True, stop=False)
                return t.matmul(PS_WA[:, 128:256], KF[:, PT + t0:PT + t0 + 128],
                                QF[:, PT + t0:PT + t0 + 128], start=False, stop=True)
            tk = sc.inc("T", "T", _grq)
            sc.wait("V", "T", tk)
            sc.wait("V", "S", nS_up)
            sc.inc("V", "V", lambda v: v.tensor_mul(SCR[:, 128:256], PS_WA[:, 128:256], MKU[:, :]))
            sc.wait("S", "V")
            sc.inc("S", "S", lambda sl, c=c: sl.activation(
                ATT[:, :], SCR[:, 128:256], AF.Copy, scale=COLS[:, CSK + c:CSK + c + 1]))
            nS_att = sc.n("S")
            if p == 0 and c == 0 and "att0" in dbg:
                sc.wait("G", "S", nS_att)
                dmaG(dbg["att0"], ATT[:, :])
                sc.wait("S", "dG")
                sc.wait("T", "dG")
            sc.wait("T", "S", nS_att)
            def _o(t, t0=t0):
                t.matmul(PS_WC[:, 256:512], QF[:, t0:t0 + 128], SB[:, 0:256],
                         start=True, stop=False)
                t.matmul(PS_WC[:, 256:512], QF[:, PT + t0:PT + t0 + 128], SB[:, 256:512],
                         start=False, stop=False)
                return t.matmul(PS_WC[:, 256:512], ATT[:, :], UP[:, :], start=False, stop=True)
            tk = sc.inc("T", "T", _o)
            sc.wait("S", "T", tk)
            sc.inc("S", "S", lambda sl, c=c: sl.activation(
                DOTM[:, c * 256:(c + 1) * 256], PS_WC[:, 256:512], AF.Copy,
                scale=COLS[:, CSQ + c:CSQ + c + 1]))
            nS_do = sc.n("S")

            def _ds(t):
                t.matmul(PS_S[:, 0:256], KNTM[:, 0:128], UP[:, :], start=True, stop=True)
                return t.matmul(PS_S[:, 256:512], KNTM[:, 128:256], UP[:, :],
                                start=True, stop=True)
            tk = sc.inc("T", "T", _ds)
            sc.wait("V", "T", tk)
            def _sacc(v):
                v.tensor_add(SFP[:, 0:256], SFP[:, 0:256], PS_S[:, 0:256])
                return v.tensor_add(SFP[:, 256:512], SFP[:, 256:512], PS_S[:, 256:512])
            sc.inc("V", "V", _sacc)
            sc.wait("S", "V")
            sc.inc("S", "S", lambda sl: sl.copy(SB[:, :], SFP[:, :]))
            sc.wait("T", "V")

            sc.wait("T", "S")
            def _trd(t, c=c):
                t.transpose(PS_TB[:, 128:256], DOTM[:, c * 256:c * 256 + 128], IDb[:, :])
                return t.transpose(PS_TB[:, 256:384], DOTM[:, c * 256 + 128:c * 256 + 256],
                                   IDb[:, :])
            tk = sc.inc("T", "T", _trd)
            sc.wait("S", "T", tk)
            def _dof(sl, t0=t0):
                sl.copy(DOF[:, t0:t0 + 128], PS_TB[:, 128:256])
                return sl.copy(DOF[:, PT + t0:PT + t0 + 128], PS_TB[:, 256:384])
            sc.inc("S", "S", _dof)

            for (BR, bi) in ((FSTM, 0), (FLTM, 1), (DOTM, 2), (VTM, 3)):
                sl_ = BR[:, c * 256:(c + 1) * 256]
                def _redm(v, sl_=sl_, bi=bi):
                    with nc.allow_low_precision("stat cols feed bf16 matmul operands"):
                        v.reduce_sum(SPK[:, 3 * bi:3 * bi + 1], sl_, axis=AX.X)
                        return v.reduce_max(SPK[:, 3 * bi + 2:3 * bi + 3], sl_, axis=AX.X)
                sc.inc("V", "V", _redm)
                sc.inc("S", "S", lambda sl, sl_=sl_, bi=bi: sl.activation(
                    SCR[:, 0:256], sl_, AF.Square, accum_out=SCF[:, 8 + bi:9 + bi]))
            sc.inc("S", "S", lambda sl: sl.copy(SCF[:, 15:16], CSTT[:, 0:1]))
            sc.wait("V", "S")
            def _rstat(v):
                for bi in range(3):
                    v.tensor_scalar(SCF[:, 12 + bi:13 + bi], SCF[:, 8 + bi:9 + bi],
                                    1.0 / 256.0, 1e-8, ALU.mult, ALU.max)
                return v.tensor_scalar(SCF[:, 15:16], SCF[:, 11:12],
                                       1.0 / 256.0, 1e-8, ALU.mult, ALU.max)
            sc.inc("V", "V", _rstat)
            sc.wait("S", "V")
            def _rsq(sl):
                for bi in range(3):
                    sl.activation(SPK[:, 3 * bi + 1:3 * bi + 2], SCF[:, 12 + bi:13 + bi], AF.Sqrt)
                return sl.activation(SPK[:, 10:11], SCF[:, 15:16], AF.Sqrt)
            sc.inc("S", "S", _rsq)
            nS_spk = sc.n("S")
            sc.wait("T", "S", nS_spk)
            sc.wait("T", "V")
            tk = sc.inc("T", "T", lambda t: t.transpose(PS_TB[0:16, 384:512], SPK[:, 0:16],
                                                        IDb[:, :]))
            sc.wait("S", "T", tk)
            sc.inc("S", "S", lambda sl, c=c: sl.copy(
                STATS[0:16, c * 128:(c + 1) * 128], PS_TB[0:16, 384:512]))
            sc.wait("T", "S")

        def _halos(v):
            for P in (PQ, PK, PV):
                v.tensor_copy(P[:, 0:3], P[:, 512:515])
                v.tensor_copy(P[:, 515:518], P[:, 1027:1030])
            v.tensor_copy(VF[:, 0:30], VF[:, 512:542])
            return v.tensor_copy(VF[:, 542:572], VF[:, 1054:1084])
        sc.inc("V", "V", _halos)

        nS_br = sc.n("S")
        sc.wait("T", "S", nS_br)
        for m in range(4):
            bank = PSB[m % 2]
            sc.wait("T", "S")
            def _gate(t, m=m, bank=bank):
                for k in range(8):
                    t.matmul(bank[:, 0:512], GW1H[:, k * GH + m * 128:k * GH + m * 128 + 128],
                             XLN[:, k * PT:(k + 1) * PT], start=(k == 0), stop=False)
                for bi in range(4):
                    for c2 in range(2):
                        if bi == 0:
                            rhs = FSF[:, c2 * PT:(c2 + 1) * PT]
                        elif bi == 1:
                            rhs = FLF[:, c2 * PT:(c2 + 1) * PT]
                        elif bi == 2:
                            rhs = DOF[:, c2 * PT:(c2 + 1) * PT]
                        else:
                            rhs = VF[:, c2 * 542 + 30:c2 * 542 + 30 + 512]
                        t.matmul(bank[:, 0:512],
                                 GW1B[:, (bi * 2 + c2) * GH + m * 128:(bi * 2 + c2) * GH + m * 128 + 128],
                                 rhs, start=False, stop=False)
                return t.matmul(bank[:, 0:512], STL[0:16, m * 128:(m + 1) * 128],
                                STATS[0:16, :], start=False, stop=True)
            tk = sc.inc("T", "T", _gate)
            sc.wait("S", "T", tk)
            sc.inc("S", "S", lambda sl, m=m, bank=bank: sl.activation(
                GSBt[:, m * PT:(m + 1) * PT], bank[:, 0:512], AF.Gelu, bias=GB1[:, m:m + 1]))
        nS_gate = sc.n("S")

        sc.wait("T", "S", nS_gate)
        for c in range(NCH):
            t0 = c * 128
            def _lg(t, t0=t0):
                for mg in range(3):
                    t.matmul(PS_WB[:, 256:260], GSBt[:, mg * PT + t0:mg * PT + t0 + 128],
                             GW2[:, mg * 4:mg * 4 + 4], start=(mg == 0), stop=False)
                return t.matmul(PS_WB[:, 256:260], GSBt[:, 3 * PT + t0:3 * PT + t0 + 128],
                                GW2[:, 12:16], start=False, stop=True)
            tk = sc.inc("T", "T", _lg)
            sc.wait("V", "T", tk)
            def _sm1(v):
                v.tensor_add(ETM[:, :], PS_WB[:, 256:260], GB2T[:, :])
                v.tensor_copy(SCF[:, 9:10], ZRO[:, 0:1])
                v.tensor_scalar(ETM[:, :], ETM[:, :], TIV[:, 0:1], None, ALU.mult)
                v.tensor_copy(SCF[:, 9:10], ZRO[:, 0:1])
                v.reduce_max(SCF[:, 5:6], ETM[:, :], axis=AX.X)
                v.tensor_copy(SCF[:, 10:11], ZRO[:, 0:1])
                return v.tensor_scalar(SCF[:, 5:6], SCF[:, 5:6], -1.0, None, ALU.mult)
            sc.inc("V", "V", _sm1)
            sc.wait("T", "V")
            sc.wait("S", "V")
            sc.inc("S", "S", lambda sl: sl.activation(ETM[:, :], ETM[:, :], AF.Exp,
                                                      bias=SCF[:, 5:6]))
            sc.wait("V", "S")
            def _sm2(v, c=c):
                v.reduce_sum(SCF[:, 6:7], ETM[:, :], axis=AX.X)
                v.tensor_copy(SCF[:, 9:10], ZRO[:, 0:1])
                v.reciprocal(SCF[:, 6:7], SCF[:, 6:7])
                v.tensor_copy(SCF[:, 9:10], FLT[:, 0:1])
                v.tensor_scalar(WGt[:, :], ETM[:, :], SCF[:, 6:7], None, ALU.mult)
                v.tensor_copy(SCF[:, 9:10], ZRO[:, 0:1])
                v.tensor_max(WGt[:, :], WGt[:, :], FLT[:, :])
                v.tensor_copy(SCF[:, 9:10], ZRO[:, 0:1])
                v.reduce_sum(SCF[:, 6:7], WGt[:, :], axis=AX.X)
                v.tensor_copy(SCF[:, 9:10], ZRO[:, 0:1])
                v.reciprocal(SCF[:, 6:7], SCF[:, 6:7])
                v.tensor_copy(SCF[:, 9:10], FLT[:, 0:1])
                v.tensor_scalar(WGt[:, :], WGt[:, :], SCF[:, 6:7], None, ALU.mult)
                v.tensor_copy(SCF[:, 9:10], ZRO[:, 0:1])
                v.tensor_scalar(OT1[:, :], FSTM[:, c * 256:(c + 1) * 256], WGt[:, 0:1],
                                None, ALU.mult)
                v.tensor_copy(SCF[:, 9:10], ZRO[:, 0:1])
                v.scalar_tensor_tensor(OT2[:, :], FLTM[:, c * 256:(c + 1) * 256], WGt[:, 1:2],
                                       OT1[:, :], ALU.mult, ALU.add)
                v.tensor_copy(SCF[:, 9:10], ZRO[:, 0:1])
                v.scalar_tensor_tensor(OT1[:, :], DOTM[:, c * 256:(c + 1) * 256], WGt[:, 2:3],
                                       OT2[:, :], ALU.mult, ALU.add)
                v.tensor_copy(SCF[:, 9:10], ZRO[:, 0:1])
                return v.scalar_tensor_tensor(OT2[:, :], VTM[:, c * 256:(c + 1) * 256],
                                              WGt[:, 3:4], OT1[:, :], ALU.mult, ALU.add)
            sc.inc("V", "V", _sm2)
            sc.wait("S", "V")
            def _rmsacc(sl):
                sl.activation(SCR[:, 0:256], OT2[:, :], AF.Square, accum_out=SCF[:, 7:8])
                return sl.copy(SCF[:, 15:16], CSTT[:, 0:1])
            sc.inc("S", "S", _rmsacc)
            sc.wait("V", "S")
            sc.inc("V", "V", lambda v: v.tensor_scalar(SCF[:, 7:8], SCF[:, 7:8],
                                                       1.0 / 256.0, 1e-5, ALU.mult, ALU.add))
            sc.wait("S", "V")
            sc.inc("S", "S", lambda sl: sl.sqrt(SCF[:, 7:8], SCF[:, 7:8]))
            sc.wait("V", "S")
            sc.inc("V", "V", lambda v: v.reciprocal(SCF[:, 7:8], SCF[:, 7:8]))
            sc.wait("S", "V")
            sc.inc("S", "S", lambda sl: sl.activation(OT1[:, :], OT2[:, :], AF.Copy,
                                                      scale=SCF[:, 7:8]))
            sc.wait("V", "S")
            sc.inc("V", "V", lambda v: v.tensor_mul(OTB[:, :], OT1[:, :], ONT[:, :]))
            nV_o = sc.n("V")
            sc.wait("T", "V", nV_o)
            sc.wait("T", "S")
            def _tro(t):
                t.transpose(PS_TB[:, 512:640], OTB[:, 0:128], IDb[:, :])
                return t.transpose(PS_TB[:, 640:768], OTB[:, 128:256], IDb[:, :])
            tk = sc.inc("T", "T", _tro)
            sc.wait("S", "T", tk)
            def _opf(sl, t0=t0):
                sl.copy(OPF[:, t0:t0 + 128], PS_TB[:, 512:640])
                return sl.copy(OPF[:, PT + t0:PT + t0 + 128], PS_TB[:, 640:768])
            sc.inc("S", "S", _opf)
            sc.wait("T", "S")

        nS_opf = sc.n("S")
        sc.wait("T", "S", nS_opf)
        for tt in range(4):
            t0 = tt * 128
            for n_ in range(2):
                bank = PSB[n_]
                sc.wait("T", "S")
                def _wo(t, t0=t0, n_=n_, bank=bank):
                    t.matmul(bank[:, 0:512], OPF[:, t0:t0 + 128],
                             WOs[:, n_ * 512:n_ * 512 + 512], start=True, stop=False)
                    return t.matmul(bank[:, 0:512], OPF[:, PT + t0:PT + t0 + 128],
                                    WOs[:, 1024 + n_ * 512:1024 + n_ * 512 + 512],
                                    start=False, stop=True)
                tk = sc.inc("T", "T", _wo)
                sc.wait("S", "T", tk)
                sc.inc("S", "S", lambda sl, tt=tt, n_=n_, bank=bank: sl.copy(
                    OPART[:, tt * D + n_ * 512:tt * D + n_ * 512 + 512], bank[:, 0:512]))
        nS_op = sc.n("S")
        sc.wait("Y", "S", nS_op)
        def _ccw(y, p=p):
            return y.dma_start(
                bass.AP(cc_in, p * PT * D, [[D, 128], [128 * D, 4], [1, D]]),
                bass.AP(OPART, 0, [[4 * D, 128], [D, 4], [1, D]]))
        sc.inc("Y", "dY", _ccw, 16)

    if dbg:
        sc.wait("G", "S")
        sc.wait("G", "V")
        for name, ap_src in (("qf", QF), ("kf", KF), ("vf", VF), ("xln", XLN),
                             ("fsf", FSF), ("flf", FLF), ("dof", DOF), ("gsb", GSBt),
                             ("opf", OPF), ("opart", OPART), ("kbf", KBF),
                             ("vtm", VTM), ("dotm", DOTM), ("spk", SPK), ("wgt", WGt)):
            if name in dbg:
                dmaG(dbg[name], ap_src[:, :])
        if "beta" in dbg:
            dmaG(dbg["beta"], RRF[0:1, 0:512])
        if "xt" in dbg:
            dmaG(dbg["xt"], XT[(npass - 1) % 2][:, :])
        if "rows" in dbg:
            dmaG(dbg["rows"], RRF[:, :])
        if "stats" in dbg:
            dmaG(dbg["stats"], STATS[0:16, :])
        if "sb" in dbg:
            dmaG(dbg["sb"], SB[:, :])
        sc.wait("Y", "dG")

    n_cc = sc.n("dY")
    sc.wait("G", "dY", n_cc)
    sc.wait("G", "dG")
    def _rs(g):
        if split:
            return g.collective_compute(
                "ReduceScatter", ALU.add,
                replica_groups=[[0, 1, 2, 3], [4, 5, 6, 7]],
                ins=[cc_in.ap()[L // 2:L, :].opt()],
                outs=[cc_rs.ap()[TOK // 2:TOK, :].opt()])
        return g.collective_compute(
            "ReduceScatter", ALU.add,
            replica_groups=[[0, 1, 2, 3], [4, 5, 6, 7]],
            ins=[cc_in.ap().opt()], outs=[cc_rs.ap().opt()])
    sc.inc("G", "CC", _rs)
    p1.close()

    if not do_ffn:
        sc.wait("Y", "CC")
        AOT = E(nc.sbuf_tensor("aot", [128, D], F32))
        AOF = E(nc.sbuf_tensor("aof", [128, D], F32))
        for tt in range(TOK // 128):
            dmaY(AOT[:, :], cc_rs.ap()[tt * 128:(tt + 1) * 128, :])
            sc.wait("V", "dY")
            sc.inc("V", "V", lambda v: v.tensor_copy(AOF[:, :], AOT[:, :]))
            sc.wait("Y", "V")
            dmaY(out[tt * 128:(tt + 1) * 128, :], AOF[:, :])
        sc.wait("Y", "dY")
        nc_final(nc, sc)
        st.close()
        return nc, dbg

    p2 = ExitStack()
    E2 = p2.enter_context
    XRES = E2(nc.sbuf_tensor("xres", [128, D], F32))
    AOT = E2(nc.sbuf_tensor("aot", [128, D], F32))
    X2 = E2(nc.sbuf_tensor("x2b", [128, D], F32))
    X2LNT = E2(nc.sbuf_tensor("x2lnt", [128, D], BF16))
    X2LN = E2(nc.sbuf_tensor("x2ln", [128, 8 * 1024], BF16))
    SCF2 = E2(nc.sbuf_tensor("scf2", [128, 16], F32))
    S1C = E2(nc.sbuf_tensor("s1cb", [128, 32], F32))
    S2C = E2(nc.sbuf_tensor("s2cb", [128, 32], F32))
    SH1C = E2(nc.sbuf_tensor("sh1cb", [128, 32], F32))
    SH2C = E2(nc.sbuf_tensor("sh2cb", [128, 32], F32))
    S3C = E2(nc.sbuf_tensor("s3cb", [128, 8], F32))
    HT = E2(nc.sbuf_tensor("ht", [128, 32 * 1024], BF16))
    G1 = E2(nc.sbuf_tensor("g1", [128, 32 * 1024], BF16))
    RING = [E2(nc.sbuf_tensor(f"ring{i}", [128, 4096], BF16)) for i in range(3)]
    OASM = E2(nc.sbuf_tensor("oasm", [128, 8 * 1024], BF16))
    TM1 = E2(nc.sbuf_tensor("tm1", [128, 512], BF16))
    TM2 = E2(nc.sbuf_tensor("tm2", [128, 512], BF16))
    OUTT = E2(nc.sbuf_tensor("outt", [128, 1024], F32))

    dmaY(S1C[:, :], s1c)
    dmaY(S2C[:, :], s2c)
    dmaY(SH1C[:, :], sh1c)
    dmaY(SH2C[:, :], sh2c)
    dmaY(S3C[:, :], s3col)
    n_scl = sc.n("dY")

    sc.wait("G", "CC")
    ring_free = [0, 0, 0]

    for hh in range(2):
        tok0 = hh * 1024
        sc.wait("S", "dY", n_scl)
        cc_need = 1 if (split and hh == 0) else None
        sc.wait("V", "CC", cc_need)
        sc.wait("Y", "CC", cc_need)
        for tt in range(8):
            r0 = tok0 + tt * 128
            nlx = dmaY(XRES[:, :], x_res[r0:r0 + 128, :])
            nla = dmaY(AOT[:, :], cc_rs.ap()[r0:r0 + 128, :])
            sc.wait("V", "dY")
            sc.wait("V", "S")
            sc.inc("V", "V", lambda v: v.tensor_add(X2[:, :], XRES[:, :], AOT[:, :]))
            nV_x2 = sc.n("V")
            sc.wait("Y", "V", nV_x2)
            dmaY(x2stash[r0:r0 + 128, :], X2[:, :])
            sc.wait("S", "V", nV_x2)
            def _ln2acc(sl):
                sl.activation(OASM[:, 0:1024], X2[:, :], AF.Copy, accum_out=SCF2[:, 0:1])
                sl.activation(OASM[:, 0:1024], X2[:, :], AF.Square, accum_out=SCF2[:, 1:2])
                return sl.copy(SCF2[:, 15:16], CSTT[:, 0:1])
            sc.inc("S", "S", _ln2acc)
            sc.wait("V", "S")
            def _ln2c(v):
                v.tensor_scalar(SCF2[:, 2:3], SCF2[:, 0:1], 1.0 / D, None, ALU.mult)
                v.tensor_scalar(SCF2[:, 3:4], SCF2[:, 1:2], 1.0 / D, None, ALU.mult)
                v.tensor_copy(SCF2[:, 9:10], ZRO[:, 0:1])
                v.tensor_mul(SCF2[:, 4:5], SCF2[:, 2:3], SCF2[:, 2:3])
                v.tensor_copy(SCF2[:, 9:10], ZRO[:, 0:1])
                v.tensor_sub(SCF2[:, 3:4], SCF2[:, 3:4], SCF2[:, 4:5])
                v.tensor_copy(SCF2[:, 9:10], ZRO[:, 0:1])
                return v.tensor_scalar(SCF2[:, 3:4], SCF2[:, 3:4], 1e-5, None, ALU.add)
            sc.inc("V", "V", _ln2c)
            sc.wait("S", "V")
            sc.inc("S", "S", lambda sl: sl.sqrt(SCF2[:, 3:4], SCF2[:, 3:4]))
            sc.wait("V", "S")
            def _ln2i(v):
                v.reciprocal(SCF2[:, 3:4], SCF2[:, 3:4])
                v.tensor_copy(SCF2[:, 5:6], SCF2[:, 2:3])
                v.tensor_mul(SCF2[:, 4:5], SCF2[:, 2:3], SCF2[:, 3:4])
                v.tensor_copy(SCF2[:, 9:10], ZRO[:, 0:1])
                return v.tensor_scalar(SCF2[:, 4:5], SCF2[:, 4:5], -1.0, None, ALU.mult)
            sc.inc("V", "V", _ln2i)
            sc.wait("S", "V")
            sc.inc("S", "S", lambda sl: sl.activation(X2LNT[:, :], X2[:, :], AF.Identity,
                                                      bias=SCF2[:, 4:5], scale=SCF2[:, 3:4]))
            nS_l2 = sc.n("S")
            sc.wait("T", "S", nS_l2)
            def _trx2(t):
                for k in range(7):
                    t.transpose(PS_TA[:, k * 128:(k + 1) * 128],
                                X2LNT[:, k * 128:(k + 1) * 128], IDb[:, :])
                return t.transpose(PS_TA[:, 896:1024], X2LNT[:, 896:1024], IDb[:, :])
            tk = sc.inc("T", "T", _trx2)
            sc.wait("S", "T", tk)
            sc.inc("S", "S", lambda sl, tt=tt: sl.copy(
                bass.AP(X2LN, tt * 128, [[8 * 1024, 128], [1024, 8], [1, 128]]),
                bass.AP(PS_TA, 0, [[1024, 128], [128, 8], [1, 128]])))
            sc.wait("T", "S")

        nS_x2ln = sc.n("S")
        sc.wait("T", "S", nS_x2ln)
        if do_ffn == "x":
            continue

        for mg in range(8):
            sa, sb = (2 * mg) % 3, (2 * mg + 1) % 3
            sc.wait("G", "T", ring_free[sa])
            n1 = dmaG(RING[sa][:, :], w1t[mg])
            sc.wait("G", "T", ring_free[sb])
            n2 = dmaG(RING[sb][:, :], w2t[mg])
            sc.wait("T", "dG", n2)
            for m in range(4):
                gm = mg * 4 + m
                for t2 in range(2):
                    b1 = PSB[0] if t2 == 0 else PSB[1]
                    b2 = PS_WA if t2 == 0 else PS_WB
                    sc.wait("T", "S")
                    def _a(t, sa=sa, sb=sb, m=m, t2=t2, b1=b1, b2=b2):
                        for k in range(8):
                            t.matmul(b1[:, 0:512],
                                     RING[sa][:, k * 512 + m * 128:k * 512 + m * 128 + 128],
                                     X2LN[:, k * 1024 + t2 * 512:k * 1024 + t2 * 512 + 512],
                                     start=(k == 0), stop=(k == 7))
                        for k in range(7):
                            t.matmul(b2[:, 0:512],
                                     RING[sb][:, k * 512 + m * 128:k * 512 + m * 128 + 128],
                                     X2LN[:, k * 1024 + t2 * 512:k * 1024 + t2 * 512 + 512],
                                     start=(k == 0), stop=False)
                        return t.matmul(b2[:, 0:512],
                                        RING[sb][:, 7 * 512 + m * 128:7 * 512 + m * 128 + 128],
                                        X2LN[:, 7 * 1024 + t2 * 512:7 * 1024 + t2 * 512 + 512],
                                        start=False, stop=True)
                    tk = sc.inc("T", "T", _a)
                    sc.wait("S", "T", tk)
                    def _ha(sl, gm=gm, b1=b1, b2=b2):
                        sl.activation(TM1[:, :], b1[:, 0:512], AF.Silu, scale=S1C[:, gm:gm + 1])
                        return sl.activation(TM2[:, :], b2[:, 0:512], AF.Copy,
                                             scale=S2C[:, gm:gm + 1])
                    sc.inc("S", "S", _ha)
                    sc.wait("V", "S")
                    sc.inc("V", "V", lambda v, gm=gm, t2=t2: v.tensor_mul(
                        HT[:, gm * 1024 + t2 * 512:gm * 1024 + t2 * 512 + 512],
                        TM1[:, :], TM2[:, :]))
                    sc.wait("S", "V")
            ring_free[sa] = sc.n("T")
            ring_free[sb] = sc.n("T")

        nV_h = sc.n("V")
        sc.wait("T", "V", nV_h)
        if do_ffn == "a":
            continue

        def big_stage(wsrc, SRC, DST, SCL, act):
            banks = [PSB[0], PSB[1], PS_WA, PS_WB]
            for mg in range(16):
                sa, sb = (2 * mg) % 3, (2 * mg + 1) % 3
                sc.wait("G", "T", ring_free[sa])
                n1 = dmaG(RING[sa][:, :], wsrc[2 * mg])
                sc.wait("G", "T", ring_free[sb])
                n2 = dmaG(RING[sb][:, :], wsrc[2 * mg + 1])
                sc.wait("T", "dG", n2)
                sc.wait("T", "S")
                import os
                if os.environ.get("BSTAGE") == "dma":
                    continue
                def _grp(t, sa=sa, sb=sb):
                    last = None
                    for gi, (m, t2) in enumerate(((0, 0), (0, 1), (1, 0), (1, 1))):
                        for k in range(32):
                            slot, kk = (sa, k) if k < 16 else (sb, k - 16)
                            last = t.matmul(banks[gi][:, 0:512],
                                     RING[slot][:, kk * 256 + m * 128:kk * 256 + m * 128 + 128],
                                     SRC[:, k * 1024 + t2 * 512:k * 1024 + t2 * 512 + 512],
                                     start=(k == 0), stop=(k == 31))
                    return last
                tk2 = sc.inc("T", "T", _grp)
                ring_free[sa] = sc.n("T")
                ring_free[sb] = sc.n("T")
                if os.environ.get("BSTAGE") == "mm":
                    continue
                sc.wait("S", "T", tk2)
                def _ev(sl, mg=mg):
                    mode = os.environ.get("BSTAGE", "")
                    for gi, (m, t2) in enumerate(((0, 0), (0, 1), (1, 0), (1, 1))):
                        gm = mg * 2 + m
                        dst = DST[:, gm * 1024 + t2 * 512:gm * 1024 + t2 * 512 + 512]
                        if mode == "evcopy":
                            sl.copy(dst, banks[gi][:, 0:512])
                        elif mode == "evtm":
                            sl.copy(TM1[:, :], banks[gi][:, 0:512])
                        elif mode == "evsb":
                            sl.copy(dst, ZRO[:, 0:512])
                        elif mode == "evns":
                            sl.activation(dst, banks[gi][:, 0:512], act)
                        else:
                            sl.activation(dst, banks[gi][:, 0:512], act, scale=SCL[:, gm:gm + 1])
                    return sl.nop()
                sc.inc("S", "S", _ev)

        big_stage(h1t, HT, G1, SH1C, AF.Gelu)

        nS_g1 = sc.n("S")
        sc.wait("T", "S", nS_g1)
        if do_ffn == "b":
            continue
        big_stage(h2t, G1, HT, SH2C, AF.Gelu)

        nS_g2 = sc.n("S")
        sc.wait("T", "S", nS_g2)
        for m in range(8):
            slot = m % 3
            sc.wait("G", "T", ring_free[slot])
            nw = dmaG(RING[slot][:, :], w3t[m])
            sc.wait("T", "dG", nw)
            sc.wait("T", "S")
            def _d(t, slot=slot):
                last = None
                for n_ in range(2):
                    for k in range(32):
                        last = t.matmul(PSB[n_][:, 0:512],
                                 RING[slot][:, k * 128:(k + 1) * 128],
                                 HT[:, k * 1024 + n_ * 512:k * 1024 + n_ * 512 + 512],
                                 start=(k == 0), stop=(k == 31))
                return last
            tk = sc.inc("T", "T", _d)
            ring_free[slot] = sc.n("T")
            sc.wait("S", "T", tk)
            def _dev(sl, m=m):
                sl.activation(OASM[:, m * 1024:m * 1024 + 512], PSB[0][:, 0:512], AF.Copy,
                              scale=S3C[:, m:m + 1])
                return sl.activation(OASM[:, m * 1024 + 512:(m + 1) * 1024], PSB[1][:, 0:512],
                                     AF.Copy, scale=S3C[:, m:m + 1])
            sc.inc("S", "S", _dev)
        nS_ffo = sc.n("S")
        sc.wait("T", "S", nS_ffo)

        for tt in range(8):
            r0 = tok0 + tt * 128
            nrx = dmaY(XRES[:, :], x2stash[r0:r0 + 128, :])
            sc.wait("T", "V")
            def _trf(t, tt=tt):
                for m in range(7):
                    t.transpose(PS_TA[:, m * 128:(m + 1) * 128],
                                OASM[:, m * 1024 + tt * 128:m * 1024 + tt * 128 + 128], IDb[:, :])
                return t.transpose(PS_TA[:, 896:1024],
                                   OASM[:, 7 * 1024 + tt * 128:7 * 1024 + tt * 128 + 128],
                                   IDb[:, :])
            tk = sc.inc("T", "T", _trf)
            sc.wait("V", "T", tk)
            sc.wait("V", "dY")
            def _fadd(v):
                v.tensor_add(OUTT[:, 0:512], XRES[:, 0:512], PS_TA[:, 0:512])
                return v.tensor_add(OUTT[:, 512:1024], XRES[:, 512:1024], PS_TA[:, 512:1024])
            sc.inc("V", "V", _fadd)
            nV_out = sc.n("V")
            sc.wait("Y", "V", nV_out)
            dmaY(out[r0:r0 + 128, :], OUTT[:, :])
            sc.wait("T", "V", nV_out)

    if do_ffn is not True:
        sc.wait("V", "CC")
        sc.inc("V", "V", lambda v: v.memset(OUTT[:, :], 0.0))
        sc.wait("Y", "V")
        for tt in range(TOK // 128):
            dmaY(out[tt * 128:(tt + 1) * 128, :], OUTT[:, :])
    sc.wait("Y", "dY")
    nc_final(nc, sc)
    p2.close()
    st.close()
    return nc, dbg


def nc_final(nc, sc):
    with nc.Block() as block:
        def mk(key):
            def body(eng):
                for k, fn in sc.items:
                    if k == key:
                        fn(eng)
            return body
        block.tensor(mk("T"))
        block.vector(mk("V"))
        block.scalar(mk("S"))
        block.gpsimd(mk("G"))
        block.sync(mk("Y"))


_CACHE = {}


def kernel(**inputs):
    if "prog" not in _CACHE:
        _CACHE["prog"] = build_program()
    nc, _ = _CACHE["prog"]
    in_maps = [prepare_core_inputs(inputs, c) for c in range(8)]
    res = run_bass_kernel_spmd(nc, in_maps, core_ids=list(range(8)))
    outp = np.zeros((B, L, D), np.float32)
    for c in range(8):
        b, win = c // 4, c % 4
        o = res.results[c]["out"]
        outp[b, win * 1024:win * 1024 + 1024] = o[0:1024]
        outp[b, 4096 + win * 1024:4096 + win * 1024 + 1024] = o[1024:2048]
    return outp


# version 07
# speedup vs baseline: 1.0532x; 1.0236x over previous
import numpy as np
import ml_dtypes
from contextlib import ExitStack

import concourse.bass as bass
import concourse.mybir as mybir
from concourse.bass_utils import run_bass_kernel_spmd

F32 = mybir.dt.float32
BF16 = mybir.dt.bfloat16
AF = mybir.ActivationFunctionType
ALU = mybir.AluOpType
AX = mybir.AxisListType

B, L, D = 2, 8192, 1024
CH, PT = 128, 512
NCH = PT // CH
TOK = 2048
GH = 512


class Sched:
    KEYS = ("T", "V", "S", "G", "Y")

    def __init__(self):
        self.items = []
        self.counts = {}
        self.h = {}

    def inc(self, key, sem, fn, n=1):
        self.counts[sem] = self.counts.get(sem, 0) + n

        def g(eng, fn=fn, sem=sem, n=n):
            fn(eng).then_inc(self.h[sem], n)
        self.items.append((key, g))
        return self.counts[sem]

    def run(self, key, fn):
        self.items.append((key, fn))

    def wait(self, key, sem, val=None):
        if val is None:
            val = self.counts.get(sem, 0)
        assert 0 <= val <= self.counts.get(sem, 0), (sem, val, self.counts.get(sem, 0))

        def g(eng, sem=sem, val=val):
            eng.wait_ge(self.h[sem], val)
        self.items.append((key, g))

    def n(self, sem):
        return self.counts.get(sem, 0)


def _lhsT(w):
    K, M = w.shape
    return np.ascontiguousarray(w.reshape(K // 128, 128, M).transpose(1, 0, 2).reshape(128, -1))


def prepare_core_inputs(A, core):
    b, h = core // 4, core % 4
    win = core % 4
    hs = slice(h * 256, (h + 1) * 256)
    d = {}
    d["x_attn"] = A["L0_attn_x"][b]
    xb = A["L0_attn_x"][b]
    d["x_res"] = np.concatenate([xb[win * 1024:win * 1024 + 1024],
                                 xb[4096 + win * 1024:4096 + win * 1024 + 1024]])
    d["wq"] = _lhsT(A["L0_attn_Wq"][:, hs])
    d["wk"] = _lhsT(A["L0_attn_Wk"][:, hs])
    d["wv"] = _lhsT(A["L0_attn_Wv"][:, hs])
    d["wb"] = _lhsT(A["L0_attn_Wb"][:, h:h + 1])
    cw = [A["L0_attn_" + nm][hs].reshape(2, 128, 4) for nm in ("qc_w", "kc_w", "vc_w")]
    d["convw"] = np.stack(cw, 0).reshape(6, 128, 4).transpose(1, 0, 2).reshape(128, 24)
    fs = A["L0_attn_fir_s"][h].reshape(2, 128, 3).transpose(1, 0, 2).reshape(128, 6)
    fl = A["L0_attn_fir_l"][h].reshape(2, 128, 31).transpose(1, 0, 2).reshape(128, 62)
    d["firw"] = np.concatenate([fs, fl], 1)
    g = A["L0_attn_gW1"]
    d["gw1h"] = _lhsT(g[0:1024])
    d["gw1b"] = g[4096:5120].reshape(8, 128, GH).transpose(1, 0, 2).reshape(128, 8 * GH)
    d["gw1s"] = g[1024:4096].reshape(24, 128, GH).transpose(1, 0, 2).reshape(128, 24 * GH)
    d["gb1"] = A["L0_attn_gb1"].reshape(4, 128).T
    d["gw2"] = A["L0_attn_gW2"].reshape(4, 128, 4).transpose(1, 0, 2).reshape(128, 16)
    d["gb2"] = A["L0_attn_gb2"].reshape(1, 4)
    d["temp"] = A["L0_attn_temp"][h].reshape(1, 1)
    d["floor"] = A["L0_attn_eps_floor"][h].reshape(1, 4)
    d["onorm"] = A["L0_attn_onorm_w"].reshape(1, 256)
    d["wo"] = A["L0_attn_Wo"][hs].reshape(2, 128, 1024).transpose(1, 0, 2).reshape(128, 2048)
    w1t, w2t = A["L0_ffn_w1_w"].T, A["L0_ffn_w2_w"].T
    h1t, h2t, w3t = A["L0_ffn_h1_w"].T, A["L0_ffn_h2_w"].T, A["L0_ffn_w3_w"].T
    d["w1t"] = w1t.reshape(8, 128, 8, 512).transpose(2, 1, 0, 3).reshape(8, 128, 4096)
    d["w2t"] = w2t.reshape(8, 128, 8, 512).transpose(2, 1, 0, 3).reshape(8, 128, 4096)
    d["h1t"] = (h1t.reshape(32, 128, 16, 256).transpose(2, 1, 0, 3).reshape(16, 128, 2, 4096)
                .transpose(0, 2, 1, 3).reshape(32, 128, 4096))
    d["h2t"] = (h2t.reshape(32, 128, 16, 256).transpose(2, 1, 0, 3).reshape(16, 128, 2, 4096)
                .transpose(0, 2, 1, 3).reshape(32, 128, 4096))
    d["w3t"] = w3t.reshape(32, 128, 8, 128).transpose(2, 1, 0, 3).reshape(8, 128, 4096)
    d["s1"] = A["L0_ffn_w1_s"].reshape(32, 128).T
    d["s2"] = A["L0_ffn_w2_s"].reshape(32, 128).T
    d["sh1"] = A["L0_ffn_h1_s"].reshape(32, 128).T
    d["sh2"] = A["L0_ffn_h2_s"].reshape(32, 128).T
    d["s3"] = A["L0_ffn_w3_s"].reshape(8, 128).T
    return {k: np.ascontiguousarray(v, dtype=np.float32) for k, v in d.items()}


def build_program(npass=L // PT, do_ffn=True, debug=()):
    nc = bass.Bass("TRN2", target_bir_lowering=False, detect_race_conditions=False)
    sc = Sched()
    split = (npass == L // PT)

    def din(name, shape, dt=F32):
        return nc.dram_tensor(name, list(shape), dt, kind="ExternalInput").ap()

    x_attn = din("x_attn", (L, D))
    x_res = din("x_res", (TOK, D))
    wq, wk, wv = din("wq", (128, 2048)), din("wk", (128, 2048)), din("wv", (128, 2048))
    wb = din("wb", (128, 8))
    convw, firw = din("convw", (128, 24)), din("firw", (128, 68))
    gw1h, gw1b = din("gw1h", (128, 8 * GH)), din("gw1b", (128, 8 * GH))
    gw1s = din("gw1s", (128, 24 * GH))
    gb1, gw2, gb2 = din("gb1", (128, 4)), din("gw2", (128, 16)), din("gb2", (1, 4))
    temp, floor, onorm = din("temp", (1, 1)), din("floor", (1, 4)), din("onorm", (1, 256))
    wo = din("wo", (128, 2048))
    w1t, w2t = din("w1t", (8, 128, 4096)), din("w2t", (8, 128, 4096))
    h1t, h2t = din("h1t", (32, 128, 4096)), din("h2t", (32, 128, 4096))
    w3t = din("w3t", (8, 128, 4096))
    s1c, s2c = din("s1", (128, 32)), din("s2", (128, 32))
    sh1c, sh2c = din("sh1", (128, 32)), din("sh2", (128, 32))
    s3col = din("s3", (128, 8))

    out = nc.dram_tensor("out", [TOK, D], F32, kind="ExternalOutput").ap()
    dbg = {}
    for name, shape in debug:
        dbg[name] = nc.dram_tensor("dbg_" + name, list(shape), F32, kind="ExternalOutput").ap()

    cc_in = nc.dram_tensor("cc_in", [L, D], F32, kind="Internal")
    cc_rs = nc.dram_tensor("cc_rs", [TOK, D], F32, kind="Internal")
    x2stash = nc.dram_tensor("x2stash", [TOK, D], F32, kind="Internal").ap()

    st = ExitStack()
    E = st.enter_context
    for s in ("dG", "dY", "T", "V", "S", "G", "CC"):
        sc.h[s] = E(nc.semaphore("sm" + s))

    PS_S = E(nc.psum_tensor("ps_s", [128, 512], F32))
    PS_TA = E(nc.psum_tensor("ps_ta", [128, 1024], BF16))
    PS_TB = E(nc.psum_tensor("ps_tb", [128, 1024], BF16))
    PS_WA = E(nc.psum_tensor("ps_wa", [128, 512], F32))
    PS_WB = E(nc.psum_tensor("ps_wb", [128, 512], F32))
    PS_WC = E(nc.psum_tensor("ps_wc", [128, 512], F32))
    PS_B0 = E(nc.psum_tensor("ps_b0", [128, 512], F32))
    PS_B1 = E(nc.psum_tensor("ps_b1", [128, 512], F32))
    PSB = [PS_B0, PS_B1]

    IDb = E(nc.sbuf_tensor("idb", [128, 128], BF16))
    IDF = E(nc.sbuf_tensor("idf", [128, 128], F32))
    ONE2 = E(nc.sbuf_tensor("one2", [2, 128], BF16))
    CSTT = E(nc.sbuf_tensor("cstt", [128, 4], F32))
    ZRO = E(nc.sbuf_tensor("zro", [128, 512], BF16))
    nc.const_aps.register(F32, 1e-6, CSTT[:, 0:1])
    p1 = ExitStack()
    E1 = p1.enter_context
    MKU = E1(nc.sbuf_tensor("mku", [128, 128], BF16))
    MKUS = E1(nc.sbuf_tensor("mkus", [128, 128], BF16))
    ONESC = E1(nc.sbuf_tensor("onesc", [128, 1], BF16))
    WQ = E1(nc.sbuf_tensor("wqb", [128, 2048], BF16))
    WK = E1(nc.sbuf_tensor("wkb", [128, 2048], BF16))
    WV = E1(nc.sbuf_tensor("wvb", [128, 2048], BF16))
    WBc = E1(nc.sbuf_tensor("wbb", [128, 8], BF16))
    WOs = E1(nc.sbuf_tensor("wos", [128, 2048], BF16))
    CDIAG = E1(nc.sbuf_tensor("cdiag", [128, 24 * 128], BF16))
    FDIAG = E1(nc.sbuf_tensor("fdiag", [128, 68 * 128], BF16))
    CVW = E1(nc.sbuf_tensor("cvw", [128, 24], F32))
    FIW = E1(nc.sbuf_tensor("fiw", [128, 68], F32))
    GW1H = E1(nc.sbuf_tensor("gw1hb", [128, 8 * GH], BF16))
    GW1B = E1(nc.sbuf_tensor("gw1bb", [128, 8 * GH], BF16))
    GW2 = E1(nc.sbuf_tensor("gw2b", [128, 16], BF16))
    GB1 = E1(nc.sbuf_tensor("gb1b", [128, 4], F32))
    STL = E1(nc.sbuf_tensor("stl", [128, GH], BF16))
    GB2T = E1(nc.sbuf_tensor("gb2t", [128, 4], F32))
    FLT = E1(nc.sbuf_tensor("flt", [128, 4], F32))
    ONT = E1(nc.sbuf_tensor("ont", [128, 256], BF16))
    TIV = E1(nc.sbuf_tensor("tiv", [128, 1], F32))
    RR2 = E1(nc.sbuf_tensor("rr2", [2, 512], BF16))

    def _prev(v):
        v.memset(IDb[:, :], 0.0)
        v.memset(IDF[:, :], 0.0)
        v.memset(MKU[:, :], 0.0)
        v.memset(MKUS[:, :], 0.0)
        v.memset(ONE2[:, :], 0.0)
        v.memset(ONE2[0:1, :], 1.0)
        v.memset(RR2[:, :], 0.0)
        v.memset(ONT[:, :], 0.0)
        v.memset(CSTT[:, :], 0.0)
        v.memset(ZRO[:, :], 0.0)
        v.memset(CSTT[:, 0:1], 1e-6)
        return v.memset(ONESC[:, :], 1.0)
    sc.inc("V", "V", _prev)
    sc.wait("G", "V")
    def _pre(g):
        g.affine_select(out=IDb[:, :], in_=IDb[:, :], compare_op=ALU.not_equal,
                        fill=1.0, base=0, pattern=[[-1, 128]], channel_multiplier=1)
        g.affine_select(out=IDF[:, :], in_=IDF[:, :], compare_op=ALU.not_equal,
                        fill=1.0, base=0, pattern=[[-1, 128]], channel_multiplier=1)
        g.affine_select(out=MKU[:, :], in_=MKU[:, :], compare_op=ALU.is_gt,
                        fill=1.0, base=0, pattern=[[-1, 128]], channel_multiplier=1)
        return g.affine_select(out=MKUS[:, :], in_=MKUS[:, :], compare_op=ALU.is_ge,
                               fill=1.0, base=0, pattern=[[-1, 128]], channel_multiplier=1)
    sc.inc("G", "G", _pre)
    nG_pre = sc.n("G")

    def dmaG(dst, src):
        return sc.inc("G", "dG", lambda g: g.dma_start(dst, src), 16)

    def dmaY(dst, src, **kw):
        return sc.inc("Y", "dY", lambda y: y.dma_start(dst, src, **kw), 16)

    for dst, src in ((WQ, wq), (WK, wk), (WV, wv), (WBc, wb), (CVW, convw), (FIW, firw),
                     (GW1H, gw1h), (GW1B, gw1b), (GW2, gw2), (GB1, gb1), (WOs, wo)):
        dmaG(dst[:, :], src)
    n_wl = sc.n("dG")

    sc.wait("V", "dG", n_wl)
    sc.wait("V", "G", nG_pre)
    def _diags(v):
        for i in range(24):
            v.tensor_scalar(CDIAG[:, i * 128:(i + 1) * 128], IDb[:, :], CVW[:, i:i + 1],
                            None, ALU.mult)
        for i in range(67):
            v.tensor_scalar(FDIAG[:, i * 128:(i + 1) * 128], IDb[:, :], FIW[:, i:i + 1],
                            None, ALU.mult)
        return v.tensor_scalar(FDIAG[:, 67 * 128:68 * 128], IDb[:, :], FIW[:, 67:68],
                               None, ALU.mult)
    sc.inc("V", "V", _diags)
    nV_diag = sc.n("V")

    pgs = ExitStack()
    GS = pgs.enter_context(nc.sbuf_tensor("gsb0", [128, 24 * GH], BF16, side="right"))
    dmaG(GS[:, :], gw1s)
    dmaG(ONT[0:1, 0:256], onorm)
    n_gs = sc.n("dG")
    SRW = pgs.enter_context(nc.sbuf_tensor("srw", [1, 12 * GH], F32, side="right"))
    sc.wait("T", "dG", n_gs)
    sc.wait("T", "G", nG_pre)
    for s in range(12):
        def _mm(t, s=s):
            t.matmul(PS_WA[0:1, 0:512], ONESC[:, :], GS[:, (2 * s) * GH:(2 * s + 1) * GH],
                     start=True, stop=False)
            return t.matmul(PS_WA[0:1, 0:512], ONESC[:, :],
                            GS[:, (2 * s + 1) * GH:(2 * s + 2) * GH], start=False, stop=True)
        tk = sc.inc("T", "T", _mm)
        sc.wait("S", "T", tk)
        scale = (1.0 / 256.0) if s % 3 == 0 else 1.0
        sc.inc("S", "S", lambda sl, s=s, scale=scale: sl.activation(
            SRW[0:1, s * GH:(s + 1) * GH], PS_WA[0:1, 0:512], AF.Copy, scale=scale))
        sc.wait("T", "S")
    sc.inc("V", "V", lambda v: v.memset(STL[:, :], 0.0))
    sc.wait("G", "S")
    sc.wait("G", "V")
    for s in range(12):
        sc.inc("G", "dG", lambda g, s=s: g.dma_start(
            bass.AP(STL, s * GH, [[GH, 1], [1, GH]]),
            bass.AP(SRW, s * GH, [[12 * GH, 1], [1, GH]])), 16)

    SC1 = E1(nc.sbuf_tensor("sc1", [1, 32], F32))
    sc.inc("V", "V", lambda v: v.memset(SC1[:, :], 0.0))
    sc.wait("Y", "V")
    dmaY(SC1[0:1, 0:4], gb2)
    dmaY(SC1[0:1, 8:12], floor)
    dmaY(SC1[0:1, 16:17], temp)
    n_sc1 = sc.n("dY")
    sc.wait("S", "dY", n_sc1)
    sc.wait("S", "dG", n_gs)
    sc.wait("S", "G", nG_pre)
    sc.inc("S", "S", lambda sl: sl.copy(RR2[0:1, 0:32], SC1[0:1, 0:32]))
    sc.wait("T", "S")
    def _bc1(t):
        t.matmul(PS_WB[:, 0:4], ONE2[:, 0:128], RR2[0:2, 0:4], start=True, stop=True)
        t.matmul(PS_WB[:, 4:8], ONE2[:, 0:128], RR2[0:2, 8:12], start=True, stop=True)
        t.matmul(PS_WB[:, 8:9], ONE2[:, 0:128], RR2[0:2, 16:17], start=True, stop=True)
        return t.matmul(PS_WB[:, 16:272], ONE2[:, 0:128],
                        bass.AP(ONT, 0, [[256, 2], [1, 256]]), start=True, stop=True)
    tk = sc.inc("T", "T", _bc1)
    sc.wait("V", "T", tk)
    def _bc2(v):
        v.tensor_copy(GB2T[:, :], PS_WB[:, 0:4])
        v.tensor_scalar(FLT[:, :], PS_WB[:, 4:8], 1e-7, 0.1, ALU.max, ALU.min)
        v.tensor_scalar(TIV[:, :], PS_WB[:, 8:9], 0.2, 10.0, ALU.max, ALU.min)
        v.tensor_copy(TIV[:, 0:1], PS_WB[:, 8:9])
        v.reciprocal(TIV[:, :], TIV[:, :])
        v.tensor_scalar(FLT[:, :], PS_WB[:, 4:8], 1e-7, 0.1, ALU.max, ALU.min)
        return v.tensor_copy(ONT[:, :], PS_WB[:, 16:272])
    sc.inc("V", "V", _bc2)
    sc.wait("T", "V")

    pgs.close()
    sc.wait("G", "T")
    XT = [E1(nc.sbuf_tensor(f"xt{i}", [128, 4 * D], F32)) for i in range(2)]
    XLNT = E1(nc.sbuf_tensor("xlnt", [128, D], BF16))
    XLN = E1(nc.sbuf_tensor("xln", [128, 8 * PT], BF16))
    PQ = E1(nc.sbuf_tensor("pq", [128, 2 * 515], BF16))
    PK = E1(nc.sbuf_tensor("pk", [128, 2 * 515], BF16))
    PV = E1(nc.sbuf_tensor("pv", [128, 2 * 515], BF16))
    QF = E1(nc.sbuf_tensor("qf", [128, 2 * PT], BF16))
    KF = E1(nc.sbuf_tensor("kf", [128, 2 * PT], BF16))
    VF = E1(nc.sbuf_tensor("vf", [128, 2 * 542], BF16))
    KBF = E1(nc.sbuf_tensor("kbf", [128, 2 * PT], BF16))
    FSF = E1(nc.sbuf_tensor("fsf", [128, 2 * PT], BF16))
    FLF = E1(nc.sbuf_tensor("flf", [128, 2 * PT], BF16))
    DOF = E1(nc.sbuf_tensor("dof", [128, 2 * PT], BF16))
    OPF = E1(nc.sbuf_tensor("opf", [128, 2 * PT], BF16))
    OPART = E1(nc.sbuf_tensor("opart", [128, 4 * D], F32))
    SQF = E1(nc.sbuf_tensor("sqf", [128, 2 * PT], BF16))
    VTM = E1(nc.sbuf_tensor("vtm", [128, NCH * 256], BF16))
    FSTM = E1(nc.sbuf_tensor("fstm", [128, NCH * 256], BF16))
    FLTM = E1(nc.sbuf_tensor("fltm", [128, NCH * 256], BF16))
    DOTM = E1(nc.sbuf_tensor("dotm", [128, NCH * 256], BF16))
    KNTM = E1(nc.sbuf_tensor("kntm", [128, 256], BF16))
    KBTM = E1(nc.sbuf_tensor("kbtm", [128, 256], BF16))
    VBTM = E1(nc.sbuf_tensor("vbtm", [128, 256], BF16))
    UP = E1(nc.sbuf_tensor("upb", [128, 256], BF16))
    SB = E1(nc.sbuf_tensor("sbb", [128, 512], BF16))
    SFP = E1(nc.sbuf_tensor("sfp", [128, 512], F32))
    XPI = E1(nc.sbuf_tensor("xpi", [128, 128], BF16))
    XCt = E1(nc.sbuf_tensor("xcb", [128, 128], BF16))
    YCt = E1(nc.sbuf_tensor("ycb", [128, 128], BF16))
    PCt = E1(nc.sbuf_tensor("pcb", [128, 128], BF16))
    PTt = E1(nc.sbuf_tensor("ptcb", [128, 128], BF16))
    ATT = E1(nc.sbuf_tensor("attb", [128, 128], BF16))
    WNEG = E1(nc.sbuf_tensor("wneg", [128, 256], BF16))
    GSBt = E1(nc.sbuf_tensor("gsbt", [128, 4 * PT], BF16))
    STATS = E1(nc.sbuf_tensor("stats", [128, GH], BF16))
    SPK = E1(nc.sbuf_tensor("spk", [128, 16], BF16))
    WGt = E1(nc.sbuf_tensor("wgt", [128, 4], F32))
    OT1 = E1(nc.sbuf_tensor("ot1", [128, 256], BF16))
    OT2 = E1(nc.sbuf_tensor("ot2", [128, 256], BF16))
    OTB = E1(nc.sbuf_tensor("otb", [128, 256], BF16))
    ROWS = E1(nc.sbuf_tensor("rows", [128, 512], BF16))
    RRF = E1(nc.sbuf_tensor("rrf", [128, 1024], F32))
    COLS = E1(nc.sbuf_tensor("colsb", [128, 64], F32))
    SCR = E1(nc.sbuf_tensor("scrb", [128, 1024], BF16))
    SCF = E1(nc.sbuf_tensor("scfb", [128, 16], F32))
    ETM = E1(nc.sbuf_tensor("etm", [128, 4], F32))

    def _z1(v):
        v.memset(PQ[:, :], 0.0)
        v.memset(PK[:, :], 0.0)
        v.memset(PV[:, :], 0.0)
        v.memset(VF[:, :], 0.0)
        v.memset(ROWS[:, :], 0.0)
        v.memset(SPK[:, :], 0.0)
        v.memset(SB[:, :], 0.0)
        v.memset(SFP[:, :], 0.0)
        return v.memset(STATS[:, :], 0.0)
    sc.inc("V", "V", _z1)

    def load_x(p):
        buf = XT[p % 2]
        for t in range(4):
            dmaG(buf[:, t * D:(t + 1) * D],
                 x_attn[p * PT + t * 128:p * PT + (t + 1) * 128, :])
    load_x(0)

    sc.wait("T", "G", sc.n("G"))
    sc.wait("T", "V", nV_diag)
    sc.wait("S", "G", sc.n("G"))
    CB, CSK, CSQ, CBSK, CNSK = 0, 4, 8, 12, 16

    for p in range(npass):
        sc.wait("S", "V")
        sc.wait("T", "V")
        sc.wait("G", "S")
        buf = XT[p % 2]
        nx = sc.n("dG")
        if p + 1 < npass:
            load_x(p + 1)
        if split and p == 8:
            sc.wait("G", "dY")
            sc.wait("G", "dG")
            def _rs1(g):
                return g.collective_compute(
                    "ReduceScatter", ALU.add,
                    replica_groups=[[0, 1, 2, 3], [4, 5, 6, 7]],
                    ins=[cc_in.ap()[0:L // 2, :].opt()],
                    outs=[cc_rs.ap()[0:TOK // 2, :].opt()])
            sc.inc("G", "CC", _rs1)
            sc.wait("G", "CC")

        sc.wait("S", "dG", nx)
        for t in range(4):
            xt = buf[:, t * D:(t + 1) * D]
            def _lnacc(sl, xt=xt):
                sl.activation(SCR[:, 0:1024], xt, AF.Copy, accum_out=SCF[:, 0:1])
                sl.activation(SCR[:, 0:1024], xt, AF.Square, accum_out=SCF[:, 1:2])
                return sl.copy(SCF[:, 15:16], CSTT[:, 0:1])
            sc.inc("S", "S", _lnacc)
            sc.wait("V", "S")
            def _lncols(v):
                v.tensor_scalar(SCF[:, 2:3], SCF[:, 0:1], 1.0 / D, None, ALU.mult)
                v.tensor_scalar(SCF[:, 3:4], SCF[:, 1:2], 1.0 / D, None, ALU.mult)
                v.tensor_copy(SCF[:, 9:10], ZRO[:, 0:1])
                v.tensor_mul(SCF[:, 4:5], SCF[:, 2:3], SCF[:, 2:3])
                v.tensor_copy(SCF[:, 9:10], ZRO[:, 0:1])
                v.tensor_sub(SCF[:, 3:4], SCF[:, 3:4], SCF[:, 4:5])
                v.tensor_copy(SCF[:, 9:10], ZRO[:, 0:1])
                return v.tensor_scalar(SCF[:, 3:4], SCF[:, 3:4], 1e-5, None, ALU.add)
            sc.inc("V", "V", _lncols)
            sc.wait("S", "V")
            sc.inc("S", "S", lambda sl: sl.sqrt(SCF[:, 3:4], SCF[:, 3:4]))
            sc.wait("V", "S")
            def _lninv(v):
                v.reciprocal(SCF[:, 3:4], SCF[:, 3:4])
                v.tensor_copy(SCF[:, 5:6], SCF[:, 2:3])
                v.tensor_mul(SCF[:, 4:5], SCF[:, 2:3], SCF[:, 3:4])
                v.tensor_copy(SCF[:, 9:10], ZRO[:, 0:1])
                return v.tensor_scalar(SCF[:, 4:5], SCF[:, 4:5], -1.0, None, ALU.mult)
            sc.inc("V", "V", _lninv)
            sc.wait("S", "V")
            sc.inc("S", "S", lambda sl, xt=xt: sl.activation(
                XLNT[:, :], xt, AF.Identity, bias=SCF[:, 4:5], scale=SCF[:, 3:4]))
            nS_ln = sc.n("S")
            if p == npass - 1 and t == 3 and "xlnt" in dbg:
                sc.wait("G", "S", nS_ln)
                sc.wait("G", "V")
                dmaG(dbg["xlnt"], XLNT[:, :])
                dmaG(dbg["lncols"], SCF[:, 0:8])
                sc.wait("S", "dG")
                sc.wait("T", "dG")
                sc.wait("V", "dG")
            sc.wait("T", "S", nS_ln)
            def _trx(tt):
                for k in range(7):
                    tt.transpose(PS_TA[:, k * 128:(k + 1) * 128],
                                 XLNT[:, k * 128:(k + 1) * 128], IDb[:, :])
                return tt.transpose(PS_TA[:, 896:1024], XLNT[:, 896:1024], IDb[:, :])
            tk = sc.inc("T", "T", _trx)
            sc.wait("S", "T", tk)
            sc.inc("S", "S", lambda sl, t=t: sl.copy(
                bass.AP(XLN, t * 128, [[8 * PT, 128], [PT, 8], [1, 128]]),
                bass.AP(PS_TA, 0, [[1024, 128], [128, 8], [1, 128]])))
            sc.wait("T", "S")

        nS_xln = sc.n("S")

        sc.wait("T", "S", nS_xln)
        for W, P, pi in ((WQ, PQ, 0), (WK, PK, 1), (WV, PV, 2)):
            for ch in range(2):
                bank = PSB[(pi * 2 + ch) % 2]
                sc.wait("T", "S")
                def _proj(t, W=W, ch=ch, bank=bank):
                    for k in range(7):
                        t.matmul(bank[:, 0:512], W[:, k * 256 + ch * 128:k * 256 + ch * 128 + 128],
                                 XLN[:, k * PT:(k + 1) * PT], start=(k == 0), stop=False)
                    return t.matmul(bank[:, 0:512], W[:, 7 * 256 + ch * 128:7 * 256 + ch * 128 + 128],
                                    XLN[:, 7 * PT:8 * PT], start=False, stop=True)
                tk = sc.inc("T", "T", _proj)
                sc.wait("S", "T", tk)
                sc.inc("S", "S", lambda sl, P=P, ch=ch, bank=bank: sl.copy(
                    P[:, ch * 515 + 3:ch * 515 + 515], bank[:, 0:512]))
        def _betamm(t):
            for k in range(7):
                t.matmul(PS_WA[0:1, 0:512], WBc[:, k:k + 1], XLN[:, k * PT:(k + 1) * PT],
                         start=(k == 0), stop=False)
            return t.matmul(PS_WA[0:1, 0:512], WBc[:, 7:8], XLN[:, 7 * PT:8 * PT],
                            start=False, stop=True)
        sc.inc("T", "T", _betamm)
        nS_proj = sc.n("S")
        sc.wait("T", "S", nS_proj)
        for P, O, pi in ((PQ, QF, 0), (PK, KF, 1), (PV, VF, 2)):
            for ch in range(2):
                bank = PSB[(pi * 2 + ch) % 2]
                sc.wait("T", "S")
                def _conv(t, P=P, pi=pi, ch=ch, bank=bank):
                    for j in range(3):
                        t.matmul(bank[:, 0:512],
                                 CDIAG[:, (pi * 8 + ch * 4 + j) * 128:(pi * 8 + ch * 4 + j + 1) * 128],
                                 P[:, ch * 515 + j:ch * 515 + j + 512], start=(j == 0), stop=False)
                    return t.matmul(bank[:, 0:512],
                                    CDIAG[:, (pi * 8 + ch * 4 + 3) * 128:(pi * 8 + ch * 4 + 4) * 128],
                                    P[:, ch * 515 + 3:ch * 515 + 3 + 512], start=False, stop=True)
                tk = sc.inc("T", "T", _conv)
                sc.wait("S", "T", tk)
                off = (ch * 542 + 30) if O is VF else (ch * PT)
                sc.inc("S", "S", lambda sl, O=O, off=off, bank=bank: sl.activation(
                    O[:, off:off + 512], bank[:, 0:512], AF.Silu))
        nS_conv = sc.n("S")

        for ch in range(2):
            sc.inc("S", "S", lambda sl, ch=ch: sl.activation(
                SQF[:, ch * PT:(ch + 1) * PT], KF[:, ch * PT:(ch + 1) * PT], AF.Square))
        nS_ksq = sc.n("S")
        sc.wait("T", "S", nS_ksq)
        def _ksum(t):
            t.matmul(PS_WA[32:33, 0:512], ONESC[:, :], SQF[:, 0:PT], start=True, stop=False)
            return t.matmul(PS_WA[32:33, 0:512], ONESC[:, :], SQF[:, PT:2 * PT],
                            start=False, stop=True)
        tk_k = sc.inc("T", "T", _ksum)
        sc.wait("S", "T", tk_k)
        for ch in range(2):
            sc.inc("S", "S", lambda sl, ch=ch: sl.activation(
                SQF[:, ch * PT:(ch + 1) * PT], QF[:, ch * PT:(ch + 1) * PT], AF.Square))
        nS_qsq = sc.n("S")
        sc.wait("T", "S", nS_qsq)
        def _qsum(t):
            t.matmul(PS_WA[64:65, 0:512], ONESC[:, :], SQF[:, 0:PT], start=True, stop=False)
            return t.matmul(PS_WA[64:65, 0:512], ONESC[:, :], SQF[:, PT:2 * PT],
                            start=False, stop=True)
        tk_q = sc.inc("T", "T", _qsum)
        sc.wait("S", "T", tk_q)
        def _rows(sl):
            sl.activation(RRF[0:1, 0:512], PS_WA[0:1, 0:512], AF.Sigmoid)
            sl.activation(RRF[32:33, 0:512], PS_WA[32:33, 0:512], AF.Sqrt, bias=1e-6)
            return sl.activation(RRF[64:65, 0:512], PS_WA[64:65, 0:512], AF.Sqrt, bias=1e-6)
        sc.inc("S", "S", _rows)
        sc.wait("V", "S")
        def _rows2(v):
            v.reciprocal(RRF[32:33, 0:512], RRF[32:33, 0:512])
            v.tensor_copy(SCF[:, 9:10], ZRO[:, 0:1])
            v.reciprocal(RRF[64:65, 0:512], RRF[64:65, 0:512])
            v.tensor_copy(SCF[:, 9:10], ZRO[:, 0:1])
            v.tensor_copy(RRF[0:1, 512:1024], RRF[32:33, 0:512])
            v.tensor_copy(SCF[:, 9:10], ZRO[:, 0:1])
            return v.tensor_mul(RRF[0:1, 512:1024], RRF[0:1, 512:1024], RRF[0:1, 0:512])
        sc.inc("V", "V", _rows2)
        sc.wait("T", "V")
        sc.wait("S", "V")
        def _rows3(sl):
            sl.copy(ROWS[0:1, 0:512], RRF[0:1, 0:512])
            sl.copy(ROWS[32:33, 0:512], RRF[32:33, 0:512])
            sl.copy(ROWS[64:65, 0:512], RRF[64:65, 0:512])
            return sl.copy(RR2[0:1, 0:512], RRF[0:1, 512:1024])
        sc.inc("S", "S", _rows3)
        nS_rows = sc.n("S")
        sc.wait("T", "S", nS_rows)
        def _bskmm(t):
            return t.matmul(PS_WC[:, 0:512], ONE2[:, 0:128], RR2[0:2, 0:512],
                            start=True, stop=True)
        tk = sc.inc("T", "T", _bskmm)
        sc.wait("V", "T", tk)
        def _kbf(v):
            v.tensor_mul(KBF[:, 0:PT], KF[:, 0:PT], PS_WC[:, 0:512])
            return v.tensor_mul(KBF[:, PT:2 * PT], KF[:, PT:2 * PT], PS_WC[:, 0:512])
        sc.inc("V", "V", _kbf)
        sc.wait("T", "V")
        for c in range(NCH):
            def _trr(t, c=c):
                return t.transpose(PS_TB[:, 0:128], ROWS[:, c * 128:(c + 1) * 128], IDb[:, :])
            tk = sc.inc("T", "T", _trr)
            sc.wait("V", "T", tk)
            def _colx(v, c=c):
                v.tensor_copy(COLS[:, CB + c:CB + c + 1], PS_TB[:, 0:1])
                v.tensor_copy(COLS[:, CSK + c:CSK + c + 1], PS_TB[:, 32:33])
                v.tensor_copy(COLS[:, CSQ + c:CSQ + c + 1], PS_TB[:, 64:65])
                v.tensor_copy(SCF[:, 9:10], ZRO[:, 0:1])
                v.tensor_mul(COLS[:, CBSK + c:CBSK + c + 1], COLS[:, CB + c:CB + c + 1],
                             COLS[:, CSK + c:CSK + c + 1])
                v.tensor_copy(SCF[:, 9:10], ZRO[:, 0:1])
                return v.tensor_scalar(COLS[:, CNSK + c:CNSK + c + 1],
                                       COLS[:, CSK + c:CSK + c + 1], -1.0, None, ALU.mult)
            sc.inc("V", "V", _colx)
            sc.wait("T", "V")

        for (O, base, ntap) in ((FSF, 0, 3), (FLF, 6, 31)):
            for ch in range(2):
                bank = PSB[ch]
                sc.wait("T", "S")
                def _fir(t, O=O, base=base, ntap=ntap, ch=ch, bank=bank):
                    for j in range(ntap - 1):
                        t.matmul(bank[:, 0:512],
                                 FDIAG[:, (base + ch * ntap + j) * 128:(base + ch * ntap + j + 1) * 128],
                                 VF[:, ch * 542 + (31 - ntap) + j:ch * 542 + (31 - ntap) + j + 512],
                                 start=(j == 0), stop=False)
                    j = ntap - 1
                    return t.matmul(bank[:, 0:512],
                                    FDIAG[:, (base + ch * ntap + j) * 128:(base + ch * ntap + j + 1) * 128],
                                    VF[:, ch * 542 + (31 - ntap) + j:ch * 542 + (31 - ntap) + j + 512],
                                    start=False, stop=True)
                tk = sc.inc("T", "T", _fir)
                sc.wait("S", "T", tk)
                sc.inc("S", "S", lambda sl, O=O, ch=ch, bank=bank: sl.copy(
                    O[:, ch * PT:(ch + 1) * PT], bank[:, 0:512]))
        nS_fir = sc.n("S")

        for c in range(NCH):
            gc = p * NCH + c
            t0 = c * 128
            sc.wait("T", "S", nS_fir)
            sc.wait("T", "S")
            def _trc(t, t0=t0):
                srcs = [KF[:, t0:t0 + 128], KF[:, PT + t0:PT + t0 + 128],
                        VF[:, 30 + t0:30 + t0 + 128], VF[:, 542 + 30 + t0:542 + 30 + t0 + 128],
                        FSF[:, t0:t0 + 128], FSF[:, PT + t0:PT + t0 + 128],
                        FLF[:, t0:t0 + 128], FLF[:, PT + t0:PT + t0 + 128]]
                for i, s_ in enumerate(srcs[:-1]):
                    t.transpose(PS_TA[:, i * 128:(i + 1) * 128], s_, IDb[:, :])
                return t.transpose(PS_TA[:, 896:1024], srcs[-1], IDb[:, :])
            tk = sc.inc("T", "T", _trc)
            sc.wait("S", "T", tk)
            def _evc(sl, c=c):
                sl.activation(KNTM[:, :], PS_TA[:, 0:256], AF.Copy,
                              scale=COLS[:, CSK + c:CSK + c + 1])
                sl.activation(KBTM[:, :], PS_TA[:, 0:256], AF.Copy,
                              scale=COLS[:, CBSK + c:CBSK + c + 1])
                sl.activation(VBTM[:, :], PS_TA[:, 256:512], AF.Copy,
                              scale=COLS[:, CB + c:CB + c + 1])
                sl.copy(VTM[:, c * 256:(c + 1) * 256], PS_TA[:, 256:512])
                sl.copy(FSTM[:, c * 256:(c + 1) * 256], PS_TA[:, 512:768])
                return sl.copy(FLTM[:, c * 256:(c + 1) * 256], PS_TA[:, 768:1024])
            sc.inc("S", "S", _evc)
            nS_evc = sc.n("S")

            sc.wait("T", "S", nS_evc)
            def _grA(t, t0=t0):
                t.matmul(PS_WA[:, 0:128], KF[:, t0:t0 + 128], KBF[:, t0:t0 + 128],
                         start=True, stop=False)
                return t.matmul(PS_WA[:, 0:128], KF[:, PT + t0:PT + t0 + 128],
                                KBF[:, PT + t0:PT + t0 + 128], start=False, stop=True)
            tk = sc.inc("T", "T", _grA)
            sc.wait("V", "T", tk)
            sc.inc("V", "V", lambda v: v.tensor_mul(SCR[:, 0:128], PS_WA[:, 0:128], MKUS[:, :]))
            sc.wait("S", "V")
            sc.inc("S", "S", lambda sl, c=c: sl.activation(
                XCt[:, :], SCR[:, 0:128], AF.Copy, scale=COLS[:, CNSK + c:CNSK + c + 1]))
            nS_x = sc.n("S")
            sc.wait("T", "S", nS_x)
            tk = sc.inc("T", "T", lambda t: t.transpose(PS_TB[:, 0:128], XCt[:, :], IDb[:, :]))
            sc.wait("S", "T", tk)
            sc.inc("S", "S", lambda sl: sl.copy(YCt[:, :], PS_TB[:, 0:128]))
            nS_y = sc.n("S")
            if p == 0 and c == 0 and "xc0" in dbg:
                sc.wait("G", "S", nS_y)
                dmaG(dbg["xc0"], XCt[:, :])
                dmaG(dbg["kntm"], KNTM[:, :])
                dmaG(dbg["kbtm"], KBTM[:, :])
                dmaG(dbg["vbtm"], VBTM[:, :])
                sc.wait("S", "dG")
                sc.wait("T", "dG")

            sc.wait("V", "S", nS_y)
            def _pinit(v):
                v.tensor_add(PCt[:, :], XCt[:, :], IDb[:, :])
                return v.tensor_add(PTt[:, :], YCt[:, :], IDb[:, :])
            sc.inc("V", "V", _pinit)
            sc.wait("T", "V")
            sc.wait("S", "V")
            for j in range(1, 7):
                def _sq2(t):
                    t.matmul(PS_WA[:, 128:256], YCt[:, :], XCt[:, :], start=True, stop=True)
                    return t.matmul(PS_WA[:, 256:384], XCt[:, :], YCt[:, :], start=True, stop=True)
                tk = sc.inc("T", "T", _sq2)
                sc.wait("S", "T", tk)
                def _sqe(sl):
                    sl.copy(XCt[:, :], PS_WA[:, 128:256])
                    return sl.copy(YCt[:, :], PS_WA[:, 256:384])
                sc.inc("S", "S", _sqe)
                sc.wait("V", "S")
                sc.inc("V", "V", lambda v: v.tensor_add(XPI[:, :], XCt[:, :], IDb[:, :]))
                sc.wait("T", "V")
                def _upd(t):
                    t.matmul(PS_WB[:, 0:128], PTt[:, :], XPI[:, :], start=True, stop=True)
                    return t.matmul(PS_WB[:, 128:256], XPI[:, :], PTt[:, :], start=True, stop=True)
                tk = sc.inc("T", "T", _upd)
                sc.wait("S", "T", tk)
                def _pce(sl):
                    sl.copy(PCt[:, :], PS_WB[:, 0:128])
                    return sl.copy(PTt[:, :], PS_WB[:, 128:256])
                sc.inc("S", "S", _pce)
            nS_chain = sc.n("S")
            if p == 0 and c == 0 and "pc0" in dbg:
                sc.wait("G", "S", nS_chain)
                dmaG(dbg["pc0"], PCt[:, :])
                dmaG(dbg["ptc0"], PTt[:, :])
                sc.wait("S", "dG")
                sc.wait("T", "dG")

            sc.wait("T", "S", nS_chain)
            def _wmm(t):
                t.matmul(PS_WA[:, 0:128], KBTM[:, 0:128], PCt[:, :], start=True, stop=True)
                return t.matmul(PS_WA[:, 384:512], KBTM[:, 128:256], PCt[:, :],
                                start=True, stop=True)
            tk = sc.inc("T", "T", _wmm)
            sc.wait("S", "T", tk)
            def _wng(sl):
                sl.mul(WNEG[:, 0:128], PS_WA[:, 0:128], -1.0)
                return sl.mul(WNEG[:, 128:256], PS_WA[:, 384:512], -1.0)
            sc.inc("S", "S", _wng)
            nS_w = sc.n("S")
            sc.wait("T", "S", nS_w)
            def _ups(t):
                t.matmul(PS_WC[:, 0:256], PCt[:, :], VBTM[:, :], start=True, stop=False)
                t.matmul(PS_WC[:, 0:256], WNEG[:, 0:128], SB[:, 0:256], start=False, stop=False)
                return t.matmul(PS_WC[:, 0:256], WNEG[:, 128:256], SB[:, 256:512],
                                start=False, stop=True)
            tk = sc.inc("T", "T", _ups)
            sc.wait("S", "T", tk)
            sc.inc("S", "S", lambda sl: sl.copy(UP[:, :], PS_WC[:, 0:256]))
            nS_up = sc.n("S")
            if p == 0 and c == 0 and "up0" in dbg:
                sc.wait("G", "S", nS_up)
                dmaG(dbg["up0"], UP[:, :])
                dmaG(dbg["wneg0"], WNEG[:, :])
                sc.wait("S", "dG")
                sc.wait("T", "dG")

            def _grq(t, t0=t0):
                t.matmul(PS_WA[:, 128:256], KF[:, t0:t0 + 128], QF[:, t0:t0 + 128],
                         start=True, stop=False)
                return t.matmul(PS_WA[:, 128:256], KF[:, PT + t0:PT + t0 + 128],
                                QF[:, PT + t0:PT + t0 + 128], start=False, stop=True)
            tk = sc.inc("T", "T", _grq)
            sc.wait("V", "T", tk)
            sc.wait("V", "S", nS_up)
            sc.inc("V", "V", lambda v: v.tensor_mul(SCR[:, 128:256], PS_WA[:, 128:256], MKU[:, :]))
            sc.wait("S", "V")
            sc.inc("S", "S", lambda sl, c=c: sl.activation(
                ATT[:, :], SCR[:, 128:256], AF.Copy, scale=COLS[:, CSK + c:CSK + c + 1]))
            nS_att = sc.n("S")
            if p == 0 and c == 0 and "att0" in dbg:
                sc.wait("G", "S", nS_att)
                dmaG(dbg["att0"], ATT[:, :])
                sc.wait("S", "dG")
                sc.wait("T", "dG")
            sc.wait("T", "S", nS_att)
            def _o(t, t0=t0):
                t.matmul(PS_WC[:, 256:512], QF[:, t0:t0 + 128], SB[:, 0:256],
                         start=True, stop=False)
                t.matmul(PS_WC[:, 256:512], QF[:, PT + t0:PT + t0 + 128], SB[:, 256:512],
                         start=False, stop=False)
                return t.matmul(PS_WC[:, 256:512], ATT[:, :], UP[:, :], start=False, stop=True)
            tk = sc.inc("T", "T", _o)
            sc.wait("S", "T", tk)
            sc.inc("S", "S", lambda sl, c=c: sl.activation(
                DOTM[:, c * 256:(c + 1) * 256], PS_WC[:, 256:512], AF.Copy,
                scale=COLS[:, CSQ + c:CSQ + c + 1]))
            nS_do = sc.n("S")

            def _ds(t):
                t.matmul(PS_S[:, 0:256], KNTM[:, 0:128], UP[:, :], start=True, stop=True)
                return t.matmul(PS_S[:, 256:512], KNTM[:, 128:256], UP[:, :],
                                start=True, stop=True)
            tk = sc.inc("T", "T", _ds)
            sc.wait("V", "T", tk)
            def _sacc(v):
                v.tensor_add(SFP[:, 0:256], SFP[:, 0:256], PS_S[:, 0:256])
                return v.tensor_add(SFP[:, 256:512], SFP[:, 256:512], PS_S[:, 256:512])
            sc.inc("V", "V", _sacc)
            sc.wait("S", "V")
            sc.inc("S", "S", lambda sl: sl.copy(SB[:, :], SFP[:, :]))
            sc.wait("T", "V")

            sc.wait("T", "S")
            def _trd(t, c=c):
                t.transpose(PS_TB[:, 128:256], DOTM[:, c * 256:c * 256 + 128], IDb[:, :])
                return t.transpose(PS_TB[:, 256:384], DOTM[:, c * 256 + 128:c * 256 + 256],
                                   IDb[:, :])
            tk = sc.inc("T", "T", _trd)
            sc.wait("S", "T", tk)
            def _dof(sl, t0=t0):
                sl.copy(DOF[:, t0:t0 + 128], PS_TB[:, 128:256])
                return sl.copy(DOF[:, PT + t0:PT + t0 + 128], PS_TB[:, 256:384])
            sc.inc("S", "S", _dof)

            for (BR, bi) in ((FSTM, 0), (FLTM, 1), (DOTM, 2), (VTM, 3)):
                sl_ = BR[:, c * 256:(c + 1) * 256]
                def _redm(v, sl_=sl_, bi=bi):
                    with nc.allow_low_precision("stat cols feed bf16 matmul operands"):
                        v.reduce_sum(SPK[:, 3 * bi:3 * bi + 1], sl_, axis=AX.X)
                        return v.reduce_max(SPK[:, 3 * bi + 2:3 * bi + 3], sl_, axis=AX.X)
                sc.inc("V", "V", _redm)
                sc.inc("S", "S", lambda sl, sl_=sl_, bi=bi: sl.activation(
                    SCR[:, 0:256], sl_, AF.Square, accum_out=SCF[:, 8 + bi:9 + bi]))
            sc.inc("S", "S", lambda sl: sl.copy(SCF[:, 15:16], CSTT[:, 0:1]))
            sc.wait("V", "S")
            def _rstat(v):
                for bi in range(3):
                    v.tensor_scalar(SCF[:, 12 + bi:13 + bi], SCF[:, 8 + bi:9 + bi],
                                    1.0 / 256.0, 1e-8, ALU.mult, ALU.max)
                return v.tensor_scalar(SCF[:, 15:16], SCF[:, 11:12],
                                       1.0 / 256.0, 1e-8, ALU.mult, ALU.max)
            sc.inc("V", "V", _rstat)
            sc.wait("S", "V")
            def _rsq(sl):
                for bi in range(3):
                    sl.activation(SPK[:, 3 * bi + 1:3 * bi + 2], SCF[:, 12 + bi:13 + bi], AF.Sqrt)
                return sl.activation(SPK[:, 10:11], SCF[:, 15:16], AF.Sqrt)
            sc.inc("S", "S", _rsq)
            nS_spk = sc.n("S")
            sc.wait("T", "S", nS_spk)
            sc.wait("T", "V")
            tk = sc.inc("T", "T", lambda t: t.transpose(PS_TB[0:16, 384:512], SPK[:, 0:16],
                                                        IDb[:, :]))
            sc.wait("S", "T", tk)
            sc.inc("S", "S", lambda sl, c=c: sl.copy(
                STATS[0:16, c * 128:(c + 1) * 128], PS_TB[0:16, 384:512]))
            sc.wait("T", "S")

        def _halos(v):
            for P in (PQ, PK, PV):
                v.tensor_copy(P[:, 0:3], P[:, 512:515])
                v.tensor_copy(P[:, 515:518], P[:, 1027:1030])
            v.tensor_copy(VF[:, 0:30], VF[:, 512:542])
            return v.tensor_copy(VF[:, 542:572], VF[:, 1054:1084])
        sc.inc("V", "V", _halos)

        nS_br = sc.n("S")
        sc.wait("T", "S", nS_br)
        for m in range(4):
            bank = PSB[m % 2]
            sc.wait("T", "S")
            def _gate(t, m=m, bank=bank):
                for k in range(8):
                    t.matmul(bank[:, 0:512], GW1H[:, k * GH + m * 128:k * GH + m * 128 + 128],
                             XLN[:, k * PT:(k + 1) * PT], start=(k == 0), stop=False)
                for bi in range(4):
                    for c2 in range(2):
                        if bi == 0:
                            rhs = FSF[:, c2 * PT:(c2 + 1) * PT]
                        elif bi == 1:
                            rhs = FLF[:, c2 * PT:(c2 + 1) * PT]
                        elif bi == 2:
                            rhs = DOF[:, c2 * PT:(c2 + 1) * PT]
                        else:
                            rhs = VF[:, c2 * 542 + 30:c2 * 542 + 30 + 512]
                        t.matmul(bank[:, 0:512],
                                 GW1B[:, (bi * 2 + c2) * GH + m * 128:(bi * 2 + c2) * GH + m * 128 + 128],
                                 rhs, start=False, stop=False)
                return t.matmul(bank[:, 0:512], STL[0:16, m * 128:(m + 1) * 128],
                                STATS[0:16, :], start=False, stop=True)
            tk = sc.inc("T", "T", _gate)
            sc.wait("S", "T", tk)
            sc.inc("S", "S", lambda sl, m=m, bank=bank: sl.activation(
                GSBt[:, m * PT:(m + 1) * PT], bank[:, 0:512], AF.Gelu, bias=GB1[:, m:m + 1]))
        nS_gate = sc.n("S")

        sc.wait("T", "S", nS_gate)
        for c in range(NCH):
            t0 = c * 128
            def _lg(t, t0=t0):
                for mg in range(3):
                    t.matmul(PS_WB[:, 256:260], GSBt[:, mg * PT + t0:mg * PT + t0 + 128],
                             GW2[:, mg * 4:mg * 4 + 4], start=(mg == 0), stop=False)
                return t.matmul(PS_WB[:, 256:260], GSBt[:, 3 * PT + t0:3 * PT + t0 + 128],
                                GW2[:, 12:16], start=False, stop=True)
            tk = sc.inc("T", "T", _lg)
            sc.wait("V", "T", tk)
            def _sm1(v):
                v.tensor_add(ETM[:, :], PS_WB[:, 256:260], GB2T[:, :])
                v.tensor_copy(SCF[:, 9:10], ZRO[:, 0:1])
                v.tensor_scalar(ETM[:, :], ETM[:, :], TIV[:, 0:1], None, ALU.mult)
                v.tensor_copy(SCF[:, 9:10], ZRO[:, 0:1])
                v.reduce_max(SCF[:, 5:6], ETM[:, :], axis=AX.X)
                v.tensor_copy(SCF[:, 10:11], ZRO[:, 0:1])
                return v.tensor_scalar(SCF[:, 5:6], SCF[:, 5:6], -1.0, None, ALU.mult)
            sc.inc("V", "V", _sm1)
            sc.wait("T", "V")
            sc.wait("S", "V")
            sc.inc("S", "S", lambda sl: sl.activation(ETM[:, :], ETM[:, :], AF.Exp,
                                                      bias=SCF[:, 5:6]))
            sc.wait("V", "S")
            def _sm2(v, c=c):
                v.reduce_sum(SCF[:, 6:7], ETM[:, :], axis=AX.X)
                v.tensor_copy(SCF[:, 9:10], ZRO[:, 0:1])
                v.reciprocal(SCF[:, 6:7], SCF[:, 6:7])
                v.tensor_copy(SCF[:, 9:10], FLT[:, 0:1])
                v.tensor_scalar(WGt[:, :], ETM[:, :], SCF[:, 6:7], None, ALU.mult)
                v.tensor_copy(SCF[:, 9:10], ZRO[:, 0:1])
                v.tensor_max(WGt[:, :], WGt[:, :], FLT[:, :])
                v.tensor_copy(SCF[:, 9:10], ZRO[:, 0:1])
                v.reduce_sum(SCF[:, 6:7], WGt[:, :], axis=AX.X)
                v.tensor_copy(SCF[:, 9:10], ZRO[:, 0:1])
                v.reciprocal(SCF[:, 6:7], SCF[:, 6:7])
                v.tensor_copy(SCF[:, 9:10], FLT[:, 0:1])
                v.tensor_scalar(WGt[:, :], WGt[:, :], SCF[:, 6:7], None, ALU.mult)
                v.tensor_copy(SCF[:, 9:10], ZRO[:, 0:1])
                v.tensor_scalar(OT1[:, :], FSTM[:, c * 256:(c + 1) * 256], WGt[:, 0:1],
                                None, ALU.mult)
                v.tensor_copy(SCF[:, 9:10], ZRO[:, 0:1])
                v.scalar_tensor_tensor(OT2[:, :], FLTM[:, c * 256:(c + 1) * 256], WGt[:, 1:2],
                                       OT1[:, :], ALU.mult, ALU.add)
                v.tensor_copy(SCF[:, 9:10], ZRO[:, 0:1])
                v.scalar_tensor_tensor(OT1[:, :], DOTM[:, c * 256:(c + 1) * 256], WGt[:, 2:3],
                                       OT2[:, :], ALU.mult, ALU.add)
                v.tensor_copy(SCF[:, 9:10], ZRO[:, 0:1])
                return v.scalar_tensor_tensor(OT2[:, :], VTM[:, c * 256:(c + 1) * 256],
                                              WGt[:, 3:4], OT1[:, :], ALU.mult, ALU.add)
            sc.inc("V", "V", _sm2)
            sc.wait("S", "V")
            def _rmsacc(sl):
                sl.activation(SCR[:, 0:256], OT2[:, :], AF.Square, accum_out=SCF[:, 7:8])
                return sl.copy(SCF[:, 15:16], CSTT[:, 0:1])
            sc.inc("S", "S", _rmsacc)
            sc.wait("V", "S")
            sc.inc("V", "V", lambda v: v.tensor_scalar(SCF[:, 7:8], SCF[:, 7:8],
                                                       1.0 / 256.0, 1e-5, ALU.mult, ALU.add))
            sc.wait("S", "V")
            sc.inc("S", "S", lambda sl: sl.sqrt(SCF[:, 7:8], SCF[:, 7:8]))
            sc.wait("V", "S")
            sc.inc("V", "V", lambda v: v.reciprocal(SCF[:, 7:8], SCF[:, 7:8]))
            sc.wait("S", "V")
            sc.inc("S", "S", lambda sl: sl.activation(OT1[:, :], OT2[:, :], AF.Copy,
                                                      scale=SCF[:, 7:8]))
            sc.wait("V", "S")
            sc.inc("V", "V", lambda v: v.tensor_mul(OTB[:, :], OT1[:, :], ONT[:, :]))
            nV_o = sc.n("V")
            sc.wait("T", "V", nV_o)
            sc.wait("T", "S")
            def _tro(t):
                t.transpose(PS_TB[:, 512:640], OTB[:, 0:128], IDb[:, :])
                return t.transpose(PS_TB[:, 640:768], OTB[:, 128:256], IDb[:, :])
            tk = sc.inc("T", "T", _tro)
            sc.wait("S", "T", tk)
            def _opf(sl, t0=t0):
                sl.copy(OPF[:, t0:t0 + 128], PS_TB[:, 512:640])
                return sl.copy(OPF[:, PT + t0:PT + t0 + 128], PS_TB[:, 640:768])
            sc.inc("S", "S", _opf)
            sc.wait("T", "S")

        nS_opf = sc.n("S")
        sc.wait("T", "S", nS_opf)
        sc.wait("S", "dY")
        for tt in range(4):
            t0 = tt * 128
            for n_ in range(2):
                bank = PSB[n_]
                sc.wait("T", "S")
                def _wo(t, t0=t0, n_=n_, bank=bank):
                    t.matmul(bank[:, 0:512], OPF[:, t0:t0 + 128],
                             WOs[:, n_ * 512:n_ * 512 + 512], start=True, stop=False)
                    return t.matmul(bank[:, 0:512], OPF[:, PT + t0:PT + t0 + 128],
                                    WOs[:, 1024 + n_ * 512:1024 + n_ * 512 + 512],
                                    start=False, stop=True)
                tk = sc.inc("T", "T", _wo)
                sc.wait("S", "T", tk)
                sc.inc("S", "S", lambda sl, tt=tt, n_=n_, bank=bank: sl.copy(
                    OPART[:, tt * D + n_ * 512:tt * D + n_ * 512 + 512], bank[:, 0:512]))
        nS_op = sc.n("S")
        sc.wait("Y", "S", nS_op)
        def _ccw(y, p=p):
            return y.dma_start(
                bass.AP(cc_in, p * PT * D, [[D, 128], [128 * D, 4], [1, D]]),
                bass.AP(OPART, 0, [[4 * D, 128], [D, 4], [1, D]]))
        sc.inc("Y", "dY", _ccw, 16)

    if dbg:
        sc.wait("G", "S")
        sc.wait("G", "V")
        for name, ap_src in (("qf", QF), ("kf", KF), ("vf", VF), ("xln", XLN),
                             ("fsf", FSF), ("flf", FLF), ("dof", DOF), ("gsb", GSBt),
                             ("opf", OPF), ("opart", OPART), ("kbf", KBF),
                             ("vtm", VTM), ("dotm", DOTM), ("spk", SPK), ("wgt", WGt)):
            if name in dbg:
                dmaG(dbg[name], ap_src[:, :])
        if "beta" in dbg:
            dmaG(dbg["beta"], RRF[0:1, 0:512])
        if "xt" in dbg:
            dmaG(dbg["xt"], XT[(npass - 1) % 2][:, :])
        if "rows" in dbg:
            dmaG(dbg["rows"], RRF[:, :])
        if "stats" in dbg:
            dmaG(dbg["stats"], STATS[0:16, :])
        if "sb" in dbg:
            dmaG(dbg["sb"], SB[:, :])
        sc.wait("Y", "dG")

    n_cc = sc.n("dY")
    sc.wait("G", "dY", n_cc)
    sc.wait("G", "dG")
    def _rs(g):
        if split:
            return g.collective_compute(
                "ReduceScatter", ALU.add,
                replica_groups=[[0, 1, 2, 3], [4, 5, 6, 7]],
                ins=[cc_in.ap()[L // 2:L, :].opt()],
                outs=[cc_rs.ap()[TOK // 2:TOK, :].opt()])
        return g.collective_compute(
            "ReduceScatter", ALU.add,
            replica_groups=[[0, 1, 2, 3], [4, 5, 6, 7]],
            ins=[cc_in.ap().opt()], outs=[cc_rs.ap().opt()])
    sc.inc("G", "CC", _rs)
    p1.close()

    if not do_ffn:
        sc.wait("Y", "CC")
        AOT = E(nc.sbuf_tensor("aot", [128, D], F32))
        AOF = E(nc.sbuf_tensor("aof", [128, D], F32))
        for tt in range(TOK // 128):
            dmaY(AOT[:, :], cc_rs.ap()[tt * 128:(tt + 1) * 128, :])
            sc.wait("V", "dY")
            sc.inc("V", "V", lambda v: v.tensor_copy(AOF[:, :], AOT[:, :]))
            sc.wait("Y", "V")
            dmaY(out[tt * 128:(tt + 1) * 128, :], AOF[:, :])
        sc.wait("Y", "dY")
        nc_final(nc, sc)
        st.close()
        return nc, dbg

    p2 = ExitStack()
    E2 = p2.enter_context
    XRES = E2(nc.sbuf_tensor("xres", [128, D], F32))
    AOT = E2(nc.sbuf_tensor("aot", [128, D], F32))
    X2 = E2(nc.sbuf_tensor("x2b", [128, D], F32))
    X2LNT = E2(nc.sbuf_tensor("x2lnt", [128, D], BF16))
    X2LN = E2(nc.sbuf_tensor("x2ln", [128, 8 * 1024], BF16))
    SCF2 = E2(nc.sbuf_tensor("scf2", [128, 16], F32))
    S1C = E2(nc.sbuf_tensor("s1cb", [128, 32], F32))
    S2C = E2(nc.sbuf_tensor("s2cb", [128, 32], F32))
    SH1C = E2(nc.sbuf_tensor("sh1cb", [128, 32], F32))
    SH2C = E2(nc.sbuf_tensor("sh2cb", [128, 32], F32))
    S3C = E2(nc.sbuf_tensor("s3cb", [128, 8], F32))
    HT = E2(nc.sbuf_tensor("ht", [128, 32 * 1024], BF16))
    G1 = E2(nc.sbuf_tensor("g1", [128, 32 * 1024], BF16))
    RING = [E2(nc.sbuf_tensor(f"ring{i}", [128, 4096], BF16)) for i in range(3)]
    OASM = E2(nc.sbuf_tensor("oasm", [128, 8 * 1024], BF16))
    TM1 = E2(nc.sbuf_tensor("tm1", [128, 512], BF16))
    TM2 = E2(nc.sbuf_tensor("tm2", [128, 512], BF16))
    OUTT = E2(nc.sbuf_tensor("outt", [128, 1024], F32))

    dmaY(S1C[:, :], s1c)
    dmaY(S2C[:, :], s2c)
    dmaY(SH1C[:, :], sh1c)
    dmaY(SH2C[:, :], sh2c)
    dmaY(S3C[:, :], s3col)
    n_scl = sc.n("dY")

    sc.wait("G", "CC")
    ring_free = [0, 0, 0]

    for hh in range(2):
        tok0 = hh * 1024
        sc.wait("S", "dY", n_scl)
        cc_need = 1 if (split and hh == 0) else None
        sc.wait("V", "CC", cc_need)
        sc.wait("Y", "CC", cc_need)
        for tt in range(8):
            r0 = tok0 + tt * 128
            nlx = dmaY(XRES[:, :], x_res[r0:r0 + 128, :])
            nla = dmaY(AOT[:, :], cc_rs.ap()[r0:r0 + 128, :])
            sc.wait("V", "dY")
            sc.wait("V", "S")
            sc.inc("V", "V", lambda v: v.tensor_add(X2[:, :], XRES[:, :], AOT[:, :]))
            nV_x2 = sc.n("V")
            sc.wait("Y", "V", nV_x2)
            dmaY(x2stash[r0:r0 + 128, :], X2[:, :])
            sc.wait("S", "V", nV_x2)
            def _ln2acc(sl):
                sl.activation(OASM[:, 0:1024], X2[:, :], AF.Copy, accum_out=SCF2[:, 0:1])
                sl.activation(OASM[:, 0:1024], X2[:, :], AF.Square, accum_out=SCF2[:, 1:2])
                return sl.copy(SCF2[:, 15:16], CSTT[:, 0:1])
            sc.inc("S", "S", _ln2acc)
            sc.wait("V", "S")
            def _ln2c(v):
                v.tensor_scalar(SCF2[:, 2:3], SCF2[:, 0:1], 1.0 / D, None, ALU.mult)
                v.tensor_scalar(SCF2[:, 3:4], SCF2[:, 1:2], 1.0 / D, None, ALU.mult)
                v.tensor_copy(SCF2[:, 9:10], ZRO[:, 0:1])
                v.tensor_mul(SCF2[:, 4:5], SCF2[:, 2:3], SCF2[:, 2:3])
                v.tensor_copy(SCF2[:, 9:10], ZRO[:, 0:1])
                v.tensor_sub(SCF2[:, 3:4], SCF2[:, 3:4], SCF2[:, 4:5])
                v.tensor_copy(SCF2[:, 9:10], ZRO[:, 0:1])
                return v.tensor_scalar(SCF2[:, 3:4], SCF2[:, 3:4], 1e-5, None, ALU.add)
            sc.inc("V", "V", _ln2c)
            sc.wait("S", "V")
            sc.inc("S", "S", lambda sl: sl.sqrt(SCF2[:, 3:4], SCF2[:, 3:4]))
            sc.wait("V", "S")
            def _ln2i(v):
                v.reciprocal(SCF2[:, 3:4], SCF2[:, 3:4])
                v.tensor_copy(SCF2[:, 5:6], SCF2[:, 2:3])
                v.tensor_mul(SCF2[:, 4:5], SCF2[:, 2:3], SCF2[:, 3:4])
                v.tensor_copy(SCF2[:, 9:10], ZRO[:, 0:1])
                return v.tensor_scalar(SCF2[:, 4:5], SCF2[:, 4:5], -1.0, None, ALU.mult)
            sc.inc("V", "V", _ln2i)
            sc.wait("S", "V")
            sc.inc("S", "S", lambda sl: sl.activation(X2LNT[:, :], X2[:, :], AF.Identity,
                                                      bias=SCF2[:, 4:5], scale=SCF2[:, 3:4]))
            nS_l2 = sc.n("S")
            sc.wait("T", "S", nS_l2)
            def _trx2(t):
                for k in range(7):
                    t.transpose(PS_TA[:, k * 128:(k + 1) * 128],
                                X2LNT[:, k * 128:(k + 1) * 128], IDb[:, :])
                return t.transpose(PS_TA[:, 896:1024], X2LNT[:, 896:1024], IDb[:, :])
            tk = sc.inc("T", "T", _trx2)
            sc.wait("S", "T", tk)
            sc.inc("S", "S", lambda sl, tt=tt: sl.copy(
                bass.AP(X2LN, tt * 128, [[8 * 1024, 128], [1024, 8], [1, 128]]),
                bass.AP(PS_TA, 0, [[1024, 128], [128, 8], [1, 128]])))
            sc.wait("T", "S")

        nS_x2ln = sc.n("S")
        sc.wait("T", "S", nS_x2ln)
        if do_ffn == "x":
            continue

        for mg in range(8):
            sa, sb = (2 * mg) % 3, (2 * mg + 1) % 3
            sc.wait("G", "T", ring_free[sa])
            n1 = dmaG(RING[sa][:, :], w1t[mg])
            sc.wait("G", "T", ring_free[sb])
            n2 = dmaG(RING[sb][:, :], w2t[mg])
            sc.wait("T", "dG", n2)
            for m in range(4):
                gm = mg * 4 + m
                for t2 in range(2):
                    b1 = PSB[0] if t2 == 0 else PSB[1]
                    b2 = PS_WA if t2 == 0 else PS_WB
                    sc.wait("T", "S")
                    def _a(t, sa=sa, sb=sb, m=m, t2=t2, b1=b1, b2=b2):
                        for k in range(8):
                            t.matmul(b1[:, 0:512],
                                     RING[sa][:, k * 512 + m * 128:k * 512 + m * 128 + 128],
                                     X2LN[:, k * 1024 + t2 * 512:k * 1024 + t2 * 512 + 512],
                                     start=(k == 0), stop=(k == 7))
                        for k in range(7):
                            t.matmul(b2[:, 0:512],
                                     RING[sb][:, k * 512 + m * 128:k * 512 + m * 128 + 128],
                                     X2LN[:, k * 1024 + t2 * 512:k * 1024 + t2 * 512 + 512],
                                     start=(k == 0), stop=False)
                        return t.matmul(b2[:, 0:512],
                                        RING[sb][:, 7 * 512 + m * 128:7 * 512 + m * 128 + 128],
                                        X2LN[:, 7 * 1024 + t2 * 512:7 * 1024 + t2 * 512 + 512],
                                        start=False, stop=True)
                    tk = sc.inc("T", "T", _a)
                    sc.wait("S", "T", tk)
                    def _ha(sl, gm=gm, b1=b1, b2=b2):
                        sl.activation(TM1[:, :], b1[:, 0:512], AF.Silu, scale=S1C[:, gm:gm + 1])
                        return sl.activation(TM2[:, :], b2[:, 0:512], AF.Copy,
                                             scale=S2C[:, gm:gm + 1])
                    sc.inc("S", "S", _ha)
                    sc.wait("V", "S")
                    sc.inc("V", "V", lambda v, gm=gm, t2=t2: v.tensor_mul(
                        HT[:, gm * 1024 + t2 * 512:gm * 1024 + t2 * 512 + 512],
                        TM1[:, :], TM2[:, :]))
                    sc.wait("S", "V")
            ring_free[sa] = sc.n("T")
            ring_free[sb] = sc.n("T")

        nV_h = sc.n("V")
        sc.wait("T", "V", nV_h)
        if do_ffn == "a":
            continue

        def big_stage(wsrc, SRC, DST, SCL, act):
            banks = [PSB[0], PSB[1], PS_WA, PS_WB]
            for mg in range(16):
                sa, sb = (2 * mg) % 3, (2 * mg + 1) % 3
                sc.wait("G", "T", ring_free[sa])
                n1 = dmaG(RING[sa][:, :], wsrc[2 * mg])
                sc.wait("G", "T", ring_free[sb])
                n2 = dmaG(RING[sb][:, :], wsrc[2 * mg + 1])
                sc.wait("T", "dG", n2)
                sc.wait("T", "S")
                import os
                if os.environ.get("BSTAGE") == "dma":
                    continue
                def _grp(t, sa=sa, sb=sb):
                    last = None
                    for gi, (m, t2) in enumerate(((0, 0), (0, 1), (1, 0), (1, 1))):
                        for k in range(32):
                            slot, kk = (sa, k) if k < 16 else (sb, k - 16)
                            last = t.matmul(banks[gi][:, 0:512],
                                     RING[slot][:, kk * 256 + m * 128:kk * 256 + m * 128 + 128],
                                     SRC[:, k * 1024 + t2 * 512:k * 1024 + t2 * 512 + 512],
                                     start=(k == 0), stop=(k == 31))
                    return last
                tk2 = sc.inc("T", "T", _grp)
                ring_free[sa] = sc.n("T")
                ring_free[sb] = sc.n("T")
                if os.environ.get("BSTAGE") == "mm":
                    continue
                sc.wait("S", "T", tk2)
                def _ev(sl, mg=mg):
                    mode = os.environ.get("BSTAGE", "")
                    for gi, (m, t2) in enumerate(((0, 0), (0, 1), (1, 0), (1, 1))):
                        gm = mg * 2 + m
                        dst = DST[:, gm * 1024 + t2 * 512:gm * 1024 + t2 * 512 + 512]
                        if mode == "evcopy":
                            sl.copy(dst, banks[gi][:, 0:512])
                        elif mode == "evtm":
                            sl.copy(TM1[:, :], banks[gi][:, 0:512])
                        elif mode == "evsb":
                            sl.copy(dst, ZRO[:, 0:512])
                        elif mode == "evns":
                            sl.activation(dst, banks[gi][:, 0:512], act)
                        else:
                            sl.activation(dst, banks[gi][:, 0:512], act, scale=SCL[:, gm:gm + 1])
                    return sl.nop()
                sc.inc("S", "S", _ev)

        big_stage(h1t, HT, G1, SH1C, AF.Gelu)

        nS_g1 = sc.n("S")
        sc.wait("T", "S", nS_g1)
        if do_ffn == "b":
            continue
        big_stage(h2t, G1, HT, SH2C, AF.Gelu)

        nS_g2 = sc.n("S")
        sc.wait("T", "S", nS_g2)
        for m in range(8):
            slot = m % 3
            sc.wait("G", "T", ring_free[slot])
            nw = dmaG(RING[slot][:, :], w3t[m])
            sc.wait("T", "dG", nw)
            sc.wait("T", "S")
            def _d(t, slot=slot):
                last = None
                for n_ in range(2):
                    for k in range(32):
                        last = t.matmul(PSB[n_][:, 0:512],
                                 RING[slot][:, k * 128:(k + 1) * 128],
                                 HT[:, k * 1024 + n_ * 512:k * 1024 + n_ * 512 + 512],
                                 start=(k == 0), stop=(k == 31))
                return last
            tk = sc.inc("T", "T", _d)
            ring_free[slot] = sc.n("T")
            sc.wait("S", "T", tk)
            def _dev(sl, m=m):
                sl.activation(OASM[:, m * 1024:m * 1024 + 512], PSB[0][:, 0:512], AF.Copy,
                              scale=S3C[:, m:m + 1])
                return sl.activation(OASM[:, m * 1024 + 512:(m + 1) * 1024], PSB[1][:, 0:512],
                                     AF.Copy, scale=S3C[:, m:m + 1])
            sc.inc("S", "S", _dev)
        nS_ffo = sc.n("S")
        sc.wait("T", "S", nS_ffo)

        for tt in range(8):
            r0 = tok0 + tt * 128
            nrx = dmaY(XRES[:, :], x2stash[r0:r0 + 128, :])
            sc.wait("T", "V")
            def _trf(t, tt=tt):
                for m in range(7):
                    t.transpose(PS_TA[:, m * 128:(m + 1) * 128],
                                OASM[:, m * 1024 + tt * 128:m * 1024 + tt * 128 + 128], IDb[:, :])
                return t.transpose(PS_TA[:, 896:1024],
                                   OASM[:, 7 * 1024 + tt * 128:7 * 1024 + tt * 128 + 128],
                                   IDb[:, :])
            tk = sc.inc("T", "T", _trf)
            sc.wait("V", "T", tk)
            sc.wait("V", "dY")
            def _fadd(v):
                v.tensor_add(OUTT[:, 0:512], XRES[:, 0:512], PS_TA[:, 0:512])
                return v.tensor_add(OUTT[:, 512:1024], XRES[:, 512:1024], PS_TA[:, 512:1024])
            sc.inc("V", "V", _fadd)
            nV_out = sc.n("V")
            sc.wait("Y", "V", nV_out)
            dmaY(out[r0:r0 + 128, :], OUTT[:, :])
            sc.wait("T", "V", nV_out)

    if do_ffn is not True:
        sc.wait("V", "CC")
        sc.inc("V", "V", lambda v: v.memset(OUTT[:, :], 0.0))
        sc.wait("Y", "V")
        for tt in range(TOK // 128):
            dmaY(out[tt * 128:(tt + 1) * 128, :], OUTT[:, :])
    sc.wait("Y", "dY")
    nc_final(nc, sc)
    p2.close()
    st.close()
    return nc, dbg


def nc_final(nc, sc):
    with nc.Block() as block:
        def mk(key):
            def body(eng):
                for k, fn in sc.items:
                    if k == key:
                        fn(eng)
            return body
        block.tensor(mk("T"))
        block.vector(mk("V"))
        block.scalar(mk("S"))
        block.gpsimd(mk("G"))
        block.sync(mk("Y"))


_CACHE = {}


def kernel(**inputs):
    if "prog" not in _CACHE:
        _CACHE["prog"] = build_program()
    nc, _ = _CACHE["prog"]
    in_maps = [prepare_core_inputs(inputs, c) for c in range(8)]
    res = run_bass_kernel_spmd(nc, in_maps, core_ids=list(range(8)))
    outp = np.zeros((B, L, D), np.float32)
    for c in range(8):
        b, win = c // 4, c % 4
        o = res.results[c]["out"]
        outp[b, win * 1024:win * 1024 + 1024] = o[0:1024]
        outp[b, 4096 + win * 1024:4096 + win * 1024 + 1024] = o[1024:2048]
    return outp
